# version 02
# baseline (speedup 1.0000x reference)
import numpy as np
import concourse.bass as bass
import concourse.mybir as mybir

F32 = mybir.dt.float32
BF16 = mybir.dt.bfloat16
I32 = mybir.dt.int32
AF = mybir.ActivationFunctionType
ALU = mybir.AluOpType
AX = mybir.AxisListType

NCORE = 8
T = 2048
HL = 128
TH = T + HL
D = 1024
DC = 8
NHP = 8
FF = 4096
NFC = 32
SEQ = 16384
NKB = 128
NTC = 4
LN_TH = [(0, 512), (512, 512), (1024, 512), (1536, 512), (2048, 128)]
LN_T = [(128, 512), (640, 512), (1152, 512), (1664, 512)]
EXPB_A = 6.0
EXPB_B = 24.0
SCALE = 0.125
INV_D = 1.0 / 1024.0
SERIAL_ATT = False

DSTREAMS = ["dx", "dw", "dwz", "dq", "dkv", "dh", "dmx", "dout",
            "gx", "gw", "gq", "gh", "gkv"]
SEM_NAMES = [s + p for s in DSTREAMS for p in ("0", "1")] + ["t", "v", "s", "g", "cc"]


class Inst:
    __slots__ = ("r", "i")
    def __init__(self, r, i):
        self.r, self.i = r, i
    def inc(self, sem, n=1):
        self.r.cnt[sem] = self.r.cnt.get(sem, 0) + n
        if self.i is not None:
            self.i.then_inc(self.r.semh[sem], n)
        return self


class PendingDma:
    __slots__ = ("eng", "a", "k")
    def __init__(self, eng, a, k):
        self.eng, self.a, self.k = eng, a, k
    def inc(self, stream, n=16):
        r = self.eng.r
        idx = r.dsn.get(stream, 0)
        sem = f"{stream}{idx % 2}"
        r.dsn[stream] = idx + 1
        self.eng.wait(sem)
        r.cnt[sem] = r.cnt.get(sem, 0) + n
        if self.eng.real is not None:
            self.eng.real.dma_start(*self.a, **self.k).then_inc(r.semh[sem], n)
        return self


FENCED_OPS = {"tensor_reduce", "reduce_max", "reduce_sum", "reciprocal"}


class FInst(Inst):
    __slots__ = ("eng",)
    def __init__(self, r, i, eng):
        super().__init__(r, i)
        self.eng = eng
    def inc(self, sem, n=1):
        super().inc(sem, n)
        if self.eng.pend is self:
            self.eng.pend = ("done", sem, self.r.cnt[sem])
        return self


class Eng:
    def __init__(self, r, real, name=None):
        self.r, self.real, self.name = r, real, name
        self.pend = None
    def _flush(self):
        p = self.pend
        if p is None:
            return
        self.pend = None
        if isinstance(p, FInst):
            p_inc = Inst.inc
            p_inc(p, self.name, 1)
            self.wait(self.name)
        else:
            _, sem, val = p
            self.wait(sem, val)
    def __getattr__(self, op):
        real, r = self.real, self.r
        if op == "dma_start":
            def dcall(*a, **k):
                self._flush()
                return PendingDma(self, a, k)
            return dcall
        if self.name in ("v", "s", "g") and op not in ("wait_ge",):
            sem = self.name
            def fcall(*a, **k):
                self._flush()
                fence = op in FENCED_OPS
                if not fence and a:
                    try:
                        sh = list(a[0].shape)
                        free = 1
                        for d in sh[1:]:
                            free *= d
                        fence = free <= 64
                    except Exception:
                        pass
                if fence:
                    inst = FInst(r, getattr(real, op)(*a, **k) if real is not None else None, self)
                    self.pend = inst
                    return inst
                return Inst(r, getattr(real, op)(*a, **k) if real is not None else None)
            return fcall
        def call(*a, **k):
            return Inst(r, getattr(real, op)(*a, **k) if real is not None else None)
        return call
    def wait(self, sem, val=None):
        if sem in DSTREAMS:
            evs = val if isinstance(val, list) else self.r.dsnap(sem)
            for sm, vv in evs:
                if self.real is not None and vv > 0:
                    self.real.wait_ge(self.r.semh[sm], vv)
            return evs
        v = self.r.cnt.get(sem, 0) if val is None else val
        if self.real is not None and v > 0:
            self.real.wait_ge(self.r.semh[sem], v)
        return v
    def waitev(self, evs):
        if evs:
            for sem, val in evs:
                self.wait(sem, val)


class Router:
    def __init__(self, active, engines, semh):
        self.cnt = {}
        self.semh = semh
        self.free = {}
        self.dsn = {}
        for nm in ("t", "v", "s", "g", "y"):
            setattr(self, nm, Eng(self, engines[nm] if active == nm else None, nm))
    def snap(self, sem):
        if sem in DSTREAMS:
            return self.dsnap(sem)
        return self.cnt.get(sem, 0)
    def setfree(self, buf, *sems):
        evs = []
        for sm in sems:
            if sm in DSTREAMS:
                evs.extend(self.dsnap(sm))
            else:
                evs.append((sm, self.cnt.get(sm, 0)))
        self.free[buf] = evs
    def getfree(self, buf):
        return self.free.get(buf)
    def dma(self, E, stream, *a, **kw):
        idx = self.dsn.get(stream, 0)
        sem = f"{stream}{idx % 2}"
        E.wait(sem)
        E.dma_start(*a, **kw).inc(sem, 16)
        self.dsn[stream] = idx + 1
    def dsnap(self, stream):
        return [(f"{stream}0", self.cnt.get(f"{stream}0", 0)),
                (f"{stream}1", self.cnt.get(f"{stream}1", 0))]


def build_program(stop_after="p5"):
    nc = bass.Bass("TRN2", target_bir_lowering=False, detect_race_conditions=False)

    def din(name, shape, dt=F32):
        return nc.dram_tensor(name, shape, dt, kind="ExternalInput").ap()
    x_in = din("x", [TH, D])
    wqa, bqa = din("wqa", [D, D]), din("bqa", [D])
    wka, bka = din("wka", [D, D]), din("bka", [D])
    wva, bva = din("wva", [D, D]), din("bva", [D])
    woa, boa = din("woa", [D, D]), din("boa", [D])
    wqb = din("wqb", [D, D])
    wkvb = din("wkvb", [D, 2 * D])
    wob, bob = din("wob", [D, D]), din("bob", [D])
    w1, b1 = din("w1", [FF, D]), din("b1", [FF])
    w2, b2 = din("w2", [D, FF]), din("b2", [D])
    cos_in = din("cosT", [128, TH], BF16)
    sin_in = din("sinT", [128, TH], BF16)
    maskn_in = din("maskn", [128, 512], BF16)
    maskf_in = din("maskf", [128, 512], BF16)
    ident_in = din("ident", [128, 128])
    identb_in = din("identb", [128, 128], BF16)
    consts_in = din("consts", [1, 8])
    constsb_in = din("constsb", [128, 2])
    out = nc.dram_tensor("out", [T, D], F32, kind="ExternalOutput").ap()

    def dint(name, shape, dt=BF16):
        return nc.dram_tensor(name, shape, dt, kind="Internal").ap()
    wstage = dint("wstage", [D, D])
    waT = [dint(f"w{n}aT_h", [D, D]) for n in ("q", "k", "v", "o")]
    w1q_h = dint("w1q_h", [FF, D])
    w2q_h = dint("w2q_h", [D, FF])
    w1qT_h = dint("w1qT_h", [NFC, DC, 128, 128])
    w2qT_h = dint("w2qT_h", [DC, NFC, 128, 128])
    h_h = dint("h_h", [NFC, NTC, 128, 512])
    kin_h = dint("kin_h", [D, T])
    vin_h = dint("vin_h", [T, D])
    kout_h = dint("kout_h", [NCORE * D, T])
    vout_h = dint("vout_h", [SEQ, D])
    mxi_h = dint("mxi_h", [1, 16], F32)
    mxo_h = dint("mxo_h", [1, 16], F32)
    pmr_h = dint("pmr_h", [128, 4], F32)

    def sal(name, shape, dt):
        return nc.alloc_sbuf_tensor(name, shape, dt).ap()
    XT = sal("XT", [128, DC * TH], F32)
    ZBT = sal("ZBT", [128, DC * TH], BF16)
    CTXT = sal("CTXT", [128, NHP * T], BF16)
    HPB = sal("HPB", [128, 2048 + TH + TH], BF16)
    WT = sal("WT", [128, 5120], BF16)
    COS = sal("COS", [128, TH], BF16)
    SIN = sal("SIN", [128, TH], BF16)
    SHUF = sal("SHUF", [128, 2176], BF16)
    RTMP = sal("RTMP", [128, 2176], BF16)
    MASKN = sal("MASKN", [128, 512], BF16)
    MASKF = sal("MASKF", [128, 512], BF16)
    IDE = sal("IDE", [128, 128], F32)
    IDEB = sal("IDEB", [128, 128], BF16)
    ONESB = sal("ONESB", [128, 2], BF16)
    ZEROB = sal("ZEROB", [128, 128], BF16)
    ONE_RB = sal("ONE_RB", [1, 128], BF16)
    ONE_RF = sal("ONE_RF", [1, 128], F32)
    BVROW = sal("BVROW", [1, D], BF16)
    ROWQ = sal("ROWQ", [33, TH], F32)
    P0S = sal("P0S", [1, 1664], F32)
    FSC = sal("FSC", [128, 3072], F32)
    CTMP = FSC[:, 2048:3072]
    CTN = sal("CTN", [128, 256], BF16)
    HTI = HPB[:, 0:2048]
    KVS = HPB[:, 4096:6144]
    ISC = KVS.bitcast(I32)
    SMALL = sal("SMALL", [128, 256], F32)
    CONS = sal("CONS", [1, 8], F32)
    CONSB = sal("CONSB", [128, 2], F32)
    cBQA, cBKA, cBOA, cBOB = 0, 8, 16, 24
    cSW1, cSW2, cSWS1, cSWS2 = 32, 64, 72, 104
    cB1QS, cB2QS, cB1R, cB2R = 112, 144, 152, 184
    cU, cHU, cSCB, cTMP = 192, 200, 232, 240

    psS1 = nc.alloc_psum_tensor("psS1", [128, 1024], F32).ap()
    psS2 = nc.alloc_psum_tensor("psS2", [128, 1024], F32).ap()
    psA = nc.alloc_psum_tensor("psA", [128, 512], F32).ap()
    psB = nc.alloc_psum_tensor("psB", [128, 512], F32).ap()
    psC = nc.alloc_psum_tensor("psC", [128, 512], F32).ap()
    psD = nc.alloc_psum_tensor("psD", [128, 512], F32).ap()

    nc.const_aps.register(F32, 1e-5, CONS[0:1, 0:1])
    nc.const_aps.register(F32, -EXPB_A, CONSB[:, 0:1])
    nc.const_aps.register(F32, -EXPB_B, CONSB[:, 1:2])

    def xt(dc):
        return XT[:, dc * TH:(dc + 1) * TH]
    def zbt(dc):
        return ZBT[:, dc * TH:(dc + 1) * TH]
    def ctxt(hp):
        return CTXT[:, hp * T:(hp + 1) * T]

    def gen(r: Router):
        t, v, s, g, y = r.t, r.v, r.s, r.g, r.y

        def phase_sync(engines="tvsg"):
            sems = ["t", "v", "s", "dw", "dwz", "dq", "dh", "dkv", "dmx"]
            for en in engines:
                E = getattr(r, en)
                for sm in sems:
                    E.wait(sm)

        def pmax_to_scalar(src_col_ap, dst_ap):
            v.memset(FSC[:, 1024:1152], -3.0e38)
            v.tensor_copy(FSC[:, 1024:1025], src_col_ap).inc("v", 1)
            t.wait("v", r.snap("v"))
            t.waitev(r.getfree("psD"))
            t.transpose(psD[:, 0:128], FSC[:, 1024:1152], IDE).inc("t", 1)
            v.wait("t", r.snap("t"))
            v.tensor_reduce(dst_ap, psD[0:1, 0:128], AX.X, ALU.max)
            r.setfree("psD", "v")

        y.dma_start(out=CONS[:, :], in_=consts_in[:, :]).inc("dx", 16)
        y.dma_start(out=CONSB[:, :], in_=constsb_in[:, :]).inc("dx", 16)
        y.dma_start(out=COS[:, :], in_=cos_in[:, :]).inc("dx", 16)
        y.dma_start(out=SIN[:, :], in_=sin_in[:, :]).inc("dx", 16)
        y.dma_start(out=MASKN[:, :], in_=maskn_in[:, :]).inc("dx", 16)
        y.dma_start(out=MASKF[:, :], in_=maskf_in[:, :]).inc("dx", 16)
        y.dma_start(out=IDE[:, :], in_=ident_in[:, :]).inc("dx", 16)
        y.dma_start(out=IDEB[:, :], in_=identb_in[:, :]).inc("dx", 16)
        for (src, col) in ((bqa, cBQA), (bka, cBKA), (boa, cBOA), (bob, cBOB)):
            y.dma_start(out=SMALL[:, col:col + 8],
                        in_=src.rearrange("(c p) -> p c", p=128),
                        allow_slow_non_contiguous=True).inc("dx", 16)
        y.dma_start(out=SMALL[:, cB1R:cB1R + 32],
                    in_=b1.rearrange("(c p) -> p c", p=128),
                    allow_slow_non_contiguous=True).inc("dx", 16)
        y.dma_start(out=SMALL[:, cB2R:cB2R + 8],
                    in_=b2.rearrange("(c p) -> p c", p=128),
                    allow_slow_non_contiguous=True).inc("dx", 16)
        g.dma_start(out=BVROW[0:1, :], in_=bva[None, :]).inc("gx", 16)
        consts_dx = r.snap("dx")
        consts_gx = r.snap("gx")

        v.wait("dx", consts_dx)
        v.wait("gx", consts_gx)
        v.memset(ONESB[:, :], 1.0)
        v.memset(ZEROB[:, :], 0.0)
        v.memset(ONE_RB[0:1, :], 1.0)
        v.memset(ONE_RF[0:1, :], 1.0).inc("v", 1)
        s.wait("dx", consts_dx)
        s.wait("gx", consts_gx)

        t.wait("gx", consts_gx)
        wa_ready = []
        for widx, wsrc in enumerate((wqa, wka, wva, woa)):
            if widx > 0:
                g.waitev(r.getfree("wstage"))
            g.dma_start(out=wstage[:, :], in_=wsrc[:, :]).inc("gw", 16)
            cast_dw = r.snap("gw")
            for c in range(DC):
                slot = "W4"
                off = 4096
                y.wait("gw", cast_dw)
                y.waitev(r.getfree(slot))
                stg = WT[:, off:off + 1024]
                y.dma_start(out=stg, in_=wstage[:, c * 128:(c + 1) * 128],
                            transpose=True).inc("dwz", 16)
                y.wait("dwz", r.snap("dwz"))
                y.dma_start(out=waT[widx][c * 128:(c + 1) * 128, :], in_=stg).inc("dwz", 16)
                r.setfree(slot, "dwz")
            r.setfree("wstage", "dwz")
            wa_ready.append(r.snap("dwz"))

        for tt in range(17):
            slot = f"xstg{tt % 2}"
            y.waitev(r.getfree(slot))
            stg = FSC[:, (tt % 2) * 1024:(tt % 2) * 1024 + 1024]
            y.dma_start(out=stg, in_=x_in[tt * 128:(tt + 1) * 128, :]).inc("dx", 16)
            t.wait("dx", r.snap("dx"))
            for dc in range(DC):
                ps, pn = (psA, "psA") if dc % 2 == 0 else (psB, "psB")
                t.waitev(r.getfree(pn))
                t.transpose(ps[:, 0:128], stg[:, dc * 128:(dc + 1) * 128], IDE).inc("t", 1)
                v.wait("t", r.snap("t"))
                v.tensor_copy(xt(dc)[:, tt * 128:(tt + 1) * 128], ps[:, 0:128]).inc("v", 1)
                r.setfree(pn, "v")
            r.setfree(slot, "v")
        x_loaded_v = r.snap("v")

        def emit_ln(chunks, zb_out, zmax):
            def lnxb(dc, W):
                return HPB[:, dc * 512:dc * 512 + W]
            def lnxq(dc, W):
                return (SHUF if dc < 4 else RTMP)[:, (dc % 4) * 512:(dc % 4) * 512 + W]
            if zmax:
                v.memset(SMALL[:, cU:cU + 8], -3.0e38).inc("v", 1)
            s.wait("v", r.snap("v"))
            for (c0, W) in chunks:
                s.waitev(r.getfree("SHUF"))
                s.waitev(r.getfree("RTMP"))
                for dc in range(DC):
                    s.copy(lnxb(dc, W), xt(dc)[:, c0:c0 + W])
                last = None
                for dc in range(DC):
                    last = s.activation(lnxq(dc, W), xt(dc)[:, c0:c0 + W], AF.Square)
                last.inc("s", 1)
                t.wait("s", r.snap("s"))
                t.waitev(r.getfree("psA"))
                t.waitev(r.getfree("psB"))
                for dc in range(DC):
                    t.matmul(psA[0:1, 0:W], ONESB[:, 0:1], lnxb(dc, W),
                             start=(dc == 0), stop=(dc == DC - 1))
                last = None
                for dc in range(DC):
                    last = t.matmul(psB[0:1, 0:W], ONESB[:, 0:1], lnxq(dc, W),
                                    start=(dc == 0), stop=(dc == DC - 1))
                last.inc("t", 1)
                r.setfree("SHUF", "t")
                r.setfree("RTMP", "t")
                v.wait("t", r.snap("t"))
                v.tensor_scalar(P0S[0:1, 0:W], psA[0:1, 0:W], INV_D, None, ALU.mult)
                v.tensor_scalar(P0S[0:1, 512:512 + W], psB[0:1, 0:W], INV_D, None, ALU.mult)
                v.tensor_tensor(P0S[0:1, 1024:1024 + W], P0S[0:1, 0:W], P0S[0:1, 0:W],
                                ALU.mult)
                v.tensor_tensor(P0S[0:1, 512:512 + W], P0S[0:1, 512:512 + W],
                                P0S[0:1, 1024:1024 + W], ALU.subtract).inc("v", 1)
                r.setfree("psA", "v")
                r.setfree("psB", "v")
                s.wait("v", r.snap("v"))
                s.activation(P0S[0:1, 1024:1024 + W], P0S[0:1, 512:512 + W], AF.Sqrt,
                             bias=1e-5, scale=1.0).inc("s", 1)
                v.wait("s", r.snap("s"))
                v.reciprocal(ROWQ[0:1, c0:c0 + W], P0S[0:1, 1024:1024 + W])
                v.tensor_tensor(ROWQ[32:33, c0:c0 + W], P0S[0:1, 0:W],
                                ROWQ[0:1, c0:c0 + W], ALU.mult)
                v.tensor_copy(P0S[0:1, 1024:1024 + W], ROWQ[32:33, c0:c0 + W]).inc("v", 1)
                t.wait("v", r.snap("v"))
                t.waitev(r.getfree("psC"))
                t.waitev(r.getfree("psD"))
                t.matmul(psC[:, 0:W], ONE_RF[0:1, :], ROWQ[0:1, c0:c0 + W],
                         start=True, stop=True)
                t.matmul(psD[:, 0:W], ONE_RF[0:1, :], P0S[0:1, 1024:1024 + W],
                         start=True, stop=True).inc("t", 1)
                v.wait("t", r.snap("t"))
                if zb_out:
                    last = None
                    for dc in range(DC):
                        v.tensor_tensor(FSC[:, 0:W], xt(dc)[:, c0:c0 + W], psC[:, 0:W],
                                        ALU.mult)
                        last = v.tensor_tensor(zbt(dc)[:, c0:c0 + W], FSC[:, 0:W],
                                               psD[:, 0:W], ALU.subtract)
                    last.inc("v", 1)
                if zmax:
                    last = None
                    for dc in range(DC):
                        v.tensor_tensor(FSC[:, 0:W], xt(dc)[:, c0:c0 + W], psC[:, 0:W],
                                        ALU.mult)
                        v.tensor_tensor(FSC[:, 512:512 + W], FSC[:, 0:W], psD[:, 0:W],
                                        ALU.subtract)
                        v.tensor_reduce(SMALL[:, cTMP:cTMP + 1], FSC[:, 512:512 + W],
                                        AX.X, ALU.max)
                        last = v.tensor_tensor(SMALL[:, cU + dc:cU + dc + 1],
                                               SMALL[:, cU + dc:cU + dc + 1],
                                               SMALL[:, cTMP:cTMP + 1], ALU.max)
                    last.inc("v", 1)
                r.setfree("psC", "v")
                r.setfree("psD", "v")

        def emit_store():
            phase_sync("tg")
            for tt in range(16):
                slot = f"ostg{tt % 2}"
                stg = FSC[:, (tt % 2) * 1024:(tt % 2) * 1024 + 1024]
                for dc in range(DC):
                    ps, pn = (psA, "psA") if dc % 2 == 0 else (psB, "psB")
                    t.waitev(r.getfree(pn))
                    if dc == 0:
                        t.waitev(r.getfree(slot))
                    t.transpose(ps[:, 0:128],
                                xt(dc)[:, 128 + tt * 128:128 + (tt + 1) * 128],
                                IDE).inc("t", 1)
                    v.wait("t", r.snap("t"))
                    if dc == 0:
                        v.waitev(r.getfree(slot))
                    v.tensor_copy(stg[:, dc * 128:(dc + 1) * 128], ps[:, 0:128]).inc("v", 1)
                    r.setfree(pn, "v")
                g.wait("v", r.snap("v"))
                g.dma_start(out=out[tt * 128:(tt + 1) * 128, :], in_=stg).inc("dout", 16)
                r.setfree(slot, "dout")
            g.wait("dout", r.snap("dout"))

        if stop_after == "p0":
            emit_store()
            return

        v.wait("v", x_loaded_v)
        emit_ln(LN_TH, zb_out=True, zmax=False)
        ln1_v = r.snap("v")
        if stop_after == "p0ln":
            emit_store()
            return
        p1lvl = {"p1q": 1, "p1k": 2, "p1r": 3, "p1v": 4, "p1att0a": 4.95, "p1att0b": 4.97, "p1att0": 5, "p1att1": 5.1, "p1att2": 5.2, "p1att3": 5.4, "p1att": 5.6}.get(stop_after, 6)

        HQ0, HK0, HV0 = 0, 2048, 2048 + TH

        def wtiles_dma_y(key, dst_off, src_hbm, fcol, ready_dwz):
            y.wait("dwz", ready_dwz)
            y.waitev(r.getfree(key))
            src = src_hbm.rearrange("(a p) q -> p a q", p=128)
            y.dma_start(out=WT[:, dst_off:dst_off + 1024].rearrange("p (a q) -> p a q", q=128),
                        in_=src[:, :, fcol:fcol + 128]).inc("dw", 16)

        for hp in range(NHP):
            wtiles_dma_y("W0", 0, waT[0], hp * 128, wa_ready[0])
            if p1lvl >= 2:
                wtiles_dma_y("W1", 1024, waT[1], hp * 128, wa_ready[1])
            if p1lvl >= 4:
                wtiles_dma_y("W2", 2048, waT[2], hp * 128, wa_ready[2])
            wdma = r.snap("dw")
            t.wait("dw", wdma)
            if hp == 0:
                t.wait("v", ln1_v)
            t.waitev(r.getfree("hq"))
            for tc in range(NTC):
                ps, pn = (psA, "psA") if tc % 2 == 0 else (psB, "psB")
                t.waitev(r.getfree(pn))
                last = None
                for dc in range(DC):
                    last = t.matmul(ps[:, :], WT[:, dc * 128:(dc + 1) * 128],
                                    zbt(dc)[:, 128 + tc * 512:128 + (tc + 1) * 512],
                                    start=(dc == 0), stop=(dc == DC - 1))
                last.inc("t", 1)
                s.wait("t", r.snap("t"))
                s.activation(HPB[:, HQ0 + tc * 512:HQ0 + (tc + 1) * 512], ps[:, :],
                             AF.Identity, bias=SMALL[:, cBQA + hp:cBQA + hp + 1],
                             scale=1.0).inc("s", 1)
                r.setfree(pn, "s")
            qproj_s = r.snap("s")
            if p1lvl < 2:
                continue
            t.waitev(r.getfree("hk"))
            for ci, (c0, W) in enumerate(LN_TH):
                ps, pn = (psA, "psA") if ci % 2 == 0 else (psB, "psB")
                t.waitev(r.getfree(pn))
                last = None
                for dc in range(DC):
                    last = t.matmul(ps[:, 0:W], WT[:, 1024 + dc * 128:1024 + (dc + 1) * 128],
                                    zbt(dc)[:, c0:c0 + W],
                                    start=(dc == 0), stop=(dc == DC - 1))
                last.inc("t", 1)
                s.wait("t", r.snap("t"))
                s.activation(HPB[:, HK0 + c0:HK0 + c0 + W], ps[:, 0:W],
                             AF.Identity, bias=SMALL[:, cBKA + hp:cBKA + hp + 1],
                             scale=1.0).inc("s", 1)
                r.setfree(pn, "s")
            kproj_s = r.snap("s")
            r.setfree("W0", "s")
            r.setfree("W1", "s")
            if p1lvl < 3:
                continue
            v.wait("s", qproj_s)
            v.wait("t", r.snap("t"))
            v.waitev(r.getfree("SHUF"))
            v.waitev(r.getfree("RTMP"))
            QS = HPB[:, HQ0:HQ0 + 2048]
            v.tensor_copy(SHUF[0:32, 0:2048], QS[32:64, :])
            v.tensor_copy(SHUF[32:64, 0:2048], QS[0:32, :])
            v.tensor_copy(SHUF[64:96, 0:2048], QS[96:128, :])
            v.tensor_copy(SHUF[96:128, 0:2048], QS[64:96, :])
            v.tensor_tensor(RTMP[:, 0:2048], QS[:, :], COS[:, 128:TH], ALU.mult)
            v.tensor_tensor(QS[:, :], SHUF[:, 0:2048], SIN[:, 128:TH], ALU.mult)
            v.tensor_tensor(QS[:, :], RTMP[:, 0:2048], QS[:, :], ALU.add).inc("v", 1)
            v.wait("s", kproj_s)
            KS_ = HPB[:, HK0:HK0 + TH]
            v.tensor_copy(SHUF[0:32, 0:TH], KS_[32:64, :])
            v.tensor_copy(SHUF[32:64, 0:TH], KS_[0:32, :])
            v.tensor_copy(SHUF[64:96, 0:TH], KS_[96:128, :])
            v.tensor_copy(SHUF[96:128, 0:TH], KS_[64:96, :])
            v.tensor_tensor(RTMP[:, 0:TH], KS_[:, :], COS[:, :], ALU.mult)
            v.tensor_tensor(KS_[:, :], SHUF[:, 0:TH], SIN[:, :], ALU.mult)
            v.tensor_tensor(KS_[:, :], RTMP[:, 0:TH], KS_[:, :], ALU.add).inc("v", 1)
            rope_v = r.snap("v")
            r.setfree("SHUF", "v")
            r.setfree("RTMP", "v")
            if p1lvl < 4:
                continue
            t.waitev(r.getfree("hv"))
            for tt in range(17):
                ps, pn = (psA, "psA") if tt % 2 == 0 else (psB, "psB")
                t.waitev(r.getfree(pn))
                for dc in range(DC):
                    t.matmul(ps[:, 0:128], zbt(dc)[:, tt * 128:(tt + 1) * 128],
                             WT[:, 2048 + dc * 128:2048 + (dc + 1) * 128],
                             start=(dc == 0), stop=False)
                t.matmul(ps[:, 0:128], ONE_RB[0:1, :],
                         BVROW[0:1, hp * 128:(hp + 1) * 128],
                         start=False, stop=True).inc("t", 1)
                v.wait("t", r.snap("t"))
                v.tensor_copy(HPB[:, HV0 + tt * 128:HV0 + (tt + 1) * 128],
                              ps[:, 0:128]).inc("v", 1)
                r.setfree(pn, "v")
            r.setfree("W2", "v")
            if p1lvl < 5:
                continue
            t.wait("v", r.snap("v"))
            for qt in range(16):
                k0 = qt * 128
                q0 = HQ0 + qt * 128
                if SERIAL_ATT:
                    t.wait("v", r.snap("v"))
                    t.wait("s", r.snap("s"))
                    v.wait("t", r.snap("t"))
                    s.wait("v", r.snap("v"))
                t.waitev(r.getfree("psS1"))
                t.waitev(r.getfree("psS2"))
                t.matmul(psS1[:, 0:128], HPB[0:64, HK0 + k0:HK0 + k0 + 128],
                         HPB[0:64, q0:q0 + 128], start=True, stop=True)
                t.matmul(psS1[:, 512:640], HPB[0:64, HK0 + k0 + 128:HK0 + k0 + 256],
                         HPB[0:64, q0:q0 + 128], start=True, stop=True)
                t.matmul(psS2[:, 0:128], HPB[64:128, HK0 + k0:HK0 + k0 + 128],
                         HPB[64:128, q0:q0 + 128], start=True, stop=True,
                         tile_position=(64, 0))
                t.matmul(psS2[:, 512:640], HPB[64:128, HK0 + k0 + 128:HK0 + k0 + 256],
                         HPB[64:128, q0:q0 + 128], start=True, stop=True,
                         tile_position=(64, 0)).inc("t", 1)
                pts = f"pta{qt % 2}"
                pt = SHUF[:, (qt % 2) * 512:(qt % 2) * 512 + 512]
                if p1lvl < 5.1:
                    continue
                s.wait("t", r.snap("t"))
                s.waitev(r.getfree(pts))
                sv1 = psS1.rearrange("p (a b) -> p a b", b=128)[:, 0:5:4, :]
                sv2 = psS2.rearrange("p (a b) -> p a b", b=128)[:, 0:5:4, :]
                s.activation(pt[:, 0:256].rearrange("p (a b) -> p a b", b=128),
                             sv1, AF.Exp, bias=-EXPB_A, scale=SCALE).inc("s", 1)
                r.setfree("psS1", "s")
                s.activation(pt[:, 256:512].rearrange("p (a b) -> p a b", b=128),
                             sv2, AF.Exp, bias=-EXPB_A, scale=SCALE).inc("s", 1)
                r.setfree("psS2", "s")
                v.wait("s", r.snap("s"))
                v.tensor_tensor(pt, pt, MASKF if qt == 0 else MASKN, ALU.mult).inc("v", 1)
                if p1lvl < 5.2:
                    r.setfree(pts, "v")
                    continue
                t.wait("v", r.snap("v"))
                t.waitev(r.getfree("psC"))
                first = True
                last = None
                for h in range(2):
                    for blk in range(2):
                        off = (2 * h + blk) * 128
                        vt = HPB[:, HV0 + (qt + blk) * 128 + h * 64:
                                 HV0 + (qt + blk) * 128 + h * 64 + 64]
                        t.matmul(psC[:, h * 64:h * 64 + 64], pt[:, off:off + 128], vt,
                                 start=first, stop=False, skip_group_check=True)
                        first = False
                        last = t.matmul(psC[:, 128 + h:129 + h], pt[:, off:off + 128],
                                        ONESB[:, 0:1], start=False,
                                        stop=(h == 1 and blk == 1), skip_group_check=True)
                last.inc("t", 1)
                r.setfree(pts, "t")
                if p1lvl < 5.4:
                    r.setfree("psC", "t")
                    continue
                ctn = CTN[:, (qt % 2) * 128:(qt % 2) * 128 + 128]
                v.wait("t", r.snap("t"))
                v.waitev(r.getfree(f"ctn{qt % 2}"))
                v.reciprocal(FSC[:, 0:2], psC[:, 128:130])
                v.tensor_scalar(ctn[:, 0:64], psC[:, 0:64], FSC[:, 0:1], None, ALU.mult)
                v.tensor_scalar(ctn[:, 64:128], psC[:, 64:128], FSC[:, 1:2], None,
                                ALU.mult).inc("v", 1)
                r.setfree("psC", "v")
                if p1lvl < 5.6:
                    r.setfree(f"ctn{qt % 2}", "v")
                    continue
                t.wait("v", r.snap("v"))
                t.waitev(r.getfree("psD"))
                t.transpose(psD.bitcast(BF16)[:, 0:128], ctn, IDEB).inc("t", 1)
                r.setfree(f"ctn{qt % 2}", "t")
                v.wait("t", r.snap("t"))
                v.tensor_copy(ctxt(hp)[:, qt * 128:(qt + 1) * 128],
                              psD.bitcast(BF16)[:, 0:128]).inc("v", 1)
                r.setfree(f"hpb{hp % 1}", "v")
                r.setfree("psD", "v")
        attn_v = r.snap("v")
        if stop_after == "p1hpb":
            g.wait("v", attn_v)
            g.wait("s", r.snap("s"))
            ob = out.bitcast(BF16)
            for cb in range(4):
                g.dma_start(out=ob[cb * 128:(cb + 1) * 128, 0:1600],
                            in_=HPB[:, cb * 1600:(cb + 1) * 1600]).inc("dout", 16)
            g.wait("dout", r.snap("dout"))
            return
        if stop_after == "p1ctx":
            g.wait("v", attn_v)
            ob = out.bitcast(BF16)
            for cb in range(8):
                g.dma_start(out=ob[cb * 128:(cb + 1) * 128, 0:2048],
                            in_=CTXT[:, cb * 2048:(cb + 1) * 2048]).inc("dout", 16)
            g.wait("dout", r.snap("dout"))
            return
        if p1lvl < 6:
            emit_store()
            return

        for dc in range(DC):
            key = "W0" if dc % 2 == 0 else "W1"
            woff = 0 if dc % 2 == 0 else 1024
            wtiles_dma_y(key, woff, waT[3], dc * 128, wa_ready[3])
            wdma = r.snap("dw")
            t.wait("dw", wdma)
            if dc == 0:
                t.wait("v", attn_v)
            for tc in range(NTC):
                ps, pn = (psA, "psA") if tc % 2 == 0 else (psB, "psB")
                t.waitev(r.getfree(pn))
                last = None
                for hp in range(NHP):
                    last = t.matmul(ps[:, :], WT[:, woff + hp * 128:woff + (hp + 1) * 128],
                                    ctxt(hp)[:, tc * 512:(tc + 1) * 512],
                                    start=(hp == 0), stop=(hp == NHP - 1))
                last.inc("t", 1)
                v.wait("t", r.snap("t"))
                dst = xt(dc)[:, 128 + tc * 512:128 + (tc + 1) * 512]
                v.scalar_tensor_tensor(dst, ps[:, :], SMALL[:, cBOA + dc:cBOA + dc + 1],
                                       dst, ALU.add, ALU.add).inc("v", 1)
                r.setfree(pn, "v")
            r.setfree(key, "v")

        if stop_after == "p1":
            emit_store()
            return

        phase_sync("vyg")

        def quant_chunk(stg, dst_bf, rsw_col):
            v.tensor_scalar(CTMP[:, :], stg, SMALL[:, rsw_col:rsw_col + 1], 0.5,
                            ALU.mult, ALU.add)
            v.tensor_copy(ISC[:, :], CTMP[:, :])
            v.tensor_copy(stg, ISC[:, :])
            v.tensor_tensor(CTMP[:, :], stg, CTMP[:, :], ALU.is_gt)
            return v.scalar_tensor_tensor(dst_bf, CTMP[:, :], -1.0, stg,
                                          ALU.mult, ALU.add)

        for c in range(NFC):
            slot = f"wqs{c % 2}"
            y.waitev(r.getfree(slot))
            stg = FSC[:, (c % 2) * 1024:(c % 2) * 1024 + 1024]
            y.dma_start(out=stg, in_=w1[c * 128:(c + 1) * 128, :]).inc("dq", 16)
            v.wait("dq", r.snap("dq"))
            v.waitev(r.getfree(f"whti{c % 2}"))
            v.tensor_reduce(SMALL[:, cSW1 + c:cSW1 + c + 1], stg, AX.X, ALU.max,
                            apply_absolute_value=True)
            v.reciprocal(SMALL[:, cTMP:cTMP + 1], SMALL[:, cSW1 + c:cSW1 + c + 1])
            v.tensor_scalar(SMALL[:, cTMP:cTMP + 1], SMALL[:, cTMP:cTMP + 1], 127.0,
                            None, ALU.mult).inc("v", 1)
            v.wait("v", r.snap("v"))
            hti = HTI[:, (c % 2) * 1024:(c % 2) * 1024 + 1024]
            quant_chunk(stg, hti, cTMP).inc("v", 1)
            r.setfree(slot, "v")
            g.wait("v", r.snap("v"))
            g.dma_start(out=w1q_h[c * 128:(c + 1) * 128, :], in_=hti).inc("gq", 16)
            r.setfree(f"whti{c % 2}", "gq")
        for dc in range(DC):
            for cc in range(4):
                slot = f"wqs{cc % 2}"
                y.waitev(r.getfree(slot))
                stg = FSC[:, (cc % 2) * 1024:(cc % 2) * 1024 + 1024]
                y.dma_start(out=stg, in_=w2[dc * 128:(dc + 1) * 128,
                                            cc * 1024:(cc + 1) * 1024]).inc("dq", 16)
                v.wait("dq", r.snap("dq"))
                if cc == 0:
                    v.tensor_reduce(SMALL[:, cSW2 + dc:cSW2 + dc + 1], stg, AX.X,
                                    ALU.max, apply_absolute_value=True)
                else:
                    v.tensor_reduce(SMALL[:, cTMP:cTMP + 1], stg, AX.X, ALU.max,
                                    apply_absolute_value=True)
                    v.tensor_tensor(SMALL[:, cSW2 + dc:cSW2 + dc + 1],
                                    SMALL[:, cSW2 + dc:cSW2 + dc + 1],
                                    SMALL[:, cTMP:cTMP + 1], ALU.max).inc("v", 1)
                r.setfree(slot, "v")
        for dc in range(DC):
            v.reciprocal(SMALL[:, cTMP + 1:cTMP + 2], SMALL[:, cSW2 + dc:cSW2 + dc + 1])
            v.tensor_scalar(SMALL[:, cTMP + 1:cTMP + 2], SMALL[:, cTMP + 1:cTMP + 2],
                            127.0, None, ALU.mult).inc("v", 1)
            v.wait("v", r.snap("v"))
            for cc in range(4):
                slot = f"wqs{cc % 2}"
                y.waitev(r.getfree(slot))
                stg = FSC[:, (cc % 2) * 1024:(cc % 2) * 1024 + 1024]
                y.dma_start(out=stg, in_=w2[dc * 128:(dc + 1) * 128,
                                            cc * 1024:(cc + 1) * 1024]).inc("dq", 16)
                v.wait("dq", r.snap("dq"))
                v.waitev(r.getfree(f"whti{cc % 2}"))
                hti = HTI[:, (cc % 2) * 1024:(cc % 2) * 1024 + 1024]
                quant_chunk(stg, hti, cTMP + 1).inc("v", 1)
                r.setfree(slot, "v")
                g.wait("v", r.snap("v"))
                g.dma_start(out=w2q_h[dc * 128:(dc + 1) * 128,
                                      cc * 1024:(cc + 1) * 1024], in_=hti).inc("gq", 16)
                r.setfree(f"whti{cc % 2}", "gq")
        wq_nat_dq = r.snap("gq")
        for c in range(DC):
            y.wait("gq", wq_nat_dq)
            y.wait("dq", r.snap("dq"))
            y.dma_start(out=CTXT[:, 0:4096], in_=w1q_h[:, c * 128:(c + 1) * 128],
                        transpose=True).inc("dq", 16)
            y.wait("dq", r.snap("dq"))
            for fc in range(NFC):
                y.dma_start(out=w1qT_h[fc, c],
                            in_=CTXT[:, fc * 128:(fc + 1) * 128]).inc("dq", 16)
        for fc in range(NFC):
            y.wait("dq", r.snap("dq"))
            y.dma_start(out=CTXT[:, 0:1024], in_=w2q_h[:, fc * 128:(fc + 1) * 128],
                        transpose=True).inc("dq", 16)
            y.wait("dq", r.snap("dq"))
            for dc in range(DC):
                y.dma_start(out=w2qT_h[dc, fc],
                            in_=CTXT[:, dc * 128:(dc + 1) * 128]).inc("dq", 16)
        wq_ready = r.snap("dq")
        if stop_after == "p15":
            emit_store()
            return

        def emit_ffn(idx):
            flvl = {"p2sind": 1, "p2ln": 1, "p2zq": 2, "p2m1": 3, "p2s2": 4}.get(stop_after, 9)
            phase_sync("tvsg")
            emit_ln(LN_T, zb_out=False, zmax=True)
            v.tensor_reduce(SMALL[:, cTMP + 2:cTMP + 3], SMALL[:, cU:cU + 8], AX.X,
                            ALU.max)
            pmax_to_scalar(SMALL[:, cTMP + 2:cTMP + 3], P0S[0:1, 1536:1537])
            v.memset(FSC[0:1, 1152:1168], -3.0e38)
            v.tensor_scalar(FSC[0:1, 1152:1168], FSC[0:1, 1152:1168],
                            P0S[0:1, 1536:1537], None, ALU.max).inc("v", 1)
            g.wait("v", r.snap("v"))
            g.dma_start(out=mxi_h[0:1, :], in_=FSC[0:1, 1152:1168]).inc("dmx", 16)
            g.wait("dmx", r.snap("dmx"))
            g.collective_compute("AllReduce", ALU.max, replica_groups=[list(range(NCORE))],
                                 ins=[mxi_h], outs=[mxo_h]).inc("cc", 1)
            g.wait("cc", r.snap("cc"))
            g.dma_start(out=P0S[0:1, 1537:1538], in_=mxo_h[0:1, 0:1]).inc("dmx", 16)
            v.wait("dmx", r.snap("dmx"))
            v.tensor_scalar(P0S[0:1, 1538:1539], P0S[0:1, 1537:1538], 1e-8, 1.0 / 255.0,
                            ALU.max, ALU.mult)
            v.reciprocal(P0S[0:1, 1539:1540], P0S[0:1, 1538:1539])
            t.wait("v", r.snap("v"))
            t.waitev(r.getfree("psC"))
            t.matmul(psC[:, 0:2], ONE_RF[0:1, :], P0S[0:1, 1538:1540],
                     start=True, stop=True).inc("t", 1)
            v.wait("t", r.snap("t"))
            v.tensor_copy(SMALL[:, cSCB:cSCB + 2], psC[:, 0:2]).inc("v", 1)
            v.wait("v", r.snap("v"))
            r.setfree("psC", "v")
            v.tensor_scalar(SMALL[:, cSWS1:cSWS1 + 32], SMALL[:, cSW1:cSW1 + 32],
                            SMALL[:, cSCB:cSCB + 1], None, ALU.mult)
            v.tensor_scalar(SMALL[:, cSWS1:cSWS1 + 32], SMALL[:, cSWS1:cSWS1 + 32],
                            1.0 / 127.0, None, ALU.mult)
            v.reciprocal(FSC[:, 0:32], SMALL[:, cSWS1:cSWS1 + 32])
            v.tensor_tensor(FSC[:, 32:64], SMALL[:, cB1R:cB1R + 32], FSC[:, 0:32],
                            ALU.mult)
            v.tensor_scalar(FSC[:, 32:64], FSC[:, 32:64], 0.5, None, ALU.add)
            v.tensor_copy(ISC[:, 0:32], FSC[:, 32:64])
            v.tensor_copy(FSC[:, 64:96], ISC[:, 0:32])
            v.tensor_tensor(FSC[:, 96:128], FSC[:, 64:96], FSC[:, 32:64], ALU.is_gt)
            v.scalar_tensor_tensor(FSC[:, 64:96], FSC[:, 96:128], -1.0, FSC[:, 64:96],
                                   ALU.mult, ALU.add)
            v.tensor_tensor(SMALL[:, cB1QS:cB1QS + 32], FSC[:, 64:96],
                            SMALL[:, cSWS1:cSWS1 + 32], ALU.mult).inc("v", 1)
            if flvl < 2 and stop_after != "p2sind":
                return
            if stop_after == "p2sind":
                g.wait("v", r.snap("v"))
                g.dma_start(out=out[0:1, 0:4], in_=P0S[0:1, 1536:1540]).inc("dout", 16)
                g.dma_start(out=out[1:2, 0:8], in_=SMALL[0:1, cU:cU + 8]).inc("dout", 16)
                g.dma_start(out=out[2:3, 0:3], in_=SMALL[0:1, cTMP:cTMP + 3]).inc("dout", 16)
                g.wait("dout", r.snap("dout"))
                return
            v.tensor_scalar(ROWQ[0:1, 128:TH], ROWQ[0:1, 128:TH],
                            P0S[0:1, 1539:1540], None, ALU.mult)
            v.tensor_scalar(ROWQ[32:33, 128:TH], ROWQ[32:33, 128:TH],
                            SMALL[32:33, cSCB + 1:cSCB + 2], None, ALU.mult)
            v.tensor_scalar(ROWQ[32:33, 128:TH], ROWQ[32:33, 128:TH], 0.5, None,
                            ALU.subtract).inc("v", 1)
            for (c0, W) in LN_T:
                v.tensor_copy(P0S[0:1, 1024:1024 + W], ROWQ[32:33, c0:c0 + W]).inc("v", 1)
                t.wait("v", r.snap("v"))
                t.waitev(r.getfree("psC"))
                t.waitev(r.getfree("psD"))
                t.matmul(psC[:, 0:W], ONE_RF[0:1, :], ROWQ[0:1, c0:c0 + W],
                         start=True, stop=True)
                t.matmul(psD[:, 0:W], ONE_RF[0:1, :], P0S[0:1, 1024:1024 + W],
                         start=True, stop=True).inc("t", 1)
                v.wait("t", r.snap("t"))
                last = None
                for dc in range(DC):
                    v.tensor_tensor(FSC[:, 0:W], xt(dc)[:, c0:c0 + W], psC[:, 0:W],
                                    ALU.mult)
                    v.tensor_tensor(FSC[:, 0:W], FSC[:, 0:W], psD[:, 0:W], ALU.subtract)
                    v.tensor_copy(ISC[:, 0:W], FSC[:, 0:W])
                    v.tensor_copy(FSC[:, 512:512 + W], ISC[:, 0:W])
                    v.tensor_tensor(FSC[:, 1024:1024 + W], FSC[:, 512:512 + W],
                                    FSC[:, 0:W], ALU.is_gt)
                    v.scalar_tensor_tensor(FSC[:, 512:512 + W], FSC[:, 1024:1024 + W],
                                           -1.0, FSC[:, 512:512 + W], ALU.mult, ALU.add)
                    last = v.tensor_scalar(zbt(dc)[:, c0:c0 + W], FSC[:, 512:512 + W],
                                           0.0, None, ALU.max)
                last.inc("v", 1)
                r.setfree("psC", "v")
                r.setfree("psD", "v")
            zq_v = r.snap("v")
            if stop_after == "p2zqd":
                g.wait("v", zq_v)
                ob = out.bitcast(BF16)
                for cb in range(8):
                    g.dma_start(out=ob[cb * 128:(cb + 1) * 128, 0:2048],
                                in_=ZBT[:, cb * TH + 128:cb * TH + 2176]).inc("dout", 16)
                g.wait("dout", r.snap("dout"))
                return
            if flvl < 3 or stop_after == "p2zqd":
                return
            v.memset(SMALL[:, cHU:cHU + 32], -3.0e38).inc("v", 1)
            for fc in range(NFC):
                key = "W0" if fc % 2 == 0 else "W1"
                woff = 0 if fc % 2 == 0 else 1024
                y.wait("dq", wq_ready)
                y.waitev(r.getfree(key))
                y.dma_start(out=WT[:, woff:woff + 1024].rearrange("p (a q) -> p a q", q=128),
                            in_=w1qT_h[fc, :, :, :].rearrange("a p q -> p a q")).inc("dw", 16)
                wdma = r.snap("dw")
                t.wait("dw", wdma)
                if fc == 0:
                    t.wait("v", zq_v)
                for tc in range(NTC):
                    ps, pn = (psA, "psA") if tc % 2 == 0 else (psB, "psB")
                    t.waitev(r.getfree(pn))
                    last = None
                    for dc in range(DC):
                        last = t.matmul(ps[:, :],
                                        WT[:, woff + dc * 128:woff + (dc + 1) * 128],
                                        zbt(dc)[:, 128 + tc * 512:128 + (tc + 1) * 512],
                                        start=(dc == 0), stop=(dc == DC - 1))
                    last.inc("t", 1)
                    hs = f"hti{(fc * NTC + tc) % 2}"
                    hoff = ((fc * NTC + tc) % 2) * 1024
                    hti = HTI[:, hoff:hoff + 512]
                    s.wait("t", r.snap("t"))
                    s.wait("v", zq_v)
                    s.waitev(r.getfree(hs))
                    s.activation(hti, ps[:, :], AF.Relu,
                                 bias=SMALL[:, cB1QS + fc:cB1QS + fc + 1],
                                 scale=SMALL[:, cSWS1 + fc:cSWS1 + fc + 1]).inc("s", 1)
                    v.wait("t", r.snap("t"))
                    v.wait("s", r.snap("s"))
                    v.tensor_reduce(SMALL[:, cTMP:cTMP + 1], ps[:, :], AX.X, ALU.max)
                    v.tensor_tensor(SMALL[:, cHU + fc:cHU + fc + 1],
                                    SMALL[:, cHU + fc:cHU + fc + 1],
                                    SMALL[:, cTMP:cTMP + 1], ALU.max).inc("v", 1)
                    r.setfree(pn, "v", "s")
                    g.wait("s", r.snap("s"))
                    g.dma_start(out=h_h[fc, tc], in_=hti).inc("gh", 16)
                    r.setfree(hs, "gh")
                r.setfree(key, "t")
            if flvl < 4:
                return
            v.tensor_tensor(FSC[:, 0:32], SMALL[:, cHU:cHU + 32],
                            SMALL[:, cSWS1:cSWS1 + 32], ALU.mult)
            v.tensor_tensor(FSC[:, 0:32], FSC[:, 0:32], SMALL[:, cB1QS:cB1QS + 32],
                            ALU.add)
            v.tensor_scalar(FSC[:, 0:32], FSC[:, 0:32], 0.0, None, ALU.max)
            v.tensor_reduce(SMALL[:, cTMP + 2:cTMP + 3], FSC[:, 0:32], AX.X,
                            ALU.max)
            pmax_to_scalar(SMALL[:, cTMP + 2:cTMP + 3], P0S[0:1, 1536:1537])
            v.memset(FSC[0:1, 1152:1168], -3.0e38)
            v.tensor_scalar(FSC[0:1, 1152:1168], FSC[0:1, 1152:1168],
                            P0S[0:1, 1536:1537], None, ALU.max).inc("v", 1)
            g.wait("v", r.snap("v"))
            g.dma_start(out=mxi_h[0:1, :], in_=FSC[0:1, 1152:1168]).inc("dmx", 16)
            g.wait("dmx", r.snap("dmx"))
            g.collective_compute("AllReduce", ALU.max, replica_groups=[list(range(NCORE))],
                                 ins=[mxi_h], outs=[mxo_h]).inc("cc", 1)
            g.wait("cc", r.snap("cc"))
            g.dma_start(out=P0S[0:1, 1537:1538], in_=mxo_h[0:1, 0:1]).inc("dmx", 16)
            v.wait("dmx", r.snap("dmx"))
            v.tensor_scalar(P0S[0:1, 1538:1539], P0S[0:1, 1537:1538], 1e-8, 1.0 / 255.0,
                            ALU.max, ALU.mult)
            v.reciprocal(P0S[0:1, 1539:1540], P0S[0:1, 1538:1539])
            t.wait("v", r.snap("v"))
            t.waitev(r.getfree("psC"))
            t.matmul(psC[:, 0:2], ONE_RF[0:1, :], P0S[0:1, 1538:1540],
                     start=True, stop=True).inc("t", 1)
            v.wait("t", r.snap("t"))
            v.tensor_copy(SMALL[:, cSCB + 2:cSCB + 4], psC[:, 0:2]).inc("v", 1)
            v.wait("v", r.snap("v"))
            r.setfree("psC", "v")
            v.tensor_scalar(SMALL[:, cSWS2:cSWS2 + 8], SMALL[:, cSW2:cSW2 + 8],
                            SMALL[:, cSCB + 2:cSCB + 3], None, ALU.mult)
            v.tensor_scalar(SMALL[:, cSWS2:cSWS2 + 8], SMALL[:, cSWS2:cSWS2 + 8],
                            1.0 / 127.0, None, ALU.mult)
            v.reciprocal(FSC[:, 0:8], SMALL[:, cSWS2:cSWS2 + 8])
            v.tensor_tensor(FSC[:, 8:16], SMALL[:, cB2R:cB2R + 8], FSC[:, 0:8], ALU.mult)
            v.tensor_scalar(FSC[:, 8:16], FSC[:, 8:16], 0.5, None, ALU.add)
            v.tensor_copy(ISC[:, 0:8], FSC[:, 8:16])
            v.tensor_copy(FSC[:, 16:24], ISC[:, 0:8])
            v.tensor_tensor(FSC[:, 24:32], FSC[:, 16:24], FSC[:, 8:16], ALU.is_gt)
            v.scalar_tensor_tensor(FSC[:, 16:24], FSC[:, 24:32], -1.0, FSC[:, 16:24],
                                   ALU.mult, ALU.add)
            v.tensor_tensor(SMALL[:, cB2QS:cB2QS + 8], FSC[:, 16:24],
                            SMALL[:, cSWS2:cSWS2 + 8], ALU.mult).inc("v", 1)
            sc2_v = r.snap("v")
            if flvl < 9:
                return
            mm1_dh = r.snap("gh")
            mm1_t = r.snap("t")
            y.wait("gh", mm1_dh)
            y.wait("t", mm1_t)
            for tc in range(NTC):
                v.waitev(r.getfree("HQ"))
                for fc in range(NFC):
                    if SERIAL_ATT:
                        v.wait("t", r.snap("t"))
                        y.wait("v", r.snap("v"))
                    hs = f"hin{fc % 2}"
                    y.waitev(r.getfree(hs))
                    hin = HTI[:, (fc % 2) * 1024:(fc % 2) * 1024 + 512]
                    y.dma_start(out=hin, in_=h_h[fc, tc]).inc("dh", 16)
                    v.wait("dh", r.snap("dh"))
                    if fc == 0 and tc == 0:
                        v.wait("v", sc2_v)
                    v.tensor_scalar(FSC[:, 0:512], hin, SMALL[:, cSCB + 3:cSCB + 4],
                                    0.5, ALU.mult, ALU.add)
                    r.setfree(hs, "v")
                    v.tensor_copy(ISC[:, 0:512], FSC[:, 0:512])
                    v.tensor_copy(FSC[:, 512:1024], ISC[:, 0:512])
                    v.tensor_tensor(FSC[:, 1024:1536], FSC[:, 512:1024], FSC[:, 0:512],
                                    ALU.is_gt)
                    v.scalar_tensor_tensor(CTXT[:, fc * 512:(fc + 1) * 512],
                                           FSC[:, 1024:1536], -1.0, FSC[:, 512:1024],
                                           ALU.mult, ALU.add).inc("v", 1)
                hq_v = r.snap("v")
                for dc in range(DC):
                    if SERIAL_ATT:
                        t.wait("v", r.snap("v"))
                        t.wait("s", r.snap("s"))
                        v.wait("t", r.snap("t"))
                        y.wait("t", r.snap("t"))
                        y.wait("v", r.snap("v"))
                    for half in range(2):
                        key = "W0W1" if half == 0 else "W2W3"
                        off = half * 2048
                        y.waitev(r.getfree(key))
                        y.dma_start(out=WT[:, off:off + 2048]
                                    .rearrange("p (a q) -> p a q", q=128),
                                    in_=w2qT_h[dc, half * 16:(half + 1) * 16, :, :]
                                    .rearrange("a p q -> p a q")).inc("dw", 16)
                    wdma = r.snap("dw")
                    t.wait("dw", wdma)
                    t.wait("v", hq_v)
                    ps, pn = (psC, "psC") if dc % 2 == 0 else (psD, "psD")
                    t.waitev(r.getfree(pn))
                    last = None
                    for fc in range(NFC):
                        last = t.matmul(ps[:, :], WT[:, fc * 128:(fc + 1) * 128],
                                        CTXT[:, fc * 512:(fc + 1) * 512],
                                        start=(fc == 0), stop=(fc == NFC - 1))
                    last.inc("t", 1)
                    r.setfree("W0W1", "t")
                    r.setfree("W2W3", "t")
                    v.wait("t", r.snap("t"))
                    dst = xt(dc)[:, 128 + tc * 512:128 + (tc + 1) * 512]
                    v.tensor_scalar(FSC[:, 0:512], ps[:, :],
                                    SMALL[:, cSWS2 + dc:cSWS2 + dc + 1], None, ALU.mult)
                    v.tensor_scalar(FSC[:, 0:512], FSC[:, 0:512],
                                    SMALL[:, cB2QS + dc:cB2QS + dc + 1], None, ALU.add)
                    v.tensor_tensor(dst, dst, FSC[:, 0:512], ALU.add).inc("v", 1)
                    r.setfree(pn, "v")
                r.setfree("HQ", "t")

        emit_ffn(0)
        if stop_after in ("p2zqd", "p2sind"):
            return
        if stop_after in ("p2", "p2ln", "p2zq", "p2m1", "p2s2"):
            emit_store()
            return

        phase_sync("tvsgy")
        emit_ln(LN_T, zb_out=True, zmax=False)
        ln3_v = r.snap("v")
        for hp in range(NHP):
            key = "W0" if hp % 2 == 0 else "W1"
            woff = 0 if hp % 2 == 0 else 1024
            g.waitev(r.getfree(key))
            g.dma_start(out=WT[:, woff:woff + 1024].rearrange("p (a q) -> p a q", q=128),
                        in_=wkvb.rearrange("(a p) q -> p a q", p=128)
                        [:, :, hp * 128:(hp + 1) * 128]).inc("gw", 16)
            wdma = r.snap(            "gw")
            t.wait("dw", wdma)
            if hp == 0:
                t.wait("v", ln3_v)
            for tc in range(NTC):
                ps, pn = (psA, "psA") if tc % 2 == 0 else (psB, "psB")
                t.waitev(r.getfree(pn))
                last = None
                for dc in range(DC):
                    last = t.matmul(ps[:, :], WT[:, woff + dc * 128:woff + (dc + 1) * 128],
                                    zbt(dc)[:, 128 + tc * 512:128 + (tc + 1) * 512],
                                    start=(dc == 0), stop=(dc == DC - 1))
                last.inc("t", 1)
                ks = f"kvs{tc % 2}"
                v.wait("t", r.snap("t"))
                v.waitev(r.getfree(ks))
                kstg = KVS[:, (tc % 2) * 512:(tc % 2) * 512 + 512]
                v.tensor_copy(kstg, ps[:, :]).inc("v", 1)
                r.setfree(pn, "v")
                g.wait("v", r.snap("v"))
                g.dma_start(out=kin_h[hp * 128:(hp + 1) * 128, tc * 512:(tc + 1) * 512],
                            in_=kstg).inc("gkv", 16)
                r.setfree(ks, "gkv")
            r.setfree(key, "v")
        for fh in range(2):
            g.waitev(r.getfree("VRHS"))
            g.dma_start(out=CTXT[:, 0:4096].rearrange("p (a q) -> p a q", q=512),
                        in_=wkvb.rearrange("(a p) q -> p a q", p=128)
                        [:, :, D + fh * 512:D + (fh + 1) * 512]).inc("gw", 16)
            wdma = r.snap(            "gw")
            t.wait("dw", wdma)
            for tt in range(16):
                ps, pn = (psA, "psA") if tt % 2 == 0 else (psB, "psB")
                t.waitev(r.getfree(pn))
                last = None
                for dc in range(DC):
                    last = t.matmul(ps[:, :], zbt(dc)[:, 128 + tt * 128:128 + (tt + 1) * 128],
                                    CTXT[:, dc * 512:(dc + 1) * 512],
                                    start=(dc == 0), stop=(dc == DC - 1))
                last.inc("t", 1)
                ks = f"kvs{tt % 2}"
                v.wait("t", r.snap("t"))
                v.waitev(r.getfree(ks))
                kstg = KVS[:, (tt % 2) * 512:(tt % 2) * 512 + 512]
                v.tensor_copy(kstg, ps[:, :]).inc("v", 1)
                r.setfree(pn, "v")
                g.wait("v", r.snap("v"))
                g.dma_start(out=vin_h[tt * 128:(tt + 1) * 128, fh * 512:(fh + 1) * 512],
                            in_=kstg).inc("gkv", 16)
                r.setfree(ks, "gkv")
            r.setfree("VRHS", "t")
        g.wait("gkv", r.snap("gkv"))
        g.collective_compute("AllGather", ALU.bypass, replica_groups=[list(range(NCORE))],
                             ins=[kin_h], outs=[kout_h]).inc("cc", 1)
        g.collective_compute("AllGather", ALU.bypass, replica_groups=[list(range(NCORE))],
                             ins=[vin_h], outs=[vout_h]).inc("cc", 1)
        ag_cc = r.snap("cc")
        for hp in range(NHP):
            key = "W0" if hp % 2 == 0 else "W1"
            woff = 0 if hp % 2 == 0 else 1024
            g.waitev(r.getfree(key))
            g.dma_start(out=WT[:, woff:woff + 1024].rearrange("p (a q) -> p a q", q=128),
                        in_=wqb.rearrange("(a p) q -> p a q", p=128)
                        [:, :, hp * 128:(hp + 1) * 128]).inc("gw", 16)
            wdma = r.snap(            "gw")
            t.wait("dw", wdma)
            qb0 = (hp % 2) * 2048
            for tc in range(NTC):
                ps, pn = (psA, "psA") if tc % 2 == 0 else (psB, "psB")
                t.waitev(r.getfree(pn))
                t.waitev(r.getfree(f"hpbq{hp % 2}"))
                last = None
                for dc in range(DC):
                    last = t.matmul(ps[:, :], WT[:, woff + dc * 128:woff + (dc + 1) * 128],
                                    zbt(dc)[:, 128 + tc * 512:128 + (tc + 1) * 512],
                                    start=(dc == 0), stop=(dc == DC - 1))
                last.inc("t", 1)
                s.wait("t", r.snap("t"))
                s.copy(HPB[:, qb0 + tc * 512:qb0 + (tc + 1) * 512], ps[:, :]).inc("s", 1)
                r.setfree(pn, "s")
            r.setfree(key, "s")
            qproj_s = r.snap("s")
            if hp == 0:
                g.wait("cc", ag_cc)
                g.dma_start(out=P0S[0:1, 1663:1664], in_=mxo_h[0:1, 0:1]).inc("dmx", 16)
            agw = r.snap("dmx")
            for qs in range(4):
                q0 = qb0 + qs * 512
                t.waitev(r.getfree("psA"))
                t.waitev(r.getfree("psC"))
                if hp == 0 and qs == 0:
                    t.wait("s", qproj_s)
                t.matmul(psA[:, :], ZEROB[:, :], MASKN[:, :], start=True, stop=False,
                         skip_group_check=True)
                t.matmul(psC[:, :], ZEROB[:, :], MASKN[:, :], start=True, stop=False,
                         skip_group_check=True)
                for kb in range(NKB):
                    ch = kb // 4
                    if kb % 4 == 0:
                        rank, cl0 = kb // 16, (kb % 16) * 128
                        kslot = f"bk{ch % 2}"
                        y.wait("dmx", agw)
                        y.waitev(r.getfree(kslot))
                        y.dma_start(out=KVS[:, (ch % 2) * 512:(ch % 2) * 512 + 512],
                                    in_=kout_h[rank * D + hp * 128:rank * D + (hp + 1) * 128,
                                               cl0:cl0 + 512]).inc("dkv", 16)
                        y.dma_start(out=KVS[:, 1024 + (ch % 2) * 512:1024 + (ch % 2) * 512 + 512]
                                    .rearrange("p (a q) -> p a q", q=128),
                                    in_=vout_h[kb * 128:(kb + 4) * 128,
                                               hp * 128:(hp + 1) * 128]
                                    .rearrange("(a p) q -> p a q", p=128)).inc("dkv", 16)
                    kv_dma = r.snap("dkv")
                    ko = (ch % 2) * 512 + (kb % 4) * 128
                    vo = 1024 + (ch % 2) * 512 + (kb % 4) * 128
                    so = (kb % 2) * 512
                    t.wait("dkv", kv_dma)
                    t.waitev(r.getfree(f"psS1{kb % 2}"))
                    t.matmul(psS1[:, so:so + 512], KVS[0:64, ko:ko + 128],
                             HPB[0:64, q0:q0 + 512], start=True, stop=True)
                    t.waitev(r.getfree(f"psS2{kb % 2}"))
                    t.matmul(psS2[:, so:so + 512], KVS[64:128, ko:ko + 128],
                             HPB[64:128, q0:q0 + 512], start=True, stop=True,
                             tile_position=(64, 0)).inc("t", 1)
                    sc_t = r.snap("t")
                    p1s, p2s = f"pb1{kb % 2}", f"pb2{kb % 2}"
                    pt1 = SHUF[:, (kb % 2) * 512:(kb % 2) * 512 + 512]
                    pt2 = RTMP[:, (kb % 2) * 512:(kb % 2) * 512 + 512]
                    s.wait("t", sc_t)
                    s.waitev(r.getfree(p1s))
                    s.activation(pt1, psS1[:, so:so + 512], AF.Exp, bias=-EXPB_B,
                                 scale=SCALE).inc("s", 1)
                    r.setfree(f"psS1{kb % 2}", "s")
                    s.waitev(r.getfree(p2s))
                    s.activation(pt2, psS2[:, so:so + 512], AF.Exp, bias=-EXPB_B,
                                 scale=SCALE).inc("s", 1)
                    r.setfree(f"psS2{kb % 2}", "s")
                    exp_s = r.snap("s")
                    t.wait("s", exp_s)
                    t.matmul(psA[0:64, :], KVS[:, vo:vo + 64], pt1,
                             start=False, stop=False, skip_group_check=True)
                    t.matmul(psA[64:128, :], KVS[:, vo + 64:vo + 128], pt2,
                             start=False, stop=(kb == NKB - 1), tile_position=(0, 64),
                             skip_group_check=True)
                    t.matmul(psC[0:1, :], ONESB[:, 0:1], pt1,
                             start=False, stop=False, skip_group_check=True)
                    last = t.matmul(psC[32:33, :], ONESB[:, 0:1], pt2,
                                    start=False, stop=(kb == NKB - 1),
                                    tile_position=(0, 32), skip_group_check=True)
                    last.inc("t", 1)
                    r.setfree(p1s, "t")
                    r.setfree(p2s, "t")
                    if kb % 4 == 3:
                        r.setfree(f"bk{ch % 2}", "t")
                v.wait("t", r.snap("t"))
                v.reciprocal(P0S[0:1, 0:512], psC[0:1, :])
                v.reciprocal(FSC[32:33, 0:512], psC[32:33, :])
                v.tensor_copy(P0S[0:1, 512:1024], FSC[32:33, 0:512]).inc("v", 1)
                r.setfree("psC", "v")
                t.wait("v", r.snap("v"))
                t.waitev(r.getfree("psB"))
                t.matmul(psB[:, :], ZEROB[:, :], MASKN[:, :], start=True, stop=False,
                         skip_group_check=True)
                t.matmul(psB[0:64, :], ONE_RF[0:1, 0:64], P0S[0:1, 0:512],
                         start=False, stop=False, skip_group_check=True)
                t.matmul(psB[64:128, :], ONE_RF[0:1, 0:64], P0S[0:1, 512:1024],
                         start=False, stop=True, tile_position=(0, 64),
                         skip_group_check=True).inc("t", 1)
                v.wait("t", r.snap("t"))
                v.tensor_copy(CTMP[:, 0:512], psA[:, :])
                v.tensor_tensor(ctxt(hp)[:, qs * 512:qs * 512 + 512], CTMP[:, 0:512],
                                psB[:, :], ALU.mult).inc("v", 1)
                r.setfree("psA", "v")
                r.setfree("psB", "v")
            r.setfree(f"hpbq{hp % 2}", "t")
        attnb_v = r.snap("v")
        for dc in range(DC):
            key = "W0" if dc % 2 == 0 else "W1"
            woff = 0 if dc % 2 == 0 else 1024
            g.waitev(r.getfree(key))
            g.dma_start(out=WT[:, woff:woff + 1024].rearrange("p (a q) -> p a q", q=128),
                        in_=wob.rearrange("(a p) q -> p a q", p=128)
                        [:, :, dc * 128:(dc + 1) * 128]).inc("gw", 16)
            wdma = r.snap(            "gw")
            t.wait("dw", wdma)
            if dc == 0:
                t.wait("v", attnb_v)
            for tc in range(NTC):
                ps, pn = (psA, "psA") if tc % 2 == 0 else (psB, "psB")
                t.waitev(r.getfree(pn))
                last = None
                for hp in range(NHP):
                    last = t.matmul(ps[:, :], WT[:, woff + hp * 128:woff + (hp + 1) * 128],
                                    ctxt(hp)[:, tc * 512:(tc + 1) * 512],
                                    start=(hp == 0), stop=(hp == NHP - 1))
                last.inc("t", 1)
                v.wait("t", r.snap("t"))
                dst = xt(dc)[:, 128 + tc * 512:128 + (tc + 1) * 512]
                v.scalar_tensor_tensor(dst, ps[:, :], SMALL[:, cBOB + dc:cBOB + dc + 1],
                                       dst, ALU.add, ALU.add).inc("v", 1)
                r.setfree(pn, "v")
            r.setfree(key, "v")

        if stop_after == "p3":
            emit_store()
            return

        emit_ffn(1)

        emit_store()

    return nc, gen


def compile_kernel(stop_after="p5"):
    nc, gen = build_program(stop_after)
    import contextlib
    stack = contextlib.ExitStack()
    semhs = {nm: stack.enter_context(nc.semaphore(f"sem_{nm}")) for nm in SEM_NAMES}
    with nc.Block() as block:
        @block.tensor
        def _(eng):
            gen(Router("t", {"t": eng, "v": None, "s": None, "g": None, "y": None}, semhs))
        @block.vector
        def _(eng):
            gen(Router("v", {"t": None, "v": eng, "s": None, "g": None, "y": None}, semhs))
        @block.scalar
        def _(eng):
            gen(Router("s", {"t": None, "v": None, "s": eng, "g": None, "y": None}, semhs))
        @block.gpsimd
        def _(eng):
            gen(Router("g", {"t": None, "v": None, "s": None, "g": eng, "y": None}, semhs))
        @block.sync
        def _(eng):
            gen(Router("y", {"t": None, "v": None, "s": None, "g": None, "y": eng}, semhs))
    stack.close()
    return nc


def make_tables(core):
    import ml_dtypes
    bf = ml_dtypes.bfloat16
    pos0 = core * T - HL
    pos = pos0 + np.arange(TH, dtype=np.float64)
    inv_freq = 1.0 / (10000.0 ** (np.arange(0, 64, 2, dtype=np.float64) / 64.0))
    ang = pos[:, None] * inv_freq[None, :]
    cos = np.concatenate([np.cos(ang), np.cos(ang)], axis=1)
    sin = np.concatenate([np.sin(ang), np.sin(ang)], axis=1)
    sin_signed = sin.copy()
    sin_signed[:, :32] *= -1.0
    cosT = np.zeros((128, TH), dtype=np.float64)
    sinT = np.zeros((128, TH), dtype=np.float64)
    cosT[0:64, :] = cos.T
    cosT[64:128, :] = cos.T
    sinT[0:64, :] = sin_signed.T
    sinT[64:128, :] = sin_signed.T
    kk = np.arange(128)[:, None]
    qq = np.arange(128)[None, :]
    mprev = (kk >= qq).astype(np.float64)
    mcur = (kk <= qq).astype(np.float64)
    maskn = np.concatenate([mprev, mcur, mprev, mcur], axis=1)
    maskf = maskn.copy()
    if core == 0:
        maskf[:, 0:128] = 0.0
        maskf[:, 256:384] = 0.0
    return (cosT.astype(bf), sinT.astype(bf), maskn.astype(bf), maskf.astype(bf))


def make_inmaps(inputs):
    import ml_dtypes
    bf = ml_dtypes.bfloat16
    x_full = np.ascontiguousarray(np.asarray(inputs["L0_attn_a_hidden_states"],
                                             dtype=np.float32).reshape(SEQ, D))
    common = dict(
        wqa=np.ascontiguousarray(np.asarray(inputs["L0_attn_a_Wq"], dtype=np.float32)),
        bqa=np.ascontiguousarray(np.asarray(inputs["L0_attn_a_bq"], dtype=np.float32)),
        wka=np.ascontiguousarray(np.asarray(inputs["L0_attn_a_Wk"], dtype=np.float32)),
        bka=np.ascontiguousarray(np.asarray(inputs["L0_attn_a_bk"], dtype=np.float32)),
        wva=np.ascontiguousarray(np.asarray(inputs["L0_attn_a_Wv"], dtype=np.float32)),
        bva=np.ascontiguousarray(np.asarray(inputs["L0_attn_a_bv"], dtype=np.float32)),
        woa=np.ascontiguousarray(np.asarray(inputs["L0_attn_a_Wo"], dtype=np.float32)),
        boa=np.ascontiguousarray(np.asarray(inputs["L0_attn_a_bo"], dtype=np.float32)),
        wqb=np.ascontiguousarray(np.asarray(inputs["L0_attn_b_Wq"], dtype=np.float32)),
        wkvb=np.ascontiguousarray(np.asarray(inputs["L0_attn_b_Wkv"], dtype=np.float32)),
        wob=np.ascontiguousarray(np.asarray(inputs["L0_attn_b_Wo"], dtype=np.float32)),
        bob=np.ascontiguousarray(np.asarray(inputs["L0_attn_b_bo"], dtype=np.float32)),
        w1=np.ascontiguousarray(np.asarray(inputs["L0_ffn_w1"], dtype=np.float32)),
        b1=np.ascontiguousarray(np.asarray(inputs["L0_ffn_b1"], dtype=np.float32)),
        w2=np.ascontiguousarray(np.asarray(inputs["L0_ffn_w2"], dtype=np.float32)),
        b2=np.ascontiguousarray(np.asarray(inputs["L0_ffn_b2"], dtype=np.float32)),
        ident=np.eye(128, dtype=np.float32),
        identb=np.eye(128).astype(bf),
        consts=np.array([[1e-5, -EXPB_A, -EXPB_B, 0.5, 0, 0, 0, 0]], dtype=np.float32),
        constsb=np.tile(np.array([[-EXPB_A, -EXPB_B]], dtype=np.float32), (128, 1)),
    )
    in_maps = []
    for c in range(NCORE):
        xs = np.zeros((TH, D), dtype=np.float32)
        if c == 0:
            xs[HL:, :] = x_full[0:T, :]
        else:
            xs[:, :] = x_full[c * T - HL:c * T - HL + TH, :]
        cosT, sinT, maskn, maskf = make_tables(c)
        m = dict(common)
        m.update(x=xs, cosT=cosT, sinT=sinT, maskn=maskn, maskf=maskf)
        in_maps.append(m)
    return in_maps


_NC_CACHE = {}


def kernel(**inputs):
    from concourse.bass_utils import run_bass_kernel_spmd
    if "nc" not in _NC_CACHE:
        _NC_CACHE["nc"] = compile_kernel("p5")
    nc = _NC_CACHE["nc"]
    in_maps = make_inmaps(inputs)
    res = run_bass_kernel_spmd(nc, in_maps, core_ids=list(range(NCORE)))
    out = np.concatenate([res.results[c]["out"] for c in range(NCORE)], axis=0)
    return out.reshape(1, SEQ, D).astype(np.float32)


# version 03
# speedup vs baseline: 1.6057x; 1.6057x over previous
import numpy as np
import concourse.bass as bass
import concourse.mybir as mybir

F32 = mybir.dt.float32
BF16 = mybir.dt.bfloat16
I32 = mybir.dt.int32
AF = mybir.ActivationFunctionType
ALU = mybir.AluOpType
AX = mybir.AxisListType

NCORE = 8
T = 2048
HL = 128
TH = T + HL
D = 1024
DC = 8
NHP = 8
FF = 4096
NFC = 32
SEQ = 16384
NKB = 128
NTC = 4
LN_TH = [(0, 512), (512, 512), (1024, 512), (1536, 512), (2048, 128)]
LN_T = [(128, 512), (640, 512), (1152, 512), (1664, 512)]
EXPB_A = 6.0
EXPB_B = 24.0
SCALE = 0.125
INV_D = 1.0 / 1024.0
SERIAL_ATT = False

DSTREAMS = ["dx", "dw", "dwz", "dq", "dkv", "dh", "dmx", "dout",
            "gx", "gw", "gq", "gh", "gkv"]
SEM_NAMES = [s + p for s in DSTREAMS for p in ("0", "1")] + ["t", "v", "s", "g", "cc"]


class Inst:
    __slots__ = ("r", "i")
    def __init__(self, r, i):
        self.r, self.i = r, i
    def inc(self, sem, n=1):
        self.r.cnt[sem] = self.r.cnt.get(sem, 0) + n
        if self.i is not None:
            self.i.then_inc(self.r.semh[sem], n)
        return self


class PendingDma:
    __slots__ = ("eng", "a", "k")
    def __init__(self, eng, a, k):
        self.eng, self.a, self.k = eng, a, k
    def inc(self, stream, n=16):
        r = self.eng.r
        idx = r.dsn.get(stream, 0)
        sem = f"{stream}{idx % 2}"
        r.dsn[stream] = idx + 1
        self.eng.wait(sem)
        r.cnt[sem] = r.cnt.get(sem, 0) + n
        if self.eng.real is not None:
            self.eng.real.dma_start(*self.a, **self.k).then_inc(r.semh[sem], n)
        return self


FENCED_OPS = {"tensor_reduce", "reduce_max", "reduce_sum", "reciprocal"}


class FInst(Inst):
    __slots__ = ("eng",)
    def __init__(self, r, i, eng):
        super().__init__(r, i)
        self.eng = eng
    def inc(self, sem, n=1):
        super().inc(sem, n)
        if self.eng.pend is self:
            self.eng.pend = ("done", sem, self.r.cnt[sem])
        return self


class Eng:
    def __init__(self, r, real, name=None):
        self.r, self.real, self.name = r, real, name
        self.pend = None
    def _flush(self):
        p = self.pend
        if p is None:
            return
        self.pend = None
        if isinstance(p, FInst):
            p_inc = Inst.inc
            p_inc(p, self.name, 1)
            self.wait(self.name)
        else:
            _, sem, val = p
            self.wait(sem, val)
    def __getattr__(self, op):
        real, r = self.real, self.r
        if op == "dma_start":
            def dcall(*a, **k):
                self._flush()
                return PendingDma(self, a, k)
            return dcall
        if self.name in ("v", "s", "g") and op not in ("wait_ge",):
            sem = self.name
            def fcall(*a, **k):
                self._flush()
                fence = op in FENCED_OPS
                if not fence and a:
                    try:
                        sh = list(a[0].shape)
                        free = 1
                        for d in sh[1:]:
                            free *= d
                        fence = free <= 64
                    except Exception:
                        pass
                if fence:
                    inst = FInst(r, getattr(real, op)(*a, **k) if real is not None else None, self)
                    self.pend = inst
                    return inst
                return Inst(r, getattr(real, op)(*a, **k) if real is not None else None)
            return fcall
        def call(*a, **k):
            return Inst(r, getattr(real, op)(*a, **k) if real is not None else None)
        return call
    def wait(self, sem, val=None):
        if sem in DSTREAMS:
            evs = val if isinstance(val, list) else self.r.dsnap(sem)
            for sm, vv in evs:
                if self.real is not None and vv > 0:
                    self.real.wait_ge(self.r.semh[sm], vv)
            return evs
        v = self.r.cnt.get(sem, 0) if val is None else val
        if self.real is not None and v > 0:
            self.real.wait_ge(self.r.semh[sem], v)
        return v
    def waitev(self, evs):
        if evs:
            for sem, val in evs:
                self.wait(sem, val)


class Router:
    def __init__(self, active, engines, semh):
        self.cnt = {}
        self.semh = semh
        self.free = {}
        self.dsn = {}
        for nm in ("t", "v", "s", "g", "y"):
            setattr(self, nm, Eng(self, engines[nm] if active == nm else None, nm))
    def snap(self, sem):
        if sem in DSTREAMS:
            return self.dsnap(sem)
        return self.cnt.get(sem, 0)
    def setfree(self, buf, *sems):
        evs = []
        for sm in sems:
            if sm in DSTREAMS:
                evs.extend(self.dsnap(sm))
            else:
                evs.append((sm, self.cnt.get(sm, 0)))
        self.free[buf] = evs
    def getfree(self, buf):
        return self.free.get(buf)
    def dma(self, E, stream, *a, **kw):
        idx = self.dsn.get(stream, 0)
        sem = f"{stream}{idx % 2}"
        E.wait(sem)
        E.dma_start(*a, **kw).inc(sem, 16)
        self.dsn[stream] = idx + 1
    def dsnap(self, stream):
        return [(f"{stream}0", self.cnt.get(f"{stream}0", 0)),
                (f"{stream}1", self.cnt.get(f"{stream}1", 0))]


def build_program(stop_after="p5"):
    nc = bass.Bass("TRN2", target_bir_lowering=False, detect_race_conditions=False)

    def din(name, shape, dt=F32):
        return nc.dram_tensor(name, shape, dt, kind="ExternalInput").ap()
    x_in = din("x", [TH, D])
    wqa, bqa = din("wqa", [D, D]), din("bqa", [D])
    wka, bka = din("wka", [D, D]), din("bka", [D])
    wva, bva = din("wva", [D, D]), din("bva", [D])
    woa, boa = din("woa", [D, D]), din("boa", [D])
    wqb = din("wqb", [D, D])
    wkvb = din("wkvb", [D, 2 * D])
    wob, bob = din("wob", [D, D]), din("bob", [D])
    w1, b1 = din("w1", [FF, D]), din("b1", [FF])
    w2, b2 = din("w2", [D, FF]), din("b2", [D])
    cos_in = din("cosT", [128, TH], BF16)
    sin_in = din("sinT", [128, TH], BF16)
    maskn_in = din("maskn", [128, 512], BF16)
    maskf_in = din("maskf", [128, 512], BF16)
    ident_in = din("ident", [128, 128])
    identb_in = din("identb", [128, 128], BF16)
    consts_in = din("consts", [1, 8])
    constsb_in = din("constsb", [128, 2])
    out = nc.dram_tensor("out", [T, D], F32, kind="ExternalOutput").ap()

    def dint(name, shape, dt=BF16):
        return nc.dram_tensor(name, shape, dt, kind="Internal").ap()
    wstage = dint("wstage", [D, D])
    waT = [dint(f"w{n}aT_h", [D, D]) for n in ("q", "k", "v", "o")]
    w1q_h = dint("w1q_h", [FF, D])
    w2q_h = dint("w2q_h", [D, FF])
    w1qT_h = dint("w1qT_h", [NFC, DC, 128, 128])
    w2qT_h = dint("w2qT_h", [DC, NFC, 128, 128])
    h_h = dint("h_h", [NFC, NTC, 128, 512])
    kin_h = dint("kin_h", [D, T])
    vin_h = dint("vin_h", [T, D])
    kout_h = dint("kout_h", [NCORE * D, T])
    vout_h = dint("vout_h", [SEQ, D])
    mxi_h = dint("mxi_h", [1, 16], F32)
    mxo_h = dint("mxo_h", [1, 16], F32)
    pmr_h = dint("pmr_h", [128, 4], F32)

    def sal(name, shape, dt):
        return nc.alloc_sbuf_tensor(name, shape, dt).ap()
    XT = sal("XT", [128, DC * TH], F32)
    ZBT = sal("ZBT", [128, DC * TH], BF16)
    CTXT = sal("CTXT", [128, NHP * T], BF16)
    HPB = sal("HPB", [128, 2048 + TH + TH], BF16)
    WT = sal("WT", [128, 5120], BF16)
    COS = sal("COS", [128, TH], BF16)
    SIN = sal("SIN", [128, TH], BF16)
    SHUF = sal("SHUF", [128, 2176], BF16)
    RTMP = sal("RTMP", [128, 2176], BF16)
    MASKN = sal("MASKN", [128, 512], BF16)
    MASKF = sal("MASKF", [128, 512], BF16)
    IDE = sal("IDE", [128, 128], F32)
    IDEB = sal("IDEB", [128, 128], BF16)
    ONESB = sal("ONESB", [128, 2], BF16)
    ZEROB = sal("ZEROB", [128, 128], BF16)
    ONE_RB = sal("ONE_RB", [1, 128], BF16)
    ONE_RF = sal("ONE_RF", [1, 128], F32)
    BVROW = sal("BVROW", [1, D], BF16)
    ROWQ = sal("ROWQ", [33, TH], F32)
    P0S = sal("P0S", [1, 1664], F32)
    FSC = sal("FSC", [128, 3072], F32)
    CTMP = FSC[:, 2048:3072]
    CTN = sal("CTN", [128, 256], BF16)
    HTI = HPB[:, 0:2048]
    KVS = HPB[:, 4096:6144]
    ISC = KVS.bitcast(I32)
    SMALL = sal("SMALL", [128, 256], F32)
    CONS = sal("CONS", [1, 8], F32)
    CONSB = sal("CONSB", [128, 2], F32)
    cBQA, cBKA, cBOA, cBOB = 0, 8, 16, 24
    cSW1, cSW2, cSWS1, cSWS2 = 32, 64, 72, 104
    cB1QS, cB2QS, cB1R, cB2R = 112, 144, 152, 184
    cU, cHU, cSCB, cTMP = 192, 200, 232, 240

    psS1 = nc.alloc_psum_tensor("psS1", [128, 1024], F32).ap()
    psS2 = nc.alloc_psum_tensor("psS2", [128, 1024], F32).ap()
    psA = nc.alloc_psum_tensor("psA", [128, 512], F32).ap()
    psB = nc.alloc_psum_tensor("psB", [128, 512], F32).ap()
    psC = nc.alloc_psum_tensor("psC", [128, 512], F32).ap()
    psD = nc.alloc_psum_tensor("psD", [128, 512], F32).ap()

    nc.const_aps.register(F32, 1e-5, CONS[0:1, 0:1])
    nc.const_aps.register(F32, -EXPB_A, CONSB[:, 0:1])
    nc.const_aps.register(F32, -EXPB_B, CONSB[:, 1:2])

    def xt(dc):
        return XT[:, dc * TH:(dc + 1) * TH]
    def zbt(dc):
        return ZBT[:, dc * TH:(dc + 1) * TH]
    def ctxt(hp):
        return CTXT[:, hp * T:(hp + 1) * T]

    def gen(r: Router):
        t, v, s, g, y = r.t, r.v, r.s, r.g, r.y

        def phase_sync(engines="tvsg"):
            sems = ["t", "v", "s", "dw", "dwz", "dq", "dh", "dkv", "dmx"]
            for en in engines:
                E = getattr(r, en)
                for sm in sems:
                    E.wait(sm)

        def pmax_to_scalar(src_col_ap, dst_ap):
            v.memset(FSC[:, 1024:1152], -3.0e38)
            v.tensor_copy(FSC[:, 1024:1025], src_col_ap).inc("v", 1)
            t.wait("v", r.snap("v"))
            t.waitev(r.getfree("psD"))
            t.transpose(psD[:, 0:128], FSC[:, 1024:1152], IDE).inc("t", 1)
            v.wait("t", r.snap("t"))
            v.tensor_reduce(dst_ap, psD[0:1, 0:128], AX.X, ALU.max)
            r.setfree("psD", "v")

        y.dma_start(out=CONS[:, :], in_=consts_in[:, :]).inc("dx", 16)
        y.dma_start(out=CONSB[:, :], in_=constsb_in[:, :]).inc("dx", 16)
        y.dma_start(out=COS[:, :], in_=cos_in[:, :]).inc("dx", 16)
        y.dma_start(out=SIN[:, :], in_=sin_in[:, :]).inc("dx", 16)
        y.dma_start(out=MASKN[:, :], in_=maskn_in[:, :]).inc("dx", 16)
        y.dma_start(out=MASKF[:, :], in_=maskf_in[:, :]).inc("dx", 16)
        y.dma_start(out=IDE[:, :], in_=ident_in[:, :]).inc("dx", 16)
        y.dma_start(out=IDEB[:, :], in_=identb_in[:, :]).inc("dx", 16)
        for (src, col) in ((bqa, cBQA), (bka, cBKA), (boa, cBOA), (bob, cBOB)):
            y.dma_start(out=SMALL[:, col:col + 8],
                        in_=src.rearrange("(c p) -> p c", p=128),
                        allow_slow_non_contiguous=True).inc("dx", 16)
        y.dma_start(out=SMALL[:, cB1R:cB1R + 32],
                    in_=b1.rearrange("(c p) -> p c", p=128),
                    allow_slow_non_contiguous=True).inc("dx", 16)
        y.dma_start(out=SMALL[:, cB2R:cB2R + 8],
                    in_=b2.rearrange("(c p) -> p c", p=128),
                    allow_slow_non_contiguous=True).inc("dx", 16)
        g.dma_start(out=BVROW[0:1, :], in_=bva[None, :]).inc("gx", 16)
        consts_dx = r.snap("dx")
        consts_gx = r.snap("gx")

        v.wait("dx", consts_dx)
        v.wait("gx", consts_gx)
        v.memset(ONESB[:, :], 1.0)
        v.memset(ZEROB[:, :], 0.0)
        v.memset(ONE_RB[0:1, :], 1.0)
        v.memset(ONE_RF[0:1, :], 1.0).inc("v", 1)
        s.wait("dx", consts_dx)
        s.wait("gx", consts_gx)

        t.wait("gx", consts_gx)
        wa_ready = []
        for widx, wsrc in enumerate((wqa, wka, wva, woa)):
            if widx > 0:
                g.waitev(r.getfree("wstage"))
            g.dma_start(out=wstage[:, :], in_=wsrc[:, :]).inc("gw", 16)
            cast_dw = r.snap("gw")
            for c in range(DC):
                slot = "W4"
                off = 4096
                y.wait("gw", cast_dw)
                y.waitev(r.getfree(slot))
                stg = WT[:, off:off + 1024]
                y.dma_start(out=stg, in_=wstage[:, c * 128:(c + 1) * 128],
                            transpose=True).inc("dwz", 16)
                y.wait("dwz", r.snap("dwz"))
                y.dma_start(out=waT[widx][c * 128:(c + 1) * 128, :], in_=stg).inc("dwz", 16)
                r.setfree(slot, "dwz")
            r.setfree("wstage", "dwz")
            wa_ready.append(r.snap("dwz"))

        for tt in range(17):
            slot = f"xstg{tt % 2}"
            y.waitev(r.getfree(slot))
            stg = FSC[:, (tt % 2) * 1024:(tt % 2) * 1024 + 1024]
            y.dma_start(out=stg, in_=x_in[tt * 128:(tt + 1) * 128, :]).inc("dx", 16)
            t.wait("dx", r.snap("dx"))
            for dc in range(DC):
                ps, pn = (psA, "psA") if dc % 2 == 0 else (psB, "psB")
                t.waitev(r.getfree(pn))
                t.transpose(ps[:, 0:128], stg[:, dc * 128:(dc + 1) * 128], IDE).inc("t", 1)
                v.wait("t", r.snap("t"))
                v.tensor_copy(xt(dc)[:, tt * 128:(tt + 1) * 128], ps[:, 0:128]).inc("v", 1)
                r.setfree(pn, "v")
            r.setfree(slot, "v")
        x_loaded_v = r.snap("v")

        def emit_ln(chunks, zb_out, zmax):
            def lnxb(dc, W):
                return HPB[:, dc * 512:dc * 512 + W]
            def lnxq(dc, W):
                return (SHUF if dc < 4 else RTMP)[:, (dc % 4) * 512:(dc % 4) * 512 + W]
            if zmax:
                v.memset(SMALL[:, cU:cU + 8], -3.0e38).inc("v", 1)
            s.wait("v", r.snap("v"))
            for (c0, W) in chunks:
                s.waitev(r.getfree("SHUF"))
                s.waitev(r.getfree("RTMP"))
                for dc in range(DC):
                    s.copy(lnxb(dc, W), xt(dc)[:, c0:c0 + W])
                last = None
                for dc in range(DC):
                    last = s.activation(lnxq(dc, W), xt(dc)[:, c0:c0 + W], AF.Square)
                last.inc("s", 1)
                t.wait("s", r.snap("s"))
                t.waitev(r.getfree("psA"))
                t.waitev(r.getfree("psB"))
                for dc in range(DC):
                    t.matmul(psA[0:1, 0:W], ONESB[:, 0:1], lnxb(dc, W),
                             start=(dc == 0), stop=(dc == DC - 1))
                last = None
                for dc in range(DC):
                    last = t.matmul(psB[0:1, 0:W], ONESB[:, 0:1], lnxq(dc, W),
                                    start=(dc == 0), stop=(dc == DC - 1))
                last.inc("t", 1)
                r.setfree("SHUF", "t")
                r.setfree("RTMP", "t")
                v.wait("t", r.snap("t"))
                v.tensor_scalar(P0S[0:1, 0:W], psA[0:1, 0:W], INV_D, None, ALU.mult)
                v.tensor_scalar(P0S[0:1, 512:512 + W], psB[0:1, 0:W], INV_D, None, ALU.mult)
                v.tensor_tensor(P0S[0:1, 1024:1024 + W], P0S[0:1, 0:W], P0S[0:1, 0:W],
                                ALU.mult)
                v.tensor_tensor(P0S[0:1, 512:512 + W], P0S[0:1, 512:512 + W],
                                P0S[0:1, 1024:1024 + W], ALU.subtract).inc("v", 1)
                r.setfree("psA", "v")
                r.setfree("psB", "v")
                s.wait("v", r.snap("v"))
                s.activation(P0S[0:1, 1024:1024 + W], P0S[0:1, 512:512 + W], AF.Sqrt,
                             bias=1e-5, scale=1.0).inc("s", 1)
                v.wait("s", r.snap("s"))
                v.reciprocal(ROWQ[0:1, c0:c0 + W], P0S[0:1, 1024:1024 + W])
                v.tensor_tensor(ROWQ[32:33, c0:c0 + W], P0S[0:1, 0:W],
                                ROWQ[0:1, c0:c0 + W], ALU.mult)
                v.tensor_copy(P0S[0:1, 1024:1024 + W], ROWQ[32:33, c0:c0 + W]).inc("v", 1)
                t.wait("v", r.snap("v"))
                t.waitev(r.getfree("psC"))
                t.waitev(r.getfree("psD"))
                t.matmul(psC[:, 0:W], ONE_RF[0:1, :], ROWQ[0:1, c0:c0 + W],
                         start=True, stop=True)
                t.matmul(psD[:, 0:W], ONE_RF[0:1, :], P0S[0:1, 1024:1024 + W],
                         start=True, stop=True).inc("t", 1)
                v.wait("t", r.snap("t"))
                if zb_out:
                    last = None
                    for dc in range(DC):
                        v.tensor_tensor(FSC[:, 0:W], xt(dc)[:, c0:c0 + W], psC[:, 0:W],
                                        ALU.mult)
                        last = v.tensor_tensor(zbt(dc)[:, c0:c0 + W], FSC[:, 0:W],
                                               psD[:, 0:W], ALU.subtract)
                    last.inc("v", 1)
                if zmax:
                    last = None
                    for dc in range(DC):
                        v.tensor_tensor(FSC[:, 0:W], xt(dc)[:, c0:c0 + W], psC[:, 0:W],
                                        ALU.mult)
                        v.tensor_tensor(FSC[:, 512:512 + W], FSC[:, 0:W], psD[:, 0:W],
                                        ALU.subtract)
                        v.tensor_reduce(SMALL[:, cTMP:cTMP + 1], FSC[:, 512:512 + W],
                                        AX.X, ALU.max)
                        last = v.tensor_tensor(SMALL[:, cU + dc:cU + dc + 1],
                                               SMALL[:, cU + dc:cU + dc + 1],
                                               SMALL[:, cTMP:cTMP + 1], ALU.max)
                    last.inc("v", 1)
                r.setfree("psC", "v")
                r.setfree("psD", "v")

        def emit_store():
            phase_sync("tg")
            for tt in range(16):
                slot = f"ostg{tt % 2}"
                stg = FSC[:, (tt % 2) * 1024:(tt % 2) * 1024 + 1024]
                for dc in range(DC):
                    ps, pn = (psA, "psA") if dc % 2 == 0 else (psB, "psB")
                    t.waitev(r.getfree(pn))
                    if dc == 0:
                        t.waitev(r.getfree(slot))
                    t.transpose(ps[:, 0:128],
                                xt(dc)[:, 128 + tt * 128:128 + (tt + 1) * 128],
                                IDE).inc("t", 1)
                    v.wait("t", r.snap("t"))
                    if dc == 0:
                        v.waitev(r.getfree(slot))
                    v.tensor_copy(stg[:, dc * 128:(dc + 1) * 128], ps[:, 0:128]).inc("v", 1)
                    r.setfree(pn, "v")
                g.wait("v", r.snap("v"))
                g.dma_start(out=out[tt * 128:(tt + 1) * 128, :], in_=stg).inc("dout", 16)
                r.setfree(slot, "dout")
            g.wait("dout", r.snap("dout"))

        if stop_after == "p0":
            emit_store()
            return

        v.wait("v", x_loaded_v)
        emit_ln(LN_TH, zb_out=True, zmax=False)
        ln1_v = r.snap("v")
        if stop_after == "p0ln":
            emit_store()
            return
        p1lvl = {"p1q": 1, "p1k": 2, "p1r": 3, "p1v": 4, "p1att0a": 4.95, "p1att0b": 4.97, "p1att0": 5, "p1att1": 5.1, "p1att2": 5.2, "p1att3": 5.4, "p1att": 5.6}.get(stop_after, 6)

        HQ0, HK0, HV0 = 0, 2048, 2048 + TH

        def wtiles_dma_y(key, dst_off, src_hbm, fcol, ready_dwz):
            y.wait("dwz", ready_dwz)
            y.waitev(r.getfree(key))
            src = src_hbm.rearrange("(a p) q -> p a q", p=128)
            y.dma_start(out=WT[:, dst_off:dst_off + 1024].rearrange("p (a q) -> p a q", q=128),
                        in_=src[:, :, fcol:fcol + 128]).inc("dw", 16)

        for hp in range(NHP):
            wtiles_dma_y("W0", 0, waT[0], hp * 128, wa_ready[0])
            if p1lvl >= 2:
                wtiles_dma_y("W1", 1024, waT[1], hp * 128, wa_ready[1])
            if p1lvl >= 4:
                wtiles_dma_y("W2", 2048, waT[2], hp * 128, wa_ready[2])
            wdma = r.snap("dw")
            t.wait("dw", wdma)
            if hp == 0:
                t.wait("v", ln1_v)
            t.waitev(r.getfree("hq"))
            for tc in range(NTC):
                ps, pn = (psA, "psA") if tc % 2 == 0 else (psB, "psB")
                t.waitev(r.getfree(pn))
                last = None
                for dc in range(DC):
                    last = t.matmul(ps[:, :], WT[:, dc * 128:(dc + 1) * 128],
                                    zbt(dc)[:, 128 + tc * 512:128 + (tc + 1) * 512],
                                    start=(dc == 0), stop=(dc == DC - 1))
                last.inc("t", 1)
                s.wait("t", r.snap("t"))
                s.activation(HPB[:, HQ0 + tc * 512:HQ0 + (tc + 1) * 512], ps[:, :],
                             AF.Identity, bias=SMALL[:, cBQA + hp:cBQA + hp + 1],
                             scale=1.0).inc("s", 1)
                r.setfree(pn, "s")
            qproj_s = r.snap("s")
            if p1lvl < 2:
                continue
            t.waitev(r.getfree("hk"))
            for ci, (c0, W) in enumerate(LN_TH):
                ps, pn = (psA, "psA") if ci % 2 == 0 else (psB, "psB")
                t.waitev(r.getfree(pn))
                last = None
                for dc in range(DC):
                    last = t.matmul(ps[:, 0:W], WT[:, 1024 + dc * 128:1024 + (dc + 1) * 128],
                                    zbt(dc)[:, c0:c0 + W],
                                    start=(dc == 0), stop=(dc == DC - 1))
                last.inc("t", 1)
                s.wait("t", r.snap("t"))
                s.activation(HPB[:, HK0 + c0:HK0 + c0 + W], ps[:, 0:W],
                             AF.Identity, bias=SMALL[:, cBKA + hp:cBKA + hp + 1],
                             scale=1.0).inc("s", 1)
                r.setfree(pn, "s")
            kproj_s = r.snap("s")
            r.setfree("W0", "s")
            r.setfree("W1", "s")
            if p1lvl < 3:
                continue
            v.wait("s", qproj_s)
            v.wait("t", r.snap("t"))
            v.waitev(r.getfree("SHUF"))
            v.waitev(r.getfree("RTMP"))
            QS = HPB[:, HQ0:HQ0 + 2048]
            v.tensor_copy(SHUF[0:32, 0:2048], QS[32:64, :])
            v.tensor_copy(SHUF[32:64, 0:2048], QS[0:32, :])
            v.tensor_copy(SHUF[64:96, 0:2048], QS[96:128, :])
            v.tensor_copy(SHUF[96:128, 0:2048], QS[64:96, :])
            v.tensor_tensor(RTMP[:, 0:2048], QS[:, :], COS[:, 128:TH], ALU.mult)
            v.tensor_tensor(QS[:, :], SHUF[:, 0:2048], SIN[:, 128:TH], ALU.mult)
            v.tensor_tensor(QS[:, :], RTMP[:, 0:2048], QS[:, :], ALU.add).inc("v", 1)
            v.wait("s", kproj_s)
            KS_ = HPB[:, HK0:HK0 + TH]
            v.tensor_copy(SHUF[0:32, 0:TH], KS_[32:64, :])
            v.tensor_copy(SHUF[32:64, 0:TH], KS_[0:32, :])
            v.tensor_copy(SHUF[64:96, 0:TH], KS_[96:128, :])
            v.tensor_copy(SHUF[96:128, 0:TH], KS_[64:96, :])
            v.tensor_tensor(RTMP[:, 0:TH], KS_[:, :], COS[:, :], ALU.mult)
            v.tensor_tensor(KS_[:, :], SHUF[:, 0:TH], SIN[:, :], ALU.mult)
            v.tensor_tensor(KS_[:, :], RTMP[:, 0:TH], KS_[:, :], ALU.add).inc("v", 1)
            rope_v = r.snap("v")
            r.setfree("SHUF", "v")
            r.setfree("RTMP", "v")
            if p1lvl < 4:
                continue
            t.waitev(r.getfree("hv"))
            for tt in range(17):
                ps, pn = (psA, "psA") if tt % 2 == 0 else (psB, "psB")
                t.waitev(r.getfree(pn))
                for dc in range(DC):
                    t.matmul(ps[:, 0:128], zbt(dc)[:, tt * 128:(tt + 1) * 128],
                             WT[:, 2048 + dc * 128:2048 + (dc + 1) * 128],
                             start=(dc == 0), stop=False)
                t.matmul(ps[:, 0:128], ONE_RB[0:1, :],
                         BVROW[0:1, hp * 128:(hp + 1) * 128],
                         start=False, stop=True).inc("t", 1)
                v.wait("t", r.snap("t"))
                v.tensor_copy(HPB[:, HV0 + tt * 128:HV0 + (tt + 1) * 128],
                              ps[:, 0:128]).inc("v", 1)
                r.setfree(pn, "v")
            r.setfree("W2", "v")
            if p1lvl < 5:
                continue
            t.wait("v", r.snap("v"))
            for qt in range(16):
                k0 = qt * 128
                q0 = HQ0 + qt * 128
                if SERIAL_ATT:
                    t.wait("v", r.snap("v"))
                    t.wait("s", r.snap("s"))
                    v.wait("t", r.snap("t"))
                    s.wait("v", r.snap("v"))
                t.waitev(r.getfree("psS1"))
                t.waitev(r.getfree("psS2"))
                t.matmul(psS1[:, 0:128], HPB[0:64, HK0 + k0:HK0 + k0 + 128],
                         HPB[0:64, q0:q0 + 128], start=True, stop=True)
                t.matmul(psS1[:, 512:640], HPB[0:64, HK0 + k0 + 128:HK0 + k0 + 256],
                         HPB[0:64, q0:q0 + 128], start=True, stop=True)
                t.matmul(psS2[:, 0:128], HPB[64:128, HK0 + k0:HK0 + k0 + 128],
                         HPB[64:128, q0:q0 + 128], start=True, stop=True,
                         tile_position=(64, 0))
                t.matmul(psS2[:, 512:640], HPB[64:128, HK0 + k0 + 128:HK0 + k0 + 256],
                         HPB[64:128, q0:q0 + 128], start=True, stop=True,
                         tile_position=(64, 0)).inc("t", 1)
                pts = f"pta{qt % 2}"
                pt = SHUF[:, (qt % 2) * 512:(qt % 2) * 512 + 512]
                if p1lvl < 5.1:
                    continue
                s.wait("t", r.snap("t"))
                s.waitev(r.getfree(pts))
                sv1 = psS1.rearrange("p (a b) -> p a b", b=128)[:, 0:5:4, :]
                sv2 = psS2.rearrange("p (a b) -> p a b", b=128)[:, 0:5:4, :]
                s.activation(pt[:, 0:256].rearrange("p (a b) -> p a b", b=128),
                             sv1, AF.Exp, bias=-EXPB_A, scale=SCALE).inc("s", 1)
                r.setfree("psS1", "s")
                s.activation(pt[:, 256:512].rearrange("p (a b) -> p a b", b=128),
                             sv2, AF.Exp, bias=-EXPB_A, scale=SCALE).inc("s", 1)
                r.setfree("psS2", "s")
                v.wait("s", r.snap("s"))
                v.tensor_tensor(pt, pt, MASKF if qt == 0 else MASKN, ALU.mult).inc("v", 1)
                if p1lvl < 5.2:
                    r.setfree(pts, "v")
                    continue
                t.wait("v", r.snap("v"))
                t.waitev(r.getfree("psC"))
                first = True
                last = None
                for h in range(2):
                    for blk in range(2):
                        off = (2 * h + blk) * 128
                        vt = HPB[:, HV0 + (qt + blk) * 128 + h * 64:
                                 HV0 + (qt + blk) * 128 + h * 64 + 64]
                        t.matmul(psC[:, h * 64:h * 64 + 64], pt[:, off:off + 128], vt,
                                 start=first, stop=False, skip_group_check=True)
                        first = False
                        last = t.matmul(psC[:, 128 + h:129 + h], pt[:, off:off + 128],
                                        ONESB[:, 0:1], start=False,
                                        stop=(h == 1 and blk == 1), skip_group_check=True)
                last.inc("t", 1)
                r.setfree(pts, "t")
                if p1lvl < 5.4:
                    r.setfree("psC", "t")
                    continue
                ctn = CTN[:, (qt % 2) * 128:(qt % 2) * 128 + 128]
                v.wait("t", r.snap("t"))
                v.waitev(r.getfree(f"ctn{qt % 2}"))
                v.reciprocal(FSC[:, 0:2], psC[:, 128:130])
                v.tensor_scalar(ctn[:, 0:64], psC[:, 0:64], FSC[:, 0:1], None, ALU.mult)
                v.tensor_scalar(ctn[:, 64:128], psC[:, 64:128], FSC[:, 1:2], None,
                                ALU.mult).inc("v", 1)
                r.setfree("psC", "v")
                if p1lvl < 5.6:
                    r.setfree(f"ctn{qt % 2}", "v")
                    continue
                t.wait("v", r.snap("v"))
                t.waitev(r.getfree("psD"))
                t.transpose(psD.bitcast(BF16)[:, 0:128], ctn, IDEB).inc("t", 1)
                r.setfree(f"ctn{qt % 2}", "t")
                v.wait("t", r.snap("t"))
                v.tensor_copy(ctxt(hp)[:, qt * 128:(qt + 1) * 128],
                              psD.bitcast(BF16)[:, 0:128]).inc("v", 1)
                r.setfree(f"hpb{hp % 1}", "v")
                r.setfree("psD", "v")
        attn_v = r.snap("v")
        if stop_after == "p1hpb":
            g.wait("v", attn_v)
            g.wait("s", r.snap("s"))
            ob = out.bitcast(BF16)
            for cb in range(4):
                g.dma_start(out=ob[cb * 128:(cb + 1) * 128, 0:1600],
                            in_=HPB[:, cb * 1600:(cb + 1) * 1600]).inc("dout", 16)
            g.wait("dout", r.snap("dout"))
            return
        if stop_after == "p1ctx":
            g.wait("v", attn_v)
            ob = out.bitcast(BF16)
            for cb in range(8):
                g.dma_start(out=ob[cb * 128:(cb + 1) * 128, 0:2048],
                            in_=CTXT[:, cb * 2048:(cb + 1) * 2048]).inc("dout", 16)
            g.wait("dout", r.snap("dout"))
            return
        if p1lvl < 6:
            emit_store()
            return

        for dc in range(DC):
            key = "W0" if dc % 2 == 0 else "W1"
            woff = 0 if dc % 2 == 0 else 1024
            wtiles_dma_y(key, woff, waT[3], dc * 128, wa_ready[3])
            wdma = r.snap("dw")
            t.wait("dw", wdma)
            if dc == 0:
                t.wait("v", attn_v)
            for tc in range(NTC):
                ps, pn = (psA, "psA") if tc % 2 == 0 else (psB, "psB")
                t.waitev(r.getfree(pn))
                last = None
                for hp in range(NHP):
                    last = t.matmul(ps[:, :], WT[:, woff + hp * 128:woff + (hp + 1) * 128],
                                    ctxt(hp)[:, tc * 512:(tc + 1) * 512],
                                    start=(hp == 0), stop=(hp == NHP - 1))
                last.inc("t", 1)
                v.wait("t", r.snap("t"))
                dst = xt(dc)[:, 128 + tc * 512:128 + (tc + 1) * 512]
                v.scalar_tensor_tensor(dst, ps[:, :], SMALL[:, cBOA + dc:cBOA + dc + 1],
                                       dst, ALU.add, ALU.add).inc("v", 1)
                r.setfree(pn, "v")
            r.setfree(key, "v")

        if stop_after == "p1":
            emit_store()
            return

        phase_sync("vyg")

        def quant_chunk(stg, dst_bf, rsw_col):
            v.tensor_scalar(CTMP[:, :], stg, SMALL[:, rsw_col:rsw_col + 1], 0.5,
                            ALU.mult, ALU.add)
            v.tensor_copy(ISC[:, :], CTMP[:, :])
            v.tensor_copy(stg, ISC[:, :])
            v.tensor_tensor(CTMP[:, :], stg, CTMP[:, :], ALU.is_gt)
            return v.scalar_tensor_tensor(dst_bf, CTMP[:, :], -1.0, stg,
                                          ALU.mult, ALU.add)

        for c in range(NFC):
            slot = f"wqs{c % 2}"
            y.waitev(r.getfree(slot))
            stg = FSC[:, (c % 2) * 1024:(c % 2) * 1024 + 1024]
            y.dma_start(out=stg, in_=w1[c * 128:(c + 1) * 128, :]).inc("dq", 16)
            v.wait("dq", r.snap("dq"))
            v.waitev(r.getfree(f"whti{c % 2}"))
            v.tensor_reduce(SMALL[:, cSW1 + c:cSW1 + c + 1], stg, AX.X, ALU.max,
                            apply_absolute_value=True)
            v.reciprocal(SMALL[:, cTMP:cTMP + 1], SMALL[:, cSW1 + c:cSW1 + c + 1])
            v.tensor_scalar(SMALL[:, cTMP:cTMP + 1], SMALL[:, cTMP:cTMP + 1], 127.0,
                            None, ALU.mult).inc("v", 1)
            v.wait("v", r.snap("v"))
            hti = HTI[:, (c % 2) * 1024:(c % 2) * 1024 + 1024]
            quant_chunk(stg, hti, cTMP).inc("v", 1)
            r.setfree(slot, "v")
            g.wait("v", r.snap("v"))
            g.dma_start(out=w1q_h[c * 128:(c + 1) * 128, :], in_=hti).inc("gq", 16)
            r.setfree(f"whti{c % 2}", "gq")
        for dc in range(DC):
            for cc in range(4):
                slot = f"wqs{cc % 2}"
                y.waitev(r.getfree(slot))
                stg = FSC[:, (cc % 2) * 1024:(cc % 2) * 1024 + 1024]
                y.dma_start(out=stg, in_=w2[dc * 128:(dc + 1) * 128,
                                            cc * 1024:(cc + 1) * 1024]).inc("dq", 16)
                v.wait("dq", r.snap("dq"))
                if cc == 0:
                    v.tensor_reduce(SMALL[:, cSW2 + dc:cSW2 + dc + 1], stg, AX.X,
                                    ALU.max, apply_absolute_value=True)
                else:
                    v.tensor_reduce(SMALL[:, cTMP:cTMP + 1], stg, AX.X, ALU.max,
                                    apply_absolute_value=True)
                    v.tensor_tensor(SMALL[:, cSW2 + dc:cSW2 + dc + 1],
                                    SMALL[:, cSW2 + dc:cSW2 + dc + 1],
                                    SMALL[:, cTMP:cTMP + 1], ALU.max).inc("v", 1)
                r.setfree(slot, "v")
        for dc in range(DC):
            v.reciprocal(SMALL[:, cTMP + 1:cTMP + 2], SMALL[:, cSW2 + dc:cSW2 + dc + 1])
            v.tensor_scalar(SMALL[:, cTMP + 1:cTMP + 2], SMALL[:, cTMP + 1:cTMP + 2],
                            127.0, None, ALU.mult).inc("v", 1)
            v.wait("v", r.snap("v"))
            for cc in range(4):
                slot = f"wqs{cc % 2}"
                y.waitev(r.getfree(slot))
                stg = FSC[:, (cc % 2) * 1024:(cc % 2) * 1024 + 1024]
                y.dma_start(out=stg, in_=w2[dc * 128:(dc + 1) * 128,
                                            cc * 1024:(cc + 1) * 1024]).inc("dq", 16)
                v.wait("dq", r.snap("dq"))
                v.waitev(r.getfree(f"whti{cc % 2}"))
                hti = HTI[:, (cc % 2) * 1024:(cc % 2) * 1024 + 1024]
                quant_chunk(stg, hti, cTMP + 1).inc("v", 1)
                r.setfree(slot, "v")
                g.wait("v", r.snap("v"))
                g.dma_start(out=w2q_h[dc * 128:(dc + 1) * 128,
                                      cc * 1024:(cc + 1) * 1024], in_=hti).inc("gq", 16)
                r.setfree(f"whti{cc % 2}", "gq")
        wq_nat_dq = r.snap("gq")
        for c in range(DC):
            y.wait("gq", wq_nat_dq)
            y.wait("dq", r.snap("dq"))
            y.dma_start(out=CTXT[:, 0:4096], in_=w1q_h[:, c * 128:(c + 1) * 128],
                        transpose=True).inc("dq", 16)
            y.wait("dq", r.snap("dq"))
            for fc in range(NFC):
                y.dma_start(out=w1qT_h[fc, c],
                            in_=CTXT[:, fc * 128:(fc + 1) * 128]).inc("dq", 16)
        for fc in range(NFC):
            y.wait("dq", r.snap("dq"))
            y.dma_start(out=CTXT[:, 0:1024], in_=w2q_h[:, fc * 128:(fc + 1) * 128],
                        transpose=True).inc("dq", 16)
            y.wait("dq", r.snap("dq"))
            for dc in range(DC):
                y.dma_start(out=w2qT_h[dc, fc],
                            in_=CTXT[:, dc * 128:(dc + 1) * 128]).inc("dq", 16)
        wq_ready = r.snap("dq")
        if stop_after == "p15":
            emit_store()
            return

        def emit_ffn(idx):
            flvl = {"p2sind": 1, "p2ln": 1, "p2zq": 2, "p2m1": 3, "p2s2": 4}.get(stop_after, 9)
            phase_sync("tvsg")
            emit_ln(LN_T, zb_out=False, zmax=True)
            v.tensor_reduce(SMALL[:, cTMP + 2:cTMP + 3], SMALL[:, cU:cU + 8], AX.X,
                            ALU.max)
            pmax_to_scalar(SMALL[:, cTMP + 2:cTMP + 3], P0S[0:1, 1536:1537])
            v.memset(FSC[0:1, 1152:1168], -3.0e38)
            v.tensor_scalar(FSC[0:1, 1152:1168], FSC[0:1, 1152:1168],
                            P0S[0:1, 1536:1537], None, ALU.max).inc("v", 1)
            g.wait("v", r.snap("v"))
            g.dma_start(out=mxi_h[0:1, :], in_=FSC[0:1, 1152:1168]).inc("dmx", 16)
            g.wait("dmx", r.snap("dmx"))
            g.collective_compute("AllReduce", ALU.max, replica_groups=[list(range(NCORE))],
                                 ins=[mxi_h], outs=[mxo_h]).inc("cc", 1)
            g.wait("cc", r.snap("cc"))
            g.dma_start(out=P0S[0:1, 1537:1538], in_=mxo_h[0:1, 0:1]).inc("dmx", 16)
            v.wait("dmx", r.snap("dmx"))
            v.tensor_scalar(P0S[0:1, 1538:1539], P0S[0:1, 1537:1538], 1e-8, 1.0 / 255.0,
                            ALU.max, ALU.mult)
            v.reciprocal(P0S[0:1, 1539:1540], P0S[0:1, 1538:1539])
            t.wait("v", r.snap("v"))
            t.waitev(r.getfree("psC"))
            t.matmul(psC[:, 0:2], ONE_RF[0:1, :], P0S[0:1, 1538:1540],
                     start=True, stop=True).inc("t", 1)
            v.wait("t", r.snap("t"))
            v.tensor_copy(SMALL[:, cSCB:cSCB + 2], psC[:, 0:2]).inc("v", 1)
            v.wait("v", r.snap("v"))
            r.setfree("psC", "v")
            v.tensor_scalar(SMALL[:, cSWS1:cSWS1 + 32], SMALL[:, cSW1:cSW1 + 32],
                            SMALL[:, cSCB:cSCB + 1], None, ALU.mult)
            v.tensor_scalar(SMALL[:, cSWS1:cSWS1 + 32], SMALL[:, cSWS1:cSWS1 + 32],
                            1.0 / 127.0, None, ALU.mult)
            v.reciprocal(FSC[:, 0:32], SMALL[:, cSWS1:cSWS1 + 32])
            v.tensor_tensor(FSC[:, 32:64], SMALL[:, cB1R:cB1R + 32], FSC[:, 0:32],
                            ALU.mult)
            v.tensor_scalar(FSC[:, 32:64], FSC[:, 32:64], 0.5, None, ALU.add)
            v.tensor_copy(ISC[:, 0:32], FSC[:, 32:64])
            v.tensor_copy(FSC[:, 64:96], ISC[:, 0:32])
            v.tensor_tensor(FSC[:, 96:128], FSC[:, 64:96], FSC[:, 32:64], ALU.is_gt)
            v.scalar_tensor_tensor(FSC[:, 64:96], FSC[:, 96:128], -1.0, FSC[:, 64:96],
                                   ALU.mult, ALU.add)
            v.tensor_tensor(SMALL[:, cB1QS:cB1QS + 32], FSC[:, 64:96],
                            SMALL[:, cSWS1:cSWS1 + 32], ALU.mult).inc("v", 1)
            if flvl < 2 and stop_after != "p2sind":
                return
            if stop_after == "p2sind":
                g.wait("v", r.snap("v"))
                g.dma_start(out=out[0:1, 0:4], in_=P0S[0:1, 1536:1540]).inc("dout", 16)
                g.dma_start(out=out[1:2, 0:8], in_=SMALL[0:1, cU:cU + 8]).inc("dout", 16)
                g.dma_start(out=out[2:3, 0:3], in_=SMALL[0:1, cTMP:cTMP + 3]).inc("dout", 16)
                g.wait("dout", r.snap("dout"))
                return
            v.tensor_scalar(ROWQ[0:1, 128:TH], ROWQ[0:1, 128:TH],
                            P0S[0:1, 1539:1540], None, ALU.mult)
            v.tensor_scalar(ROWQ[32:33, 128:TH], ROWQ[32:33, 128:TH],
                            SMALL[32:33, cSCB + 1:cSCB + 2], None, ALU.mult)
            v.tensor_scalar(ROWQ[32:33, 128:TH], ROWQ[32:33, 128:TH], 0.5, None,
                            ALU.subtract).inc("v", 1)
            for (c0, W) in LN_T:
                v.tensor_copy(P0S[0:1, 1024:1024 + W], ROWQ[32:33, c0:c0 + W]).inc("v", 1)
                t.wait("v", r.snap("v"))
                t.waitev(r.getfree("psC"))
                t.waitev(r.getfree("psD"))
                t.matmul(psC[:, 0:W], ONE_RF[0:1, :], ROWQ[0:1, c0:c0 + W],
                         start=True, stop=True)
                t.matmul(psD[:, 0:W], ONE_RF[0:1, :], P0S[0:1, 1024:1024 + W],
                         start=True, stop=True).inc("t", 1)
                v.wait("t", r.snap("t"))
                last = None
                for dc in range(DC):
                    v.tensor_tensor(FSC[:, 0:W], xt(dc)[:, c0:c0 + W], psC[:, 0:W],
                                    ALU.mult)
                    v.tensor_tensor(FSC[:, 0:W], FSC[:, 0:W], psD[:, 0:W], ALU.subtract)
                    v.tensor_copy(ISC[:, 0:W], FSC[:, 0:W])
                    v.tensor_copy(FSC[:, 512:512 + W], ISC[:, 0:W])
                    v.tensor_tensor(FSC[:, 1024:1024 + W], FSC[:, 512:512 + W],
                                    FSC[:, 0:W], ALU.is_gt)
                    v.scalar_tensor_tensor(FSC[:, 512:512 + W], FSC[:, 1024:1024 + W],
                                           -1.0, FSC[:, 512:512 + W], ALU.mult, ALU.add)
                    last = v.tensor_scalar(zbt(dc)[:, c0:c0 + W], FSC[:, 512:512 + W],
                                           0.0, None, ALU.max)
                last.inc("v", 1)
                r.setfree("psC", "v")
                r.setfree("psD", "v")
            zq_v = r.snap("v")
            if stop_after == "p2zqd":
                g.wait("v", zq_v)
                ob = out.bitcast(BF16)
                for cb in range(8):
                    g.dma_start(out=ob[cb * 128:(cb + 1) * 128, 0:2048],
                                in_=ZBT[:, cb * TH + 128:cb * TH + 2176]).inc("dout", 16)
                g.wait("dout", r.snap("dout"))
                return
            if flvl < 3 or stop_after == "p2zqd":
                return
            v.memset(SMALL[:, cHU:cHU + 32], -3.0e38).inc("v", 1)
            for fc in range(NFC):
                key = "W0" if fc % 2 == 0 else "W1"
                woff = 0 if fc % 2 == 0 else 1024
                y.wait("dq", wq_ready)
                y.waitev(r.getfree(key))
                y.dma_start(out=WT[:, woff:woff + 1024].rearrange("p (a q) -> p a q", q=128),
                            in_=w1qT_h[fc, :, :, :].rearrange("a p q -> p a q")).inc("dw", 16)
                wdma = r.snap("dw")
                t.wait("dw", wdma)
                if fc == 0:
                    t.wait("v", zq_v)
                for tc in range(NTC):
                    ps, pn = (psA, "psA") if tc % 2 == 0 else (psB, "psB")
                    t.waitev(r.getfree(pn))
                    last = None
                    for dc in range(DC):
                        last = t.matmul(ps[:, :],
                                        WT[:, woff + dc * 128:woff + (dc + 1) * 128],
                                        zbt(dc)[:, 128 + tc * 512:128 + (tc + 1) * 512],
                                        start=(dc == 0), stop=(dc == DC - 1))
                    last.inc("t", 1)
                    hs = f"hti{(fc * NTC + tc) % 2}"
                    hoff = ((fc * NTC + tc) % 2) * 1024
                    hti = HTI[:, hoff:hoff + 512]
                    s.wait("t", r.snap("t"))
                    s.wait("v", zq_v)
                    s.waitev(r.getfree(hs))
                    s.activation(hti, ps[:, :], AF.Relu,
                                 bias=SMALL[:, cB1QS + fc:cB1QS + fc + 1],
                                 scale=SMALL[:, cSWS1 + fc:cSWS1 + fc + 1]).inc("s", 1)
                    v.wait("t", r.snap("t"))
                    v.wait("s", r.snap("s"))
                    v.tensor_reduce(SMALL[:, cTMP:cTMP + 1], ps[:, :], AX.X, ALU.max)
                    v.tensor_tensor(SMALL[:, cHU + fc:cHU + fc + 1],
                                    SMALL[:, cHU + fc:cHU + fc + 1],
                                    SMALL[:, cTMP:cTMP + 1], ALU.max).inc("v", 1)
                    r.setfree(pn, "v", "s")
                    g.wait("s", r.snap("s"))
                    g.dma_start(out=h_h[fc, tc], in_=hti).inc("gh", 16)
                    r.setfree(hs, "gh")
                r.setfree(key, "t")
            if flvl < 4:
                return
            v.tensor_tensor(FSC[:, 0:32], SMALL[:, cHU:cHU + 32],
                            SMALL[:, cSWS1:cSWS1 + 32], ALU.mult)
            v.tensor_tensor(FSC[:, 0:32], FSC[:, 0:32], SMALL[:, cB1QS:cB1QS + 32],
                            ALU.add)
            v.tensor_scalar(FSC[:, 0:32], FSC[:, 0:32], 0.0, None, ALU.max)
            v.tensor_reduce(SMALL[:, cTMP + 2:cTMP + 3], FSC[:, 0:32], AX.X,
                            ALU.max)
            pmax_to_scalar(SMALL[:, cTMP + 2:cTMP + 3], P0S[0:1, 1536:1537])
            v.memset(FSC[0:1, 1152:1168], -3.0e38)
            v.tensor_scalar(FSC[0:1, 1152:1168], FSC[0:1, 1152:1168],
                            P0S[0:1, 1536:1537], None, ALU.max).inc("v", 1)
            g.wait("v", r.snap("v"))
            g.dma_start(out=mxi_h[0:1, :], in_=FSC[0:1, 1152:1168]).inc("dmx", 16)
            g.wait("dmx", r.snap("dmx"))
            g.collective_compute("AllReduce", ALU.max, replica_groups=[list(range(NCORE))],
                                 ins=[mxi_h], outs=[mxo_h]).inc("cc", 1)
            g.wait("cc", r.snap("cc"))
            g.dma_start(out=P0S[0:1, 1537:1538], in_=mxo_h[0:1, 0:1]).inc("dmx", 16)
            v.wait("dmx", r.snap("dmx"))
            v.tensor_scalar(P0S[0:1, 1538:1539], P0S[0:1, 1537:1538], 1e-8, 1.0 / 255.0,
                            ALU.max, ALU.mult)
            v.reciprocal(P0S[0:1, 1539:1540], P0S[0:1, 1538:1539])
            t.wait("v", r.snap("v"))
            t.waitev(r.getfree("psC"))
            t.matmul(psC[:, 0:2], ONE_RF[0:1, :], P0S[0:1, 1538:1540],
                     start=True, stop=True).inc("t", 1)
            v.wait("t", r.snap("t"))
            v.tensor_copy(SMALL[:, cSCB + 2:cSCB + 4], psC[:, 0:2]).inc("v", 1)
            v.wait("v", r.snap("v"))
            r.setfree("psC", "v")
            v.tensor_scalar(SMALL[:, cSWS2:cSWS2 + 8], SMALL[:, cSW2:cSW2 + 8],
                            SMALL[:, cSCB + 2:cSCB + 3], None, ALU.mult)
            v.tensor_scalar(SMALL[:, cSWS2:cSWS2 + 8], SMALL[:, cSWS2:cSWS2 + 8],
                            1.0 / 127.0, None, ALU.mult)
            v.reciprocal(FSC[:, 0:8], SMALL[:, cSWS2:cSWS2 + 8])
            v.tensor_tensor(FSC[:, 8:16], SMALL[:, cB2R:cB2R + 8], FSC[:, 0:8], ALU.mult)
            v.tensor_scalar(FSC[:, 8:16], FSC[:, 8:16], 0.5, None, ALU.add)
            v.tensor_copy(ISC[:, 0:8], FSC[:, 8:16])
            v.tensor_copy(FSC[:, 16:24], ISC[:, 0:8])
            v.tensor_tensor(FSC[:, 24:32], FSC[:, 16:24], FSC[:, 8:16], ALU.is_gt)
            v.scalar_tensor_tensor(FSC[:, 16:24], FSC[:, 24:32], -1.0, FSC[:, 16:24],
                                   ALU.mult, ALU.add)
            v.tensor_tensor(SMALL[:, cB2QS:cB2QS + 8], FSC[:, 16:24],
                            SMALL[:, cSWS2:cSWS2 + 8], ALU.mult).inc("v", 1)
            sc2_v = r.snap("v")
            if flvl < 9:
                return
            mm1_dh = r.snap("gh")
            mm1_t = r.snap("t")
            y.wait("gh", mm1_dh)
            y.wait("t", mm1_t)
            for tc in range(NTC):
                v.waitev(r.getfree("HQ"))
                for fc in range(NFC):
                    if SERIAL_ATT:
                        v.wait("t", r.snap("t"))
                        y.wait("v", r.snap("v"))
                    hs = f"hin{fc % 2}"
                    y.waitev(r.getfree(hs))
                    hin = HTI[:, (fc % 2) * 1024:(fc % 2) * 1024 + 512]
                    y.dma_start(out=hin, in_=h_h[fc, tc]).inc("dh", 16)
                    v.wait("dh", r.snap("dh"))
                    if fc == 0 and tc == 0:
                        v.wait("v", sc2_v)
                    v.tensor_scalar(FSC[:, 0:512], hin, SMALL[:, cSCB + 3:cSCB + 4],
                                    0.5, ALU.mult, ALU.add)
                    r.setfree(hs, "v")
                    v.tensor_copy(ISC[:, 0:512], FSC[:, 0:512])
                    v.tensor_copy(FSC[:, 512:1024], ISC[:, 0:512])
                    v.tensor_tensor(FSC[:, 1024:1536], FSC[:, 512:1024], FSC[:, 0:512],
                                    ALU.is_gt)
                    v.scalar_tensor_tensor(CTXT[:, fc * 512:(fc + 1) * 512],
                                           FSC[:, 1024:1536], -1.0, FSC[:, 512:1024],
                                           ALU.mult, ALU.add).inc("v", 1)
                hq_v = r.snap("v")
                for dc in range(DC):
                    if SERIAL_ATT:
                        t.wait("v", r.snap("v"))
                        t.wait("s", r.snap("s"))
                        v.wait("t", r.snap("t"))
                        y.wait("t", r.snap("t"))
                        y.wait("v", r.snap("v"))
                    for half in range(2):
                        key = "W0W1" if half == 0 else "W2W3"
                        off = half * 2048
                        y.waitev(r.getfree(key))
                        y.dma_start(out=WT[:, off:off + 2048]
                                    .rearrange("p (a q) -> p a q", q=128),
                                    in_=w2qT_h[dc, half * 16:(half + 1) * 16, :, :]
                                    .rearrange("a p q -> p a q")).inc("dw", 16)
                    wdma = r.snap("dw")
                    t.wait("dw", wdma)
                    t.wait("v", hq_v)
                    ps, pn = (psC, "psC") if dc % 2 == 0 else (psD, "psD")
                    t.waitev(r.getfree(pn))
                    last = None
                    for fc in range(NFC):
                        last = t.matmul(ps[:, :], WT[:, fc * 128:(fc + 1) * 128],
                                        CTXT[:, fc * 512:(fc + 1) * 512],
                                        start=(fc == 0), stop=(fc == NFC - 1))
                    last.inc("t", 1)
                    r.setfree("W0W1", "t")
                    r.setfree("W2W3", "t")
                    v.wait("t", r.snap("t"))
                    dst = xt(dc)[:, 128 + tc * 512:128 + (tc + 1) * 512]
                    v.tensor_scalar(FSC[:, 0:512], ps[:, :],
                                    SMALL[:, cSWS2 + dc:cSWS2 + dc + 1], None, ALU.mult)
                    v.tensor_scalar(FSC[:, 0:512], FSC[:, 0:512],
                                    SMALL[:, cB2QS + dc:cB2QS + dc + 1], None, ALU.add)
                    v.tensor_tensor(dst, dst, FSC[:, 0:512], ALU.add).inc("v", 1)
                    r.setfree(pn, "v")
                r.setfree("HQ", "t")

        emit_ffn(0)
        if stop_after in ("p2zqd", "p2sind"):
            return
        if stop_after in ("p2", "p2ln", "p2zq", "p2m1", "p2s2"):
            emit_store()
            return

        phase_sync("tvsgy")
        emit_ln(LN_T, zb_out=True, zmax=False)
        ln3_v = r.snap("v")
        for hp in range(NHP):
            key = "W0" if hp % 2 == 0 else "W1"
            woff = 0 if hp % 2 == 0 else 1024
            g.waitev(r.getfree(key))
            g.dma_start(out=WT[:, woff:woff + 1024].rearrange("p (a q) -> p a q", q=128),
                        in_=wkvb.rearrange("(a p) q -> p a q", p=128)
                        [:, :, hp * 128:(hp + 1) * 128]).inc("gw", 16)
            wdma = r.snap(            "gw")
            t.wait("dw", wdma)
            if hp == 0:
                t.wait("v", ln3_v)
            for tc in range(NTC):
                ps, pn = (psA, "psA") if tc % 2 == 0 else (psB, "psB")
                t.waitev(r.getfree(pn))
                last = None
                for dc in range(DC):
                    last = t.matmul(ps[:, :], WT[:, woff + dc * 128:woff + (dc + 1) * 128],
                                    zbt(dc)[:, 128 + tc * 512:128 + (tc + 1) * 512],
                                    start=(dc == 0), stop=(dc == DC - 1))
                last.inc("t", 1)
                ks = f"kvs{tc % 2}"
                v.wait("t", r.snap("t"))
                v.waitev(r.getfree(ks))
                kstg = KVS[:, (tc % 2) * 512:(tc % 2) * 512 + 512]
                v.tensor_copy(kstg, ps[:, :]).inc("v", 1)
                r.setfree(pn, "v")
                g.wait("v", r.snap("v"))
                g.dma_start(out=kin_h[hp * 128:(hp + 1) * 128, tc * 512:(tc + 1) * 512],
                            in_=kstg).inc("gkv", 16)
                r.setfree(ks, "gkv")
            r.setfree(key, "v")
        for fh in range(2):
            g.waitev(r.getfree("VRHS"))
            g.dma_start(out=CTXT[:, 0:4096].rearrange("p (a q) -> p a q", q=512),
                        in_=wkvb.rearrange("(a p) q -> p a q", p=128)
                        [:, :, D + fh * 512:D + (fh + 1) * 512]).inc("gw", 16)
            wdma = r.snap(            "gw")
            t.wait("dw", wdma)
            for tt in range(16):
                ps, pn = (psA, "psA") if tt % 2 == 0 else (psB, "psB")
                t.waitev(r.getfree(pn))
                last = None
                for dc in range(DC):
                    last = t.matmul(ps[:, :], zbt(dc)[:, 128 + tt * 128:128 + (tt + 1) * 128],
                                    CTXT[:, dc * 512:(dc + 1) * 512],
                                    start=(dc == 0), stop=(dc == DC - 1))
                last.inc("t", 1)
                ks = f"kvs{tt % 2}"
                v.wait("t", r.snap("t"))
                v.waitev(r.getfree(ks))
                kstg = KVS[:, (tt % 2) * 512:(tt % 2) * 512 + 512]
                v.tensor_copy(kstg, ps[:, :]).inc("v", 1)
                r.setfree(pn, "v")
                g.wait("v", r.snap("v"))
                g.dma_start(out=vin_h[tt * 128:(tt + 1) * 128, fh * 512:(fh + 1) * 512],
                            in_=kstg).inc("gkv", 16)
                r.setfree(ks, "gkv")
            r.setfree("VRHS", "t")
        g.wait("gkv", r.snap("gkv"))
        g.collective_compute("AllGather", ALU.bypass, replica_groups=[list(range(NCORE))],
                             ins=[kin_h], outs=[kout_h]).inc("cc", 1)
        g.collective_compute("AllGather", ALU.bypass, replica_groups=[list(range(NCORE))],
                             ins=[vin_h], outs=[vout_h]).inc("cc", 1)
        ag_cc = r.snap("cc")
        for hp in range(NHP):
            key = "W0" if hp % 2 == 0 else "W1"
            woff = 0 if hp % 2 == 0 else 1024
            g.waitev(r.getfree(key))
            g.dma_start(out=WT[:, woff:woff + 1024].rearrange("p (a q) -> p a q", q=128),
                        in_=wqb.rearrange("(a p) q -> p a q", p=128)
                        [:, :, hp * 128:(hp + 1) * 128]).inc("gw", 16)
            wdma = r.snap(            "gw")
            t.wait("dw", wdma)
            qb0 = (hp % 2) * 2048
            for tc in range(NTC):
                ps, pn = (psA, "psA") if tc % 2 == 0 else (psB, "psB")
                t.waitev(r.getfree(pn))
                t.waitev(r.getfree(f"hpbq{hp % 2}"))
                last = None
                for dc in range(DC):
                    last = t.matmul(ps[:, :], WT[:, woff + dc * 128:woff + (dc + 1) * 128],
                                    zbt(dc)[:, 128 + tc * 512:128 + (tc + 1) * 512],
                                    start=(dc == 0), stop=(dc == DC - 1))
                last.inc("t", 1)
                s.wait("t", r.snap("t"))
                s.copy(HPB[:, qb0 + tc * 512:qb0 + (tc + 1) * 512], ps[:, :]).inc("s", 1)
                r.setfree(pn, "s")
            r.setfree(key, "s")
            qproj_s = r.snap("s")
            if hp == 0:
                g.wait("cc", ag_cc)
                g.dma_start(out=P0S[0:1, 1663:1664], in_=mxo_h[0:1, 0:1]).inc("dmx", 16)
            agw = r.snap("dmx")
            for qs in range(4):
                q0 = qb0 + qs * 512
                t.waitev(r.getfree("psA"))
                t.waitev(r.getfree("psC"))
                if hp == 0 and qs == 0:
                    t.wait("s", qproj_s)
                t.matmul(psA[:, :], ZEROB[:, :], MASKN[:, :], start=True, stop=False,
                         skip_group_check=True)
                t.matmul(psC[:, :], ZEROB[:, :], MASKN[:, :], start=True, stop=False,
                         skip_group_check=True)
                def kv_fetch(kb):
                    ch = kb // 4
                    rank, cl0 = kb // 16, (kb % 16) * 128
                    kslot = f"bk{ch % 2}"
                    y.wait("dmx", agw)
                    y.waitev(r.getfree(kslot))
                    y.dma_start(out=KVS[:, (ch % 2) * 512:(ch % 2) * 512 + 512],
                                in_=kout_h[rank * D + hp * 128:rank * D + (hp + 1) * 128,
                                           cl0:cl0 + 512]).inc("dkv", 16)
                    y.dma_start(out=KVS[:, 1024 + (ch % 2) * 512:1024 + (ch % 2) * 512 + 512]
                                .rearrange("p (a q) -> p a q", q=128),
                                in_=vout_h[kb * 128:(kb + 4) * 128,
                                           hp * 128:(hp + 1) * 128]
                                .rearrange("(a p) q -> p a q", p=128)).inc("dkv", 16)
                    return r.snap("dkv")

                def emit_scores(kb, kv_dma):
                    ch = kb // 4
                    ko = (ch % 2) * 512 + (kb % 4) * 128
                    so = (kb % 2) * 512
                    t.wait("dkv", kv_dma)
                    t.waitev(r.getfree(f"psS1{kb % 2}"))
                    t.matmul(psS1[:, so:so + 512], KVS[0:64, ko:ko + 128],
                             HPB[0:64, q0:q0 + 512], start=True, stop=True)
                    t.waitev(r.getfree(f"psS2{kb % 2}"))
                    t.matmul(psS2[:, so:so + 512], KVS[64:128, ko:ko + 128],
                             HPB[64:128, q0:q0 + 512], start=True, stop=True,
                             tile_position=(64, 0)).inc("t", 1)
                    return r.snap("t")

                kv_dma = kv_fetch(0)
                sc_t = {0: emit_scores(0, kv_dma)}
                for kb in range(NKB):
                    ch = kb // 4
                    if kb % 4 == 3 and kb < NKB - 1:
                        kv_dma = kv_fetch(kb + 1)
                    if kb < NKB - 1:
                        sc_t[kb + 1] = emit_scores(kb + 1, kv_dma)
                    p1s, p2s = f"pb1{kb % 2}", f"pb2{kb % 2}"
                    pt1 = SHUF[:, (kb % 2) * 512:(kb % 2) * 512 + 512]
                    pt2 = RTMP[:, (kb % 2) * 512:(kb % 2) * 512 + 512]
                    s.wait("t", sc_t.pop(kb))
                    s.waitev(r.getfree(p1s))
                    s.activation(pt1, psS1[:, (kb % 2) * 512:(kb % 2) * 512 + 512],
                                 AF.Exp, bias=-EXPB_B, scale=SCALE).inc("s", 1)
                    r.setfree(f"psS1{kb % 2}", "s")
                    s.waitev(r.getfree(p2s))
                    s.activation(pt2, psS2[:, (kb % 2) * 512:(kb % 2) * 512 + 512],
                                 AF.Exp, bias=-EXPB_B, scale=SCALE).inc("s", 1)
                    r.setfree(f"psS2{kb % 2}", "s")
                    exp_s = r.snap("s")
                    vo = 1024 + (ch % 2) * 512 + (kb % 4) * 128
                    t.wait("s", exp_s)
                    t.matmul(psA[0:64, :], KVS[:, vo:vo + 64], pt1,
                             start=False, stop=False, skip_group_check=True)
                    t.matmul(psA[64:128, :], KVS[:, vo + 64:vo + 128], pt2,
                             start=False, stop=(kb == NKB - 1), tile_position=(0, 64),
                             skip_group_check=True)
                    t.matmul(psC[0:1, :], ONESB[:, 0:1], pt1,
                             start=False, stop=False, skip_group_check=True)
                    last = t.matmul(psC[32:33, :], ONESB[:, 0:1], pt2,
                                    start=False, stop=(kb == NKB - 1),
                                    tile_position=(0, 32), skip_group_check=True)
                    last.inc("t", 1)
                    r.setfree(p1s, "t")
                    r.setfree(p2s, "t")
                    if kb % 4 == 3:
                        r.setfree(f"bk{ch % 2}", "t")
                v.wait("t", r.snap("t"))
                v.reciprocal(P0S[0:1, 0:512], psC[0:1, :])
                v.reciprocal(FSC[32:33, 0:512], psC[32:33, :])
                v.tensor_copy(P0S[0:1, 512:1024], FSC[32:33, 0:512]).inc("v", 1)
                r.setfree("psC", "v")
                t.wait("v", r.snap("v"))
                t.waitev(r.getfree("psB"))
                t.matmul(psB[:, :], ZEROB[:, :], MASKN[:, :], start=True, stop=False,
                         skip_group_check=True)
                t.matmul(psB[0:64, :], ONE_RF[0:1, 0:64], P0S[0:1, 0:512],
                         start=False, stop=False, skip_group_check=True)
                t.matmul(psB[64:128, :], ONE_RF[0:1, 0:64], P0S[0:1, 512:1024],
                         start=False, stop=True, tile_position=(0, 64),
                         skip_group_check=True).inc("t", 1)
                v.wait("t", r.snap("t"))
                v.tensor_copy(CTMP[:, 0:512], psA[:, :])
                v.tensor_tensor(ctxt(hp)[:, qs * 512:qs * 512 + 512], CTMP[:, 0:512],
                                psB[:, :], ALU.mult).inc("v", 1)
                r.setfree("psA", "v")
                r.setfree("psB", "v")
            r.setfree(f"hpbq{hp % 2}", "t")
        attnb_v = r.snap("v")
        for dc in range(DC):
            key = "W0" if dc % 2 == 0 else "W1"
            woff = 0 if dc % 2 == 0 else 1024
            g.waitev(r.getfree(key))
            g.dma_start(out=WT[:, woff:woff + 1024].rearrange("p (a q) -> p a q", q=128),
                        in_=wob.rearrange("(a p) q -> p a q", p=128)
                        [:, :, dc * 128:(dc + 1) * 128]).inc("gw", 16)
            wdma = r.snap(            "gw")
            t.wait("dw", wdma)
            if dc == 0:
                t.wait("v", attnb_v)
            for tc in range(NTC):
                ps, pn = (psA, "psA") if tc % 2 == 0 else (psB, "psB")
                t.waitev(r.getfree(pn))
                last = None
                for hp in range(NHP):
                    last = t.matmul(ps[:, :], WT[:, woff + hp * 128:woff + (hp + 1) * 128],
                                    ctxt(hp)[:, tc * 512:(tc + 1) * 512],
                                    start=(hp == 0), stop=(hp == NHP - 1))
                last.inc("t", 1)
                v.wait("t", r.snap("t"))
                dst = xt(dc)[:, 128 + tc * 512:128 + (tc + 1) * 512]
                v.scalar_tensor_tensor(dst, ps[:, :], SMALL[:, cBOB + dc:cBOB + dc + 1],
                                       dst, ALU.add, ALU.add).inc("v", 1)
                r.setfree(pn, "v")
            r.setfree(key, "v")

        if stop_after == "p3":
            emit_store()
            return

        emit_ffn(1)

        emit_store()

    return nc, gen


def compile_kernel(stop_after="p5"):
    nc, gen = build_program(stop_after)
    import contextlib
    stack = contextlib.ExitStack()
    semhs = {nm: stack.enter_context(nc.semaphore(f"sem_{nm}")) for nm in SEM_NAMES}
    with nc.Block() as block:
        @block.tensor
        def _(eng):
            gen(Router("t", {"t": eng, "v": None, "s": None, "g": None, "y": None}, semhs))
        @block.vector
        def _(eng):
            gen(Router("v", {"t": None, "v": eng, "s": None, "g": None, "y": None}, semhs))
        @block.scalar
        def _(eng):
            gen(Router("s", {"t": None, "v": None, "s": eng, "g": None, "y": None}, semhs))
        @block.gpsimd
        def _(eng):
            gen(Router("g", {"t": None, "v": None, "s": None, "g": eng, "y": None}, semhs))
        @block.sync
        def _(eng):
            gen(Router("y", {"t": None, "v": None, "s": None, "g": None, "y": eng}, semhs))
    stack.close()
    return nc


def make_tables(core):
    import ml_dtypes
    bf = ml_dtypes.bfloat16
    pos0 = core * T - HL
    pos = pos0 + np.arange(TH, dtype=np.float64)
    inv_freq = 1.0 / (10000.0 ** (np.arange(0, 64, 2, dtype=np.float64) / 64.0))
    ang = pos[:, None] * inv_freq[None, :]
    cos = np.concatenate([np.cos(ang), np.cos(ang)], axis=1)
    sin = np.concatenate([np.sin(ang), np.sin(ang)], axis=1)
    sin_signed = sin.copy()
    sin_signed[:, :32] *= -1.0
    cosT = np.zeros((128, TH), dtype=np.float64)
    sinT = np.zeros((128, TH), dtype=np.float64)
    cosT[0:64, :] = cos.T
    cosT[64:128, :] = cos.T
    sinT[0:64, :] = sin_signed.T
    sinT[64:128, :] = sin_signed.T
    kk = np.arange(128)[:, None]
    qq = np.arange(128)[None, :]
    mprev = (kk >= qq).astype(np.float64)
    mcur = (kk <= qq).astype(np.float64)
    maskn = np.concatenate([mprev, mcur, mprev, mcur], axis=1)
    maskf = maskn.copy()
    if core == 0:
        maskf[:, 0:128] = 0.0
        maskf[:, 256:384] = 0.0
    return (cosT.astype(bf), sinT.astype(bf), maskn.astype(bf), maskf.astype(bf))


def make_inmaps(inputs):
    import ml_dtypes
    bf = ml_dtypes.bfloat16
    x_full = np.ascontiguousarray(np.asarray(inputs["L0_attn_a_hidden_states"],
                                             dtype=np.float32).reshape(SEQ, D))
    common = dict(
        wqa=np.ascontiguousarray(np.asarray(inputs["L0_attn_a_Wq"], dtype=np.float32)),
        bqa=np.ascontiguousarray(np.asarray(inputs["L0_attn_a_bq"], dtype=np.float32)),
        wka=np.ascontiguousarray(np.asarray(inputs["L0_attn_a_Wk"], dtype=np.float32)),
        bka=np.ascontiguousarray(np.asarray(inputs["L0_attn_a_bk"], dtype=np.float32)),
        wva=np.ascontiguousarray(np.asarray(inputs["L0_attn_a_Wv"], dtype=np.float32)),
        bva=np.ascontiguousarray(np.asarray(inputs["L0_attn_a_bv"], dtype=np.float32)),
        woa=np.ascontiguousarray(np.asarray(inputs["L0_attn_a_Wo"], dtype=np.float32)),
        boa=np.ascontiguousarray(np.asarray(inputs["L0_attn_a_bo"], dtype=np.float32)),
        wqb=np.ascontiguousarray(np.asarray(inputs["L0_attn_b_Wq"], dtype=np.float32)),
        wkvb=np.ascontiguousarray(np.asarray(inputs["L0_attn_b_Wkv"], dtype=np.float32)),
        wob=np.ascontiguousarray(np.asarray(inputs["L0_attn_b_Wo"], dtype=np.float32)),
        bob=np.ascontiguousarray(np.asarray(inputs["L0_attn_b_bo"], dtype=np.float32)),
        w1=np.ascontiguousarray(np.asarray(inputs["L0_ffn_w1"], dtype=np.float32)),
        b1=np.ascontiguousarray(np.asarray(inputs["L0_ffn_b1"], dtype=np.float32)),
        w2=np.ascontiguousarray(np.asarray(inputs["L0_ffn_w2"], dtype=np.float32)),
        b2=np.ascontiguousarray(np.asarray(inputs["L0_ffn_b2"], dtype=np.float32)),
        ident=np.eye(128, dtype=np.float32),
        identb=np.eye(128).astype(bf),
        consts=np.array([[1e-5, -EXPB_A, -EXPB_B, 0.5, 0, 0, 0, 0]], dtype=np.float32),
        constsb=np.tile(np.array([[-EXPB_A, -EXPB_B]], dtype=np.float32), (128, 1)),
    )
    in_maps = []
    for c in range(NCORE):
        xs = np.zeros((TH, D), dtype=np.float32)
        if c == 0:
            xs[HL:, :] = x_full[0:T, :]
        else:
            xs[:, :] = x_full[c * T - HL:c * T - HL + TH, :]
        cosT, sinT, maskn, maskf = make_tables(c)
        m = dict(common)
        m.update(x=xs, cosT=cosT, sinT=sinT, maskn=maskn, maskf=maskf)
        in_maps.append(m)
    return in_maps


_NC_CACHE = {}


def kernel(**inputs):
    from concourse.bass_utils import run_bass_kernel_spmd
    if "nc" not in _NC_CACHE:
        _NC_CACHE["nc"] = compile_kernel("p5")
    nc = _NC_CACHE["nc"]
    in_maps = make_inmaps(inputs)
    res = run_bass_kernel_spmd(nc, in_maps, core_ids=list(range(NCORE)))
    out = np.concatenate([res.results[c]["out"] for c in range(NCORE)], axis=0)
    return out.reshape(1, SEQ, D).astype(np.float32)


# version 05 (final)
# speedup vs baseline: 1.7684x; 1.1014x over previous
import numpy as np
import concourse.bass as bass
import concourse.mybir as mybir

F32 = mybir.dt.float32
BF16 = mybir.dt.bfloat16
I32 = mybir.dt.int32
AF = mybir.ActivationFunctionType
ALU = mybir.AluOpType
AX = mybir.AxisListType

NCORE = 8
T = 2048
HL = 128
TH = T + HL
D = 1024
DC = 8
NHP = 8
FF = 4096
NFC = 32
SEQ = 16384
NKB = 128
NTC = 4
LN_TH = [(0, 512), (512, 512), (1024, 512), (1536, 512), (2048, 128)]
LN_T = [(128, 512), (640, 512), (1152, 512), (1664, 512)]
EXPB_A = 6.0
EXPB_B = 24.0
SCALE = 0.125
INV_D = 1.0 / 1024.0
import os
SERIAL_ATT = bool(int(os.environ.get("SERIAL_ATT", "0")))

DSTREAMS = ["dx", "dw", "dwz", "dq", "dkv", "dh", "dmx", "dout",
            "gx", "gw", "gq", "gh", "gkv"]
SEM_NAMES = [s + p for s in DSTREAMS for p in ("0", "1")] + ["t", "v", "s", "g", "cc"]


class Inst:
    __slots__ = ("r", "i")
    def __init__(self, r, i):
        self.r, self.i = r, i
    def inc(self, sem, n=1):
        self.r.cnt[sem] = self.r.cnt.get(sem, 0) + n
        if self.i is not None:
            self.i.then_inc(self.r.semh[sem], n)
        return self


class PendingDma:
    __slots__ = ("eng", "a", "k")
    def __init__(self, eng, a, k):
        self.eng, self.a, self.k = eng, a, k
    def inc(self, stream, n=16):
        r = self.eng.r
        idx = r.dsn.get(stream, 0)
        sem = f"{stream}{idx % 2}"
        r.dsn[stream] = idx + 1
        self.eng.wait(sem)
        r.cnt[sem] = r.cnt.get(sem, 0) + n
        if self.eng.real is not None:
            self.eng.real.dma_start(*self.a, **self.k).then_inc(r.semh[sem], n)
        return self


FENCED_OPS = {"tensor_reduce", "reduce_max", "reduce_sum", "reciprocal"}


class FInst(Inst):
    __slots__ = ("eng",)
    def __init__(self, r, i, eng):
        super().__init__(r, i)
        self.eng = eng
    def inc(self, sem, n=1):
        super().inc(sem, n)
        if self.eng.pend is self:
            self.eng.pend = ("done", sem, self.r.cnt[sem])
        return self


class Eng:
    def __init__(self, r, real, name=None):
        self.r, self.real, self.name = r, real, name
        self.pend = None
    def _flush(self):
        p = self.pend
        if p is None:
            return
        self.pend = None
        if isinstance(p, FInst):
            p_inc = Inst.inc
            p_inc(p, self.name, 1)
            self.wait(self.name)
        else:
            _, sem, val = p
            self.wait(sem, val)
    def __getattr__(self, op):
        real, r = self.real, self.r
        if op == "dma_start":
            def dcall(*a, **k):
                self._flush()
                return PendingDma(self, a, k)
            return dcall
        if self.name in ("v", "s", "g") and op not in ("wait_ge",):
            sem = self.name
            def fcall(*a, **k):
                self._flush()
                fence = op in FENCED_OPS
                if not fence and a:
                    try:
                        sh = list(a[0].shape)
                        free = 1
                        for d in sh[1:]:
                            free *= d
                        fence = free <= 64
                    except Exception:
                        pass
                if fence:
                    inst = FInst(r, getattr(real, op)(*a, **k) if real is not None else None, self)
                    self.pend = inst
                    return inst
                return Inst(r, getattr(real, op)(*a, **k) if real is not None else None)
            return fcall
        def call(*a, **k):
            return Inst(r, getattr(real, op)(*a, **k) if real is not None else None)
        return call
    def wait(self, sem, val=None):
        if sem in DSTREAMS:
            evs = val if isinstance(val, list) else self.r.dsnap(sem)
            for sm, vv in evs:
                if self.real is not None and vv > 0:
                    self.real.wait_ge(self.r.semh[sm], vv)
            return evs
        v = self.r.cnt.get(sem, 0) if val is None else val
        if self.real is not None and v > 0:
            self.real.wait_ge(self.r.semh[sem], v)
        return v
    def waitev(self, evs):
        if evs:
            for sem, val in evs:
                self.wait(sem, val)


class Router:
    def __init__(self, active, engines, semh):
        self.cnt = {}
        self.semh = semh
        self.free = {}
        self.dsn = {}
        for nm in ("t", "v", "s", "g", "y"):
            setattr(self, nm, Eng(self, engines[nm] if active == nm else None, nm))
    def snap(self, sem):
        if sem in DSTREAMS:
            return self.dsnap(sem)
        return self.cnt.get(sem, 0)
    def setfree(self, buf, *sems):
        evs = []
        for sm in sems:
            if sm in DSTREAMS:
                evs.extend(self.dsnap(sm))
            else:
                evs.append((sm, self.cnt.get(sm, 0)))
        self.free[buf] = evs
    def getfree(self, buf):
        return self.free.get(buf)
    def dma(self, E, stream, *a, **kw):
        idx = self.dsn.get(stream, 0)
        sem = f"{stream}{idx % 2}"
        E.wait(sem)
        E.dma_start(*a, **kw).inc(sem, 16)
        self.dsn[stream] = idx + 1
    def dsnap(self, stream):
        return [(f"{stream}0", self.cnt.get(f"{stream}0", 0)),
                (f"{stream}1", self.cnt.get(f"{stream}1", 0))]


def build_program(stop_after="p5"):
    nc = bass.Bass("TRN2", target_bir_lowering=False, detect_race_conditions=False)

    def din(name, shape, dt=F32):
        return nc.dram_tensor(name, shape, dt, kind="ExternalInput").ap()
    x_in = din("x", [TH, D])
    wqa, bqa = din("wqa", [D, D]), din("bqa", [D])
    wka, bka = din("wka", [D, D]), din("bka", [D])
    wva, bva = din("wva", [D, D]), din("bva", [D])
    woa, boa = din("woa", [D, D]), din("boa", [D])
    wqb = din("wqb", [D, D])
    wkvb = din("wkvb", [D, 2 * D])
    wob, bob = din("wob", [D, D]), din("bob", [D])
    w1, b1 = din("w1", [FF, D]), din("b1", [FF])
    w2, b2 = din("w2", [D, FF]), din("b2", [D])
    cos_in = din("cosT", [128, TH], BF16)
    sin_in = din("sinT", [128, TH], BF16)
    maskn_in = din("maskn", [128, 512], BF16)
    maskf_in = din("maskf", [128, 512], BF16)
    ident_in = din("ident", [128, 128])
    identb_in = din("identb", [128, 128], BF16)
    consts_in = din("consts", [1, 8])
    constsb_in = din("constsb", [128, 2])
    out = nc.dram_tensor("out", [T, D], F32, kind="ExternalOutput").ap()

    def dint(name, shape, dt=BF16):
        return nc.dram_tensor(name, shape, dt, kind="Internal").ap()
    wstage = dint("wstage", [D, D])
    waT = [dint(f"w{n}aT_h", [D, D]) for n in ("q", "k", "v", "o")]
    w1q_h = dint("w1q_h", [FF, D])
    w2q_h = dint("w2q_h", [D, FF])
    w1qT_h = dint("w1qT_h", [NFC, DC, 128, 128])
    w2qT_h = dint("w2qT_h", [DC, NFC, 128, 128])
    h_h = dint("h_h", [NFC, NTC, 128, 512])
    kin_h = dint("kin_h", [D, T])
    vin_h = dint("vin_h", [T, D])
    kout_h = nc.dram_tensor("kout_h", [NCORE * D, T], BF16, kind="Internal",
                            addr_space="Shared").ap()
    vout_h = nc.dram_tensor("vout_h", [SEQ, D], BF16, kind="Internal",
                            addr_space="Shared").ap()
    mxi_h = dint("mxi_h", [1, 16], F32)
    mxo_h = dint("mxo_h", [1, 16], F32)
    pmr_h = dint("pmr_h", [128, 4], F32)

    def sal(name, shape, dt):
        return nc.alloc_sbuf_tensor(name, shape, dt).ap()
    XT = sal("XT", [128, DC * TH], F32)
    ZBT = sal("ZBT", [128, DC * TH], BF16)
    CTXT = sal("CTXT", [128, NHP * T], BF16)
    HPB = sal("HPB", [128, 2048 + TH + TH], BF16)
    WT = sal("WT", [128, 5120], BF16)
    COS = sal("COS", [128, TH], BF16)
    SIN = sal("SIN", [128, TH], BF16)
    SHUF = sal("SHUF", [128, 2176], BF16)
    RTMP = sal("RTMP", [128, 2176], BF16)
    MASKN = sal("MASKN", [128, 512], BF16)
    MASKF = sal("MASKF", [128, 512], BF16)
    IDE = sal("IDE", [128, 128], F32)
    IDEB = sal("IDEB", [128, 128], BF16)
    ONESB = sal("ONESB", [128, 2], BF16)
    ZEROB = sal("ZEROB", [128, 128], BF16)
    ONE_RB = sal("ONE_RB", [1, 128], BF16)
    ONE_RF = sal("ONE_RF", [1, 128], F32)
    BVROW = sal("BVROW", [1, D], BF16)
    ROWQ = sal("ROWQ", [33, TH], F32)
    P0S = sal("P0S", [1, 1664], F32)
    FSC = sal("FSC", [128, 3072], F32)
    CTMP = FSC[:, 2048:3072]
    CTN = sal("CTN", [128, 256], BF16)
    HTI = HPB[:, 0:2048]
    KVS = HPB[:, 4096:6144]
    ISC = KVS.bitcast(I32)
    SMALL = sal("SMALL", [128, 256], F32)
    CONS = sal("CONS", [1, 8], F32)
    CONSB = sal("CONSB", [128, 2], F32)
    cBQA, cBKA, cBOA, cBOB = 0, 8, 16, 24
    cSW1, cSW2, cSWS1, cSWS2 = 32, 64, 72, 104
    cB1QS, cB2QS, cB1R, cB2R = 112, 144, 152, 184
    cU, cHU, cSCB, cTMP = 192, 200, 232, 240

    psS1 = nc.alloc_psum_tensor("psS1", [128, 1024], F32).ap()
    psS2 = nc.alloc_psum_tensor("psS2", [128, 1024], F32).ap()
    psA = nc.alloc_psum_tensor("psA", [128, 512], F32).ap()
    psB = nc.alloc_psum_tensor("psB", [128, 512], F32).ap()
    psC = nc.alloc_psum_tensor("psC", [128, 512], F32).ap()
    psD = nc.alloc_psum_tensor("psD", [128, 512], F32).ap()

    nc.const_aps.register(F32, 1e-5, CONS[0:1, 0:1])
    nc.const_aps.register(F32, -EXPB_A, CONSB[:, 0:1])
    nc.const_aps.register(F32, -EXPB_B, CONSB[:, 1:2])

    def xt(dc):
        return XT[:, dc * TH:(dc + 1) * TH]
    def zbt(dc):
        return ZBT[:, dc * TH:(dc + 1) * TH]
    def ctxt(hp):
        return CTXT[:, hp * T:(hp + 1) * T]

    def gen(r: Router):
        t, v, s, g, y = r.t, r.v, r.s, r.g, r.y

        def phase_sync(engines="tvsg", sems=None):
            if sems is None:
                sems = ["t", "v", "s", "dw", "dwz", "dq", "dh", "dkv", "dmx"]
            for en in engines:
                E = getattr(r, en)
                for sm in sems:
                    E.wait(sm)

        def pmax_to_scalar(src_col_ap, dst_ap):
            v.memset(FSC[:, 1024:1152], -3.0e38)
            v.tensor_copy(FSC[:, 1024:1025], src_col_ap).inc("v", 1)
            t.wait("v", r.snap("v"))
            t.waitev(r.getfree("psD"))
            t.transpose(psD[:, 0:128], FSC[:, 1024:1152], IDE).inc("t", 1)
            v.wait("t", r.snap("t"))
            v.tensor_reduce(dst_ap, psD[0:1, 0:128], AX.X, ALU.max)
            r.setfree("psD", "v")

        y.dma_start(out=CONS[:, :], in_=consts_in[:, :]).inc("dx", 16)
        y.dma_start(out=CONSB[:, :], in_=constsb_in[:, :]).inc("dx", 16)
        y.dma_start(out=COS[:, :], in_=cos_in[:, :]).inc("dx", 16)
        y.dma_start(out=SIN[:, :], in_=sin_in[:, :]).inc("dx", 16)
        y.dma_start(out=MASKN[:, :], in_=maskn_in[:, :]).inc("dx", 16)
        y.dma_start(out=MASKF[:, :], in_=maskf_in[:, :]).inc("dx", 16)
        y.dma_start(out=IDE[:, :], in_=ident_in[:, :]).inc("dx", 16)
        y.dma_start(out=IDEB[:, :], in_=identb_in[:, :]).inc("dx", 16)
        for (src, col) in ((bqa, cBQA), (bka, cBKA), (boa, cBOA), (bob, cBOB)):
            y.dma_start(out=SMALL[:, col:col + 8],
                        in_=src.rearrange("(c p) -> p c", p=128),
                        allow_slow_non_contiguous=True).inc("dx", 16)
        y.dma_start(out=SMALL[:, cB1R:cB1R + 32],
                    in_=b1.rearrange("(c p) -> p c", p=128),
                    allow_slow_non_contiguous=True).inc("dx", 16)
        y.dma_start(out=SMALL[:, cB2R:cB2R + 8],
                    in_=b2.rearrange("(c p) -> p c", p=128),
                    allow_slow_non_contiguous=True).inc("dx", 16)
        g.dma_start(out=BVROW[0:1, :], in_=bva[None, :]).inc("gx", 16)
        consts_dx = r.snap("dx")
        consts_gx = r.snap("gx")

        v.wait("dx", consts_dx)
        v.wait("gx", consts_gx)
        v.memset(ONESB[:, :], 1.0)
        v.memset(ZEROB[:, :], 0.0)
        v.memset(ONE_RB[0:1, :], 1.0)
        v.memset(ONE_RF[0:1, :], 1.0).inc("v", 1)
        s.wait("dx", consts_dx)
        s.wait("gx", consts_gx)

        t.wait("gx", consts_gx)
        wa_ready = []
        for widx, wsrc in enumerate((wqa, wka, wva, woa)):
            if widx > 0:
                g.waitev(r.getfree("wstage"))
            g.dma_start(out=wstage[:, :], in_=wsrc[:, :]).inc("gw", 16)
            cast_dw = r.snap("gw")
            for c in range(DC):
                slot = "W4"
                off = 4096
                y.wait("gw", cast_dw)
                y.waitev(r.getfree(slot))
                stg = WT[:, off:off + 1024]
                y.dma_start(out=stg, in_=wstage[:, c * 128:(c + 1) * 128],
                            transpose=True).inc("dwz", 16)
                y.wait("dwz", r.snap("dwz"))
                y.dma_start(out=waT[widx][c * 128:(c + 1) * 128, :], in_=stg).inc("dwz", 16)
                r.setfree(slot, "dwz")
            r.setfree("wstage", "dwz")
            wa_ready.append(r.snap("dwz"))

        for tt in range(17):
            slot = f"xstg{tt % 2}"
            y.waitev(r.getfree(slot))
            stg = FSC[:, (tt % 2) * 1024:(tt % 2) * 1024 + 1024]
            y.dma_start(out=stg, in_=x_in[tt * 128:(tt + 1) * 128, :]).inc("dx", 16)
            t.wait("dx", r.snap("dx"))
            for dc in range(DC):
                ps, pn = (psA, "psA") if dc % 2 == 0 else (psB, "psB")
                t.waitev(r.getfree(pn))
                t.transpose(ps[:, 0:128], stg[:, dc * 128:(dc + 1) * 128], IDE).inc("t", 1)
                v.wait("t", r.snap("t"))
                v.tensor_copy(xt(dc)[:, tt * 128:(tt + 1) * 128], ps[:, 0:128]).inc("v", 1)
                r.setfree(pn, "v")
            r.setfree(slot, "v")
        x_loaded_v = r.snap("v")

        def emit_ln(chunks, zb_out, zmax):
            def lnxb(dc, W):
                return HPB[:, dc * 512:dc * 512 + W]
            def lnxq(dc, W):
                return (SHUF if dc < 4 else RTMP)[:, (dc % 4) * 512:(dc % 4) * 512 + W]
            if zmax:
                v.memset(SMALL[:, cU:cU + 8], -3.0e38).inc("v", 1)
            s.wait("v", r.snap("v"))
            for (c0, W) in chunks:
                s.waitev(r.getfree("SHUF"))
                s.waitev(r.getfree("RTMP"))
                for dc in range(DC):
                    s.copy(lnxb(dc, W), xt(dc)[:, c0:c0 + W])
                last = None
                for dc in range(DC):
                    last = s.activation(lnxq(dc, W), xt(dc)[:, c0:c0 + W], AF.Square)
                last.inc("s", 1)
                t.wait("s", r.snap("s"))
                t.waitev(r.getfree("psA"))
                t.waitev(r.getfree("psB"))
                for dc in range(DC):
                    t.matmul(psA[0:1, 0:W], ONESB[:, 0:1], lnxb(dc, W),
                             start=(dc == 0), stop=(dc == DC - 1))
                last = None
                for dc in range(DC):
                    last = t.matmul(psB[0:1, 0:W], ONESB[:, 0:1], lnxq(dc, W),
                                    start=(dc == 0), stop=(dc == DC - 1))
                last.inc("t", 1)
                r.setfree("SHUF", "t")
                r.setfree("RTMP", "t")
                v.wait("t", r.snap("t"))
                v.tensor_scalar(P0S[0:1, 0:W], psA[0:1, 0:W], INV_D, None, ALU.mult)
                v.tensor_scalar(P0S[0:1, 512:512 + W], psB[0:1, 0:W], INV_D, None, ALU.mult)
                v.tensor_tensor(P0S[0:1, 1024:1024 + W], P0S[0:1, 0:W], P0S[0:1, 0:W],
                                ALU.mult)
                v.tensor_tensor(P0S[0:1, 512:512 + W], P0S[0:1, 512:512 + W],
                                P0S[0:1, 1024:1024 + W], ALU.subtract).inc("v", 1)
                r.setfree("psA", "v")
                r.setfree("psB", "v")
                s.wait("v", r.snap("v"))
                s.activation(P0S[0:1, 1024:1024 + W], P0S[0:1, 512:512 + W], AF.Sqrt,
                             bias=1e-5, scale=1.0).inc("s", 1)
                v.wait("s", r.snap("s"))
                v.reciprocal(ROWQ[0:1, c0:c0 + W], P0S[0:1, 1024:1024 + W])
                v.tensor_tensor(ROWQ[32:33, c0:c0 + W], P0S[0:1, 0:W],
                                ROWQ[0:1, c0:c0 + W], ALU.mult)
                v.tensor_copy(P0S[0:1, 1024:1024 + W], ROWQ[32:33, c0:c0 + W]).inc("v", 1)
                t.wait("v", r.snap("v"))
                t.waitev(r.getfree("psC"))
                t.waitev(r.getfree("psD"))
                t.matmul(psC[:, 0:W], ONE_RF[0:1, :], ROWQ[0:1, c0:c0 + W],
                         start=True, stop=True)
                t.matmul(psD[:, 0:W], ONE_RF[0:1, :], P0S[0:1, 1024:1024 + W],
                         start=True, stop=True).inc("t", 1)
                v.wait("t", r.snap("t"))
                if zb_out:
                    last = None
                    for dc in range(DC):
                        v.tensor_tensor(FSC[:, 0:W], xt(dc)[:, c0:c0 + W], psC[:, 0:W],
                                        ALU.mult)
                        last = v.tensor_tensor(zbt(dc)[:, c0:c0 + W], FSC[:, 0:W],
                                               psD[:, 0:W], ALU.subtract)
                    last.inc("v", 1)
                if zmax:
                    last = None
                    for dc in range(DC):
                        v.tensor_tensor(FSC[:, 0:W], xt(dc)[:, c0:c0 + W], psC[:, 0:W],
                                        ALU.mult)
                        v.tensor_tensor(FSC[:, 512:512 + W], FSC[:, 0:W], psD[:, 0:W],
                                        ALU.subtract)
                        v.tensor_reduce(SMALL[:, cTMP:cTMP + 1], FSC[:, 512:512 + W],
                                        AX.X, ALU.max)
                        last = v.tensor_tensor(SMALL[:, cU + dc:cU + dc + 1],
                                               SMALL[:, cU + dc:cU + dc + 1],
                                               SMALL[:, cTMP:cTMP + 1], ALU.max)
                    last.inc("v", 1)
                r.setfree("psC", "v")
                r.setfree("psD", "v")

        def emit_store():
            phase_sync("tg")
            for tt in range(16):
                slot = f"ostg{tt % 2}"
                stg = FSC[:, (tt % 2) * 1024:(tt % 2) * 1024 + 1024]
                for dc in range(DC):
                    ps, pn = (psA, "psA") if dc % 2 == 0 else (psB, "psB")
                    t.waitev(r.getfree(pn))
                    if dc == 0:
                        t.waitev(r.getfree(slot))
                    t.transpose(ps[:, 0:128],
                                xt(dc)[:, 128 + tt * 128:128 + (tt + 1) * 128],
                                IDE).inc("t", 1)
                    v.wait("t", r.snap("t"))
                    if dc == 0:
                        v.waitev(r.getfree(slot))
                    v.tensor_copy(stg[:, dc * 128:(dc + 1) * 128], ps[:, 0:128]).inc("v", 1)
                    r.setfree(pn, "v")
                g.wait("v", r.snap("v"))
                g.dma_start(out=out[tt * 128:(tt + 1) * 128, :], in_=stg).inc("dout", 16)
                r.setfree(slot, "dout")
            g.wait("dout", r.snap("dout"))

        if stop_after == "p0":
            emit_store()
            return

        v.wait("v", x_loaded_v)
        emit_ln(LN_TH, zb_out=True, zmax=False)
        ln1_v = r.snap("v")
        if stop_after == "p0ln":
            emit_store()
            return
        p1lvl = {"p1q": 1, "p1k": 2, "p1r": 3, "p1v": 4, "p1att0a": 4.95, "p1att0b": 4.97, "p1att0": 5, "p1att1": 5.1, "p1att2": 5.2, "p1att3": 5.4, "p1att": 5.6}.get(stop_after, 6)

        HQ0, HK0, HV0 = 0, 2048, 2048 + TH

        def wtiles_dma_y(key, dst_off, src_hbm, fcol, ready_dwz):
            y.wait("dwz", ready_dwz)
            y.waitev(r.getfree(key))
            src = src_hbm.rearrange("(a p) q -> p a q", p=128)
            y.dma_start(out=WT[:, dst_off:dst_off + 1024].rearrange("p (a q) -> p a q", q=128),
                        in_=src[:, :, fcol:fcol + 128]).inc("dw", 16)

        for hp in range(NHP):
            wtiles_dma_y("W0", 0, waT[0], hp * 128, wa_ready[0])
            if p1lvl >= 2:
                wtiles_dma_y("W1", 1024, waT[1], hp * 128, wa_ready[1])
            if p1lvl >= 4:
                wtiles_dma_y("W2", 2048, waT[2], hp * 128, wa_ready[2])
            wdma = r.snap("dw")
            t.wait("dw", wdma)
            if hp == 0:
                t.wait("v", ln1_v)
            t.waitev(r.getfree("hq"))
            for tc in range(NTC):
                ps, pn = (psA, "psA") if tc % 2 == 0 else (psB, "psB")
                t.waitev(r.getfree(pn))
                last = None
                for dc in range(DC):
                    last = t.matmul(ps[:, :], WT[:, dc * 128:(dc + 1) * 128],
                                    zbt(dc)[:, 128 + tc * 512:128 + (tc + 1) * 512],
                                    start=(dc == 0), stop=(dc == DC - 1))
                last.inc("t", 1)
                s.wait("t", r.snap("t"))
                s.activation(HPB[:, HQ0 + tc * 512:HQ0 + (tc + 1) * 512], ps[:, :],
                             AF.Identity, bias=SMALL[:, cBQA + hp:cBQA + hp + 1],
                             scale=1.0).inc("s", 1)
                r.setfree(pn, "s")
            qproj_s = r.snap("s")
            if p1lvl < 2:
                continue
            t.waitev(r.getfree("hk"))
            for ci, (c0, W) in enumerate(LN_TH):
                ps, pn = (psA, "psA") if ci % 2 == 0 else (psB, "psB")
                t.waitev(r.getfree(pn))
                last = None
                for dc in range(DC):
                    last = t.matmul(ps[:, 0:W], WT[:, 1024 + dc * 128:1024 + (dc + 1) * 128],
                                    zbt(dc)[:, c0:c0 + W],
                                    start=(dc == 0), stop=(dc == DC - 1))
                last.inc("t", 1)
                s.wait("t", r.snap("t"))
                s.activation(HPB[:, HK0 + c0:HK0 + c0 + W], ps[:, 0:W],
                             AF.Identity, bias=SMALL[:, cBKA + hp:cBKA + hp + 1],
                             scale=1.0).inc("s", 1)
                r.setfree(pn, "s")
            kproj_s = r.snap("s")
            r.setfree("W0", "s")
            r.setfree("W1", "s")
            if p1lvl < 3:
                continue
            v.wait("s", qproj_s)
            v.wait("t", r.snap("t"))
            v.waitev(r.getfree("SHUF"))
            v.waitev(r.getfree("RTMP"))
            QS = HPB[:, HQ0:HQ0 + 2048]
            v.tensor_copy(SHUF[0:32, 0:2048], QS[32:64, :])
            v.tensor_copy(SHUF[32:64, 0:2048], QS[0:32, :])
            v.tensor_copy(SHUF[64:96, 0:2048], QS[96:128, :])
            v.tensor_copy(SHUF[96:128, 0:2048], QS[64:96, :])
            v.tensor_tensor(RTMP[:, 0:2048], QS[:, :], COS[:, 128:TH], ALU.mult)
            v.tensor_tensor(QS[:, :], SHUF[:, 0:2048], SIN[:, 128:TH], ALU.mult)
            v.tensor_tensor(QS[:, :], RTMP[:, 0:2048], QS[:, :], ALU.add).inc("v", 1)
            v.wait("s", kproj_s)
            KS_ = HPB[:, HK0:HK0 + TH]
            v.tensor_copy(SHUF[0:32, 0:TH], KS_[32:64, :])
            v.tensor_copy(SHUF[32:64, 0:TH], KS_[0:32, :])
            v.tensor_copy(SHUF[64:96, 0:TH], KS_[96:128, :])
            v.tensor_copy(SHUF[96:128, 0:TH], KS_[64:96, :])
            v.tensor_tensor(RTMP[:, 0:TH], KS_[:, :], COS[:, :], ALU.mult)
            v.tensor_tensor(KS_[:, :], SHUF[:, 0:TH], SIN[:, :], ALU.mult)
            v.tensor_tensor(KS_[:, :], RTMP[:, 0:TH], KS_[:, :], ALU.add).inc("v", 1)
            rope_v = r.snap("v")
            r.setfree("SHUF", "v")
            r.setfree("RTMP", "v")
            if p1lvl < 4:
                continue
            t.waitev(r.getfree("hv"))
            for tt in range(17):
                ps, pn = (psA, "psA") if tt % 2 == 0 else (psB, "psB")
                t.waitev(r.getfree(pn))
                for dc in range(DC):
                    t.matmul(ps[:, 0:128], zbt(dc)[:, tt * 128:(tt + 1) * 128],
                             WT[:, 2048 + dc * 128:2048 + (dc + 1) * 128],
                             start=(dc == 0), stop=False)
                t.matmul(ps[:, 0:128], ONE_RB[0:1, :],
                         BVROW[0:1, hp * 128:(hp + 1) * 128],
                         start=False, stop=True).inc("t", 1)
                v.wait("t", r.snap("t"))
                v.tensor_copy(HPB[:, HV0 + tt * 128:HV0 + (tt + 1) * 128],
                              ps[:, 0:128]).inc("v", 1)
                r.setfree(pn, "v")
            r.setfree("W2", "v")
            if p1lvl < 5:
                continue
            t.wait("v", r.snap("v"))
            sc_t = {}

            def sc_a(qt):
                k0 = qt * 128
                q0 = HQ0 + qt * 128
                t.waitev(r.getfree("psS1"))
                t.waitev(r.getfree("psS2"))
                t.matmul(psS1[:, 0:128], HPB[0:64, HK0 + k0:HK0 + k0 + 128],
                         HPB[0:64, q0:q0 + 128], start=True, stop=True)
                t.matmul(psS1[:, 512:640], HPB[0:64, HK0 + k0 + 128:HK0 + k0 + 256],
                         HPB[0:64, q0:q0 + 128], start=True, stop=True)
                t.matmul(psS2[:, 0:128], HPB[64:128, HK0 + k0:HK0 + k0 + 128],
                         HPB[64:128, q0:q0 + 128], start=True, stop=True,
                         tile_position=(64, 0))
                t.matmul(psS2[:, 512:640], HPB[64:128, HK0 + k0 + 128:HK0 + k0 + 256],
                         HPB[64:128, q0:q0 + 128], start=True, stop=True,
                         tile_position=(64, 0)).inc("t", 1)
                sc_t[qt] = r.snap("t")

            sc_a(0)
            for qt in range(16):
                if SERIAL_ATT:
                    t.wait("v", r.snap("v"))
                    t.wait("s", r.snap("s"))
                    v.wait("t", r.snap("t"))
                    s.wait("v", r.snap("v"))
                pts = f"pta{qt % 2}"
                pt = SHUF[:, (qt % 2) * 512:(qt % 2) * 512 + 512]
                if p1lvl < 5.1:
                    if qt < 15:
                        sc_a(qt + 1)
                    continue
                s.wait("t", sc_t[qt])
                s.waitev(r.getfree(pts))
                sv1 = psS1.rearrange("p (a b) -> p a b", b=128)[:, 0:5:4, :]
                sv2 = psS2.rearrange("p (a b) -> p a b", b=128)[:, 0:5:4, :]
                s.activation(pt[:, 0:256].rearrange("p (a b) -> p a b", b=128),
                             sv1, AF.Exp, bias=-EXPB_A, scale=SCALE).inc("s", 1)
                r.setfree("psS1", "s")
                s.activation(pt[:, 256:512].rearrange("p (a b) -> p a b", b=128),
                             sv2, AF.Exp, bias=-EXPB_A, scale=SCALE).inc("s", 1)
                r.setfree("psS2", "s")
                if qt < 15:
                    sc_a(qt + 1)
                v.wait("s", r.snap("s"))
                v.tensor_tensor(pt, pt, MASKF if qt == 0 else MASKN, ALU.mult).inc("v", 1)
                if p1lvl < 5.2:
                    r.setfree(pts, "v")
                    continue
                t.wait("v", r.snap("v"))
                t.waitev(r.getfree("psC"))
                first = True
                last = None
                for h in range(2):
                    for blk in range(2):
                        off = (2 * h + blk) * 128
                        vt = HPB[:, HV0 + (qt + blk) * 128 + h * 64:
                                 HV0 + (qt + blk) * 128 + h * 64 + 64]
                        t.matmul(psC[:, h * 64:h * 64 + 64], pt[:, off:off + 128], vt,
                                 start=first, stop=False, skip_group_check=True)
                        first = False
                        last = t.matmul(psC[:, 128 + h:129 + h], pt[:, off:off + 128],
                                        ONESB[:, 0:1], start=False,
                                        stop=(h == 1 and blk == 1), skip_group_check=True)
                last.inc("t", 1)
                r.setfree(pts, "t")
                if p1lvl < 5.4:
                    r.setfree("psC", "t")
                    continue
                ctn = CTN[:, (qt % 2) * 128:(qt % 2) * 128 + 128]
                v.wait("t", r.snap("t"))
                v.waitev(r.getfree(f"ctn{qt % 2}"))
                v.reciprocal(FSC[:, 0:2], psC[:, 128:130])
                v.tensor_scalar(ctn[:, 0:64], psC[:, 0:64], FSC[:, 0:1], None, ALU.mult)
                v.tensor_scalar(ctn[:, 64:128], psC[:, 64:128], FSC[:, 1:2], None,
                                ALU.mult).inc("v", 1)
                r.setfree("psC", "v")
                if p1lvl < 5.6:
                    r.setfree(f"ctn{qt % 2}", "v")
                    continue
                t.wait("v", r.snap("v"))
                t.waitev(r.getfree("psD"))
                t.transpose(psD.bitcast(BF16)[:, 0:128], ctn, IDEB).inc("t", 1)
                r.setfree(f"ctn{qt % 2}", "t")
                v.wait("t", r.snap("t"))
                v.tensor_copy(ctxt(hp)[:, qt * 128:(qt + 1) * 128],
                              psD.bitcast(BF16)[:, 0:128]).inc("v", 1)
                r.setfree(f"hpb{hp % 1}", "v")
                r.setfree("psD", "v")
        attn_v = r.snap("v")
        if stop_after == "p1hpb":
            g.wait("v", attn_v)
            g.wait("s", r.snap("s"))
            ob = out.bitcast(BF16)
            for cb in range(4):
                g.dma_start(out=ob[cb * 128:(cb + 1) * 128, 0:1600],
                            in_=HPB[:, cb * 1600:(cb + 1) * 1600]).inc("dout", 16)
            g.wait("dout", r.snap("dout"))
            return
        if stop_after == "p1ctx":
            g.wait("v", attn_v)
            ob = out.bitcast(BF16)
            for cb in range(8):
                g.dma_start(out=ob[cb * 128:(cb + 1) * 128, 0:2048],
                            in_=CTXT[:, cb * 2048:(cb + 1) * 2048]).inc("dout", 16)
            g.wait("dout", r.snap("dout"))
            return
        if p1lvl < 6:
            emit_store()
            return

        for dc in range(DC):
            key = "W0" if dc % 2 == 0 else "W1"
            woff = 0 if dc % 2 == 0 else 1024
            wtiles_dma_y(key, woff, waT[3], dc * 128, wa_ready[3])
            wdma = r.snap("dw")
            t.wait("dw", wdma)
            if dc == 0:
                t.wait("v", attn_v)
            for tc in range(NTC):
                ps, pn = (psA, "psA") if tc % 2 == 0 else (psB, "psB")
                t.waitev(r.getfree(pn))
                last = None
                for hp in range(NHP):
                    last = t.matmul(ps[:, :], WT[:, woff + hp * 128:woff + (hp + 1) * 128],
                                    ctxt(hp)[:, tc * 512:(tc + 1) * 512],
                                    start=(hp == 0), stop=(hp == NHP - 1))
                last.inc("t", 1)
                v.wait("t", r.snap("t"))
                dst = xt(dc)[:, 128 + tc * 512:128 + (tc + 1) * 512]
                v.scalar_tensor_tensor(dst, ps[:, :], SMALL[:, cBOA + dc:cBOA + dc + 1],
                                       dst, ALU.add, ALU.add).inc("v", 1)
                r.setfree(pn, "v")
            r.setfree(key, "v")

        if stop_after == "p1":
            emit_store()
            return

        phase_sync("vyg")

        def quant_chunk(stg, dst_bf, rsw_col):
            v.tensor_scalar(CTMP[:, :], stg, SMALL[:, rsw_col:rsw_col + 1], 0.5,
                            ALU.mult, ALU.add)
            v.tensor_copy(ISC[:, :], CTMP[:, :])
            v.tensor_copy(stg, ISC[:, :])
            v.tensor_tensor(CTMP[:, :], stg, CTMP[:, :], ALU.is_gt)
            return v.scalar_tensor_tensor(dst_bf, CTMP[:, :], -1.0, stg,
                                          ALU.mult, ALU.add)

        for c in range(NFC):
            slot = f"wqs{c % 2}"
            y.waitev(r.getfree(slot))
            stg = FSC[:, (c % 2) * 1024:(c % 2) * 1024 + 1024]
            y.dma_start(out=stg, in_=w1[c * 128:(c + 1) * 128, :]).inc("dq", 16)
            v.wait("dq", r.snap("dq"))
            v.waitev(r.getfree(f"whti{c % 2}"))
            v.tensor_reduce(SMALL[:, cSW1 + c:cSW1 + c + 1], stg, AX.X, ALU.max,
                            apply_absolute_value=True)
            v.reciprocal(SMALL[:, cTMP:cTMP + 1], SMALL[:, cSW1 + c:cSW1 + c + 1])
            v.tensor_scalar(SMALL[:, cTMP:cTMP + 1], SMALL[:, cTMP:cTMP + 1], 127.0,
                            None, ALU.mult).inc("v", 1)
            v.wait("v", r.snap("v"))
            hti = HTI[:, (c % 2) * 1024:(c % 2) * 1024 + 1024]
            quant_chunk(stg, hti, cTMP).inc("v", 1)
            r.setfree(slot, "v")
            g.wait("v", r.snap("v"))
            g.dma_start(out=w1q_h[c * 128:(c + 1) * 128, :], in_=hti).inc("gq", 16)
            r.setfree(f"whti{c % 2}", "gq")
        for dc in range(DC):
            for cc in range(4):
                slot = f"wqs{cc % 2}"
                y.waitev(r.getfree(slot))
                stg = FSC[:, (cc % 2) * 1024:(cc % 2) * 1024 + 1024]
                y.dma_start(out=stg, in_=w2[dc * 128:(dc + 1) * 128,
                                            cc * 1024:(cc + 1) * 1024]).inc("dq", 16)
                v.wait("dq", r.snap("dq"))
                if cc == 0:
                    v.tensor_reduce(SMALL[:, cSW2 + dc:cSW2 + dc + 1], stg, AX.X,
                                    ALU.max, apply_absolute_value=True)
                else:
                    v.tensor_reduce(SMALL[:, cTMP:cTMP + 1], stg, AX.X, ALU.max,
                                    apply_absolute_value=True)
                    v.tensor_tensor(SMALL[:, cSW2 + dc:cSW2 + dc + 1],
                                    SMALL[:, cSW2 + dc:cSW2 + dc + 1],
                                    SMALL[:, cTMP:cTMP + 1], ALU.max).inc("v", 1)
                r.setfree(slot, "v")
        for dc in range(DC):
            v.reciprocal(SMALL[:, cTMP + 1:cTMP + 2], SMALL[:, cSW2 + dc:cSW2 + dc + 1])
            v.tensor_scalar(SMALL[:, cTMP + 1:cTMP + 2], SMALL[:, cTMP + 1:cTMP + 2],
                            127.0, None, ALU.mult).inc("v", 1)
            v.wait("v", r.snap("v"))
            for cc in range(4):
                slot = f"wqs{cc % 2}"
                y.waitev(r.getfree(slot))
                stg = FSC[:, (cc % 2) * 1024:(cc % 2) * 1024 + 1024]
                y.dma_start(out=stg, in_=w2[dc * 128:(dc + 1) * 128,
                                            cc * 1024:(cc + 1) * 1024]).inc("dq", 16)
                v.wait("dq", r.snap("dq"))
                v.waitev(r.getfree(f"whti{cc % 2}"))
                hti = HTI[:, (cc % 2) * 1024:(cc % 2) * 1024 + 1024]
                quant_chunk(stg, hti, cTMP + 1).inc("v", 1)
                r.setfree(slot, "v")
                g.wait("v", r.snap("v"))
                g.dma_start(out=w2q_h[dc * 128:(dc + 1) * 128,
                                      cc * 1024:(cc + 1) * 1024], in_=hti).inc("gq", 16)
                r.setfree(f"whti{cc % 2}", "gq")
        wq_nat_dq = r.snap("gq")
        y.wait("gq", wq_nat_dq)
        for c in range(DC):
            h0 = (c % 2) * 4096
            y.waitev(r.getfree(f"tps{c % 2}"))
            y.dma_start(out=CTXT[:, h0:h0 + 4096], in_=w1q_h[:, c * 128:(c + 1) * 128],
                        transpose=True).inc("dq", 16)
            g.wait("dq", r.snap("dq"))
            g.dma_start(out=w1qT_h[:, c, :, :].rearrange("a p q -> p a q"),
                        in_=CTXT[:, h0:h0 + 4096]
                        .rearrange("p (a q) -> p a q", q=128)).inc("gq", 16)
            r.setfree(f"tps{c % 2}", "gq")
        for fc in range(NFC):
            h0 = (fc % 2) * 4096
            y.waitev(r.getfree(f"tps{fc % 2}"))
            y.dma_start(out=CTXT[:, h0:h0 + 1024], in_=w2q_h[:, fc * 128:(fc + 1) * 128],
                        transpose=True).inc("dq", 16)
            g.wait("dq", r.snap("dq"))
            g.dma_start(out=w2qT_h[:, fc, :, :].rearrange("a p q -> p a q"),
                        in_=CTXT[:, h0:h0 + 1024]
                        .rearrange("p (a q) -> p a q", q=128)).inc("gq", 16)
            r.setfree(f"tps{fc % 2}", "gq")
        wq_ready = r.snap("gq")
        if stop_after == "p15":
            emit_store()
            return

        def emit_ffn(idx):
            flvl = {"p2sind": 1, "p2ln": 1, "p2zq": 2, "p2m1": 3, "p2s2": 4}.get(stop_after, 9)
            phase_sync("tvsg", sems=["t", "v", "s", "gq", "gw", "dw", "dwz",
                                     "dh", "dkv", "dmx"])
            emit_ln(LN_T, zb_out=False, zmax=True)
            v.tensor_reduce(SMALL[:, cTMP + 2:cTMP + 3], SMALL[:, cU:cU + 8], AX.X,
                            ALU.max)
            pmax_to_scalar(SMALL[:, cTMP + 2:cTMP + 3], P0S[0:1, 1536:1537])
            v.memset(FSC[0:1, 1152:1168], -3.0e38)
            v.tensor_scalar(FSC[0:1, 1152:1168], FSC[0:1, 1152:1168],
                            P0S[0:1, 1536:1537], None, ALU.max).inc("v", 1)
            g.wait("v", r.snap("v"))
            g.dma_start(out=mxi_h[0:1, :], in_=FSC[0:1, 1152:1168]).inc("dmx", 16)
            g.wait("dmx", r.snap("dmx"))
            g.collective_compute("AllReduce", ALU.max, replica_groups=[list(range(NCORE))],
                                 ins=[mxi_h], outs=[mxo_h]).inc("cc", 1)
            g.wait("cc", r.snap("cc"))
            g.dma_start(out=P0S[0:1, 1537:1538], in_=mxo_h[0:1, 0:1]).inc("dmx", 16)
            v.wait("dmx", r.snap("dmx"))
            v.tensor_scalar(P0S[0:1, 1538:1539], P0S[0:1, 1537:1538], 1e-8, 1.0 / 255.0,
                            ALU.max, ALU.mult)
            v.reciprocal(P0S[0:1, 1539:1540], P0S[0:1, 1538:1539])
            t.wait("v", r.snap("v"))
            t.waitev(r.getfree("psC"))
            t.matmul(psC[:, 0:2], ONE_RF[0:1, :], P0S[0:1, 1538:1540],
                     start=True, stop=True).inc("t", 1)
            v.wait("t", r.snap("t"))
            v.tensor_copy(SMALL[:, cSCB:cSCB + 2], psC[:, 0:2]).inc("v", 1)
            v.wait("v", r.snap("v"))
            r.setfree("psC", "v")
            v.tensor_scalar(SMALL[:, cSWS1:cSWS1 + 32], SMALL[:, cSW1:cSW1 + 32],
                            SMALL[:, cSCB:cSCB + 1], None, ALU.mult)
            v.tensor_scalar(SMALL[:, cSWS1:cSWS1 + 32], SMALL[:, cSWS1:cSWS1 + 32],
                            1.0 / 127.0, None, ALU.mult)
            v.reciprocal(FSC[:, 0:32], SMALL[:, cSWS1:cSWS1 + 32])
            v.tensor_tensor(FSC[:, 32:64], SMALL[:, cB1R:cB1R + 32], FSC[:, 0:32],
                            ALU.mult)
            v.tensor_scalar(FSC[:, 32:64], FSC[:, 32:64], 0.5, None, ALU.add)
            v.tensor_copy(ISC[:, 0:32], FSC[:, 32:64])
            v.tensor_copy(FSC[:, 64:96], ISC[:, 0:32])
            v.tensor_tensor(FSC[:, 96:128], FSC[:, 64:96], FSC[:, 32:64], ALU.is_gt)
            v.scalar_tensor_tensor(FSC[:, 64:96], FSC[:, 96:128], -1.0, FSC[:, 64:96],
                                   ALU.mult, ALU.add)
            v.tensor_tensor(SMALL[:, cB1QS:cB1QS + 32], FSC[:, 64:96],
                            SMALL[:, cSWS1:cSWS1 + 32], ALU.mult).inc("v", 1)
            if flvl < 2 and stop_after != "p2sind":
                return
            if stop_after == "p2sind":
                g.wait("v", r.snap("v"))
                g.dma_start(out=out[0:1, 0:4], in_=P0S[0:1, 1536:1540]).inc("dout", 16)
                g.dma_start(out=out[1:2, 0:8], in_=SMALL[0:1, cU:cU + 8]).inc("dout", 16)
                g.dma_start(out=out[2:3, 0:3], in_=SMALL[0:1, cTMP:cTMP + 3]).inc("dout", 16)
                g.wait("dout", r.snap("dout"))
                return
            v.tensor_scalar(ROWQ[0:1, 128:TH], ROWQ[0:1, 128:TH],
                            P0S[0:1, 1539:1540], None, ALU.mult)
            v.tensor_scalar(ROWQ[32:33, 128:TH], ROWQ[32:33, 128:TH],
                            SMALL[32:33, cSCB + 1:cSCB + 2], None, ALU.mult)
            v.tensor_scalar(ROWQ[32:33, 128:TH], ROWQ[32:33, 128:TH], 0.5, None,
                            ALU.subtract).inc("v", 1)
            for (c0, W) in LN_T:
                v.tensor_copy(P0S[0:1, 1024:1024 + W], ROWQ[32:33, c0:c0 + W]).inc("v", 1)
                t.wait("v", r.snap("v"))
                t.waitev(r.getfree("psC"))
                t.waitev(r.getfree("psD"))
                t.matmul(psC[:, 0:W], ONE_RF[0:1, :], ROWQ[0:1, c0:c0 + W],
                         start=True, stop=True)
                t.matmul(psD[:, 0:W], ONE_RF[0:1, :], P0S[0:1, 1024:1024 + W],
                         start=True, stop=True).inc("t", 1)
                v.wait("t", r.snap("t"))
                last = None
                for dc in range(DC):
                    v.tensor_tensor(FSC[:, 0:W], xt(dc)[:, c0:c0 + W], psC[:, 0:W],
                                    ALU.mult)
                    v.tensor_tensor(FSC[:, 0:W], FSC[:, 0:W], psD[:, 0:W], ALU.subtract)
                    v.tensor_copy(ISC[:, 0:W], FSC[:, 0:W])
                    v.tensor_copy(FSC[:, 512:512 + W], ISC[:, 0:W])
                    v.tensor_tensor(FSC[:, 1024:1024 + W], FSC[:, 512:512 + W],
                                    FSC[:, 0:W], ALU.is_gt)
                    v.scalar_tensor_tensor(FSC[:, 512:512 + W], FSC[:, 1024:1024 + W],
                                           -1.0, FSC[:, 512:512 + W], ALU.mult, ALU.add)
                    last = v.tensor_scalar(zbt(dc)[:, c0:c0 + W], FSC[:, 512:512 + W],
                                           0.0, None, ALU.max)
                last.inc("v", 1)
                r.setfree("psC", "v")
                r.setfree("psD", "v")
            zq_v = r.snap("v")
            if stop_after == "p2zqd":
                g.wait("v", zq_v)
                ob = out.bitcast(BF16)
                for cb in range(8):
                    g.dma_start(out=ob[cb * 128:(cb + 1) * 128, 0:2048],
                                in_=ZBT[:, cb * TH + 128:cb * TH + 2176]).inc("dout", 16)
                g.wait("dout", r.snap("dout"))
                return
            if flvl < 3 or stop_after == "p2zqd":
                return
            v.memset(SMALL[:, cHU:cHU + 32], -3.0e38).inc("v", 1)
            for fc in range(NFC):
                key = "W0" if fc % 2 == 0 else "W1"
                woff = 0 if fc % 2 == 0 else 1024
                y.wait("gq", wq_ready)
                y.waitev(r.getfree(key))
                y.dma_start(out=WT[:, woff:woff + 1024].rearrange("p (a q) -> p a q", q=128),
                            in_=w1qT_h[fc, :, :, :].rearrange("a p q -> p a q")).inc("dw", 16)
                wdma = r.snap("dw")
                t.wait("dw", wdma)
                if fc == 0:
                    t.wait("v", zq_v)
                for tc in range(NTC):
                    ps, pn = (psA, "psA") if tc % 2 == 0 else (psB, "psB")
                    t.waitev(r.getfree(pn))
                    last = None
                    for dc in range(DC):
                        last = t.matmul(ps[:, :],
                                        WT[:, woff + dc * 128:woff + (dc + 1) * 128],
                                        zbt(dc)[:, 128 + tc * 512:128 + (tc + 1) * 512],
                                        start=(dc == 0), stop=(dc == DC - 1))
                    last.inc("t", 1)
                    hs = f"hti{(fc * NTC + tc) % 2}"
                    hoff = ((fc * NTC + tc) % 2) * 1024
                    hti = HTI[:, hoff:hoff + 512]
                    s.wait("t", r.snap("t"))
                    s.wait("v", zq_v)
                    s.waitev(r.getfree(hs))
                    s.activation(hti, ps[:, :], AF.Relu,
                                 bias=SMALL[:, cB1QS + fc:cB1QS + fc + 1],
                                 scale=SMALL[:, cSWS1 + fc:cSWS1 + fc + 1]).inc("s", 1)
                    v.wait("t", r.snap("t"))
                    v.wait("s", r.snap("s"))
                    v.tensor_reduce(SMALL[:, cTMP:cTMP + 1], ps[:, :], AX.X, ALU.max)
                    v.tensor_tensor(SMALL[:, cHU + fc:cHU + fc + 1],
                                    SMALL[:, cHU + fc:cHU + fc + 1],
                                    SMALL[:, cTMP:cTMP + 1], ALU.max).inc("v", 1)
                    r.setfree(pn, "v", "s")
                    g.wait("s", r.snap("s"))
                    g.dma_start(out=h_h[fc, tc], in_=hti).inc("gh", 16)
                    r.setfree(hs, "gh")
                r.setfree(key, "t")
            if flvl < 4:
                return
            v.tensor_tensor(FSC[:, 0:32], SMALL[:, cHU:cHU + 32],
                            SMALL[:, cSWS1:cSWS1 + 32], ALU.mult)
            v.tensor_tensor(FSC[:, 0:32], FSC[:, 0:32], SMALL[:, cB1QS:cB1QS + 32],
                            ALU.add)
            v.tensor_scalar(FSC[:, 0:32], FSC[:, 0:32], 0.0, None, ALU.max)
            v.tensor_reduce(SMALL[:, cTMP + 2:cTMP + 3], FSC[:, 0:32], AX.X,
                            ALU.max)
            pmax_to_scalar(SMALL[:, cTMP + 2:cTMP + 3], P0S[0:1, 1536:1537])
            v.memset(FSC[0:1, 1152:1168], -3.0e38)
            v.tensor_scalar(FSC[0:1, 1152:1168], FSC[0:1, 1152:1168],
                            P0S[0:1, 1536:1537], None, ALU.max).inc("v", 1)
            g.wait("v", r.snap("v"))
            g.dma_start(out=mxi_h[0:1, :], in_=FSC[0:1, 1152:1168]).inc("dmx", 16)
            g.wait("dmx", r.snap("dmx"))
            g.collective_compute("AllReduce", ALU.max, replica_groups=[list(range(NCORE))],
                                 ins=[mxi_h], outs=[mxo_h]).inc("cc", 1)
            g.wait("cc", r.snap("cc"))
            g.dma_start(out=P0S[0:1, 1537:1538], in_=mxo_h[0:1, 0:1]).inc("dmx", 16)
            v.wait("dmx", r.snap("dmx"))
            v.tensor_scalar(P0S[0:1, 1538:1539], P0S[0:1, 1537:1538], 1e-8, 1.0 / 255.0,
                            ALU.max, ALU.mult)
            v.reciprocal(P0S[0:1, 1539:1540], P0S[0:1, 1538:1539])
            t.wait("v", r.snap("v"))
            t.waitev(r.getfree("psC"))
            t.matmul(psC[:, 0:2], ONE_RF[0:1, :], P0S[0:1, 1538:1540],
                     start=True, stop=True).inc("t", 1)
            v.wait("t", r.snap("t"))
            v.tensor_copy(SMALL[:, cSCB + 2:cSCB + 4], psC[:, 0:2]).inc("v", 1)
            v.wait("v", r.snap("v"))
            r.setfree("psC", "v")
            v.tensor_scalar(SMALL[:, cSWS2:cSWS2 + 8], SMALL[:, cSW2:cSW2 + 8],
                            SMALL[:, cSCB + 2:cSCB + 3], None, ALU.mult)
            v.tensor_scalar(SMALL[:, cSWS2:cSWS2 + 8], SMALL[:, cSWS2:cSWS2 + 8],
                            1.0 / 127.0, None, ALU.mult)
            v.reciprocal(FSC[:, 0:8], SMALL[:, cSWS2:cSWS2 + 8])
            v.tensor_tensor(FSC[:, 8:16], SMALL[:, cB2R:cB2R + 8], FSC[:, 0:8], ALU.mult)
            v.tensor_scalar(FSC[:, 8:16], FSC[:, 8:16], 0.5, None, ALU.add)
            v.tensor_copy(ISC[:, 0:8], FSC[:, 8:16])
            v.tensor_copy(FSC[:, 16:24], ISC[:, 0:8])
            v.tensor_tensor(FSC[:, 24:32], FSC[:, 16:24], FSC[:, 8:16], ALU.is_gt)
            v.scalar_tensor_tensor(FSC[:, 16:24], FSC[:, 24:32], -1.0, FSC[:, 16:24],
                                   ALU.mult, ALU.add)
            v.tensor_tensor(SMALL[:, cB2QS:cB2QS + 8], FSC[:, 16:24],
                            SMALL[:, cSWS2:cSWS2 + 8], ALU.mult).inc("v", 1)
            sc2_v = r.snap("v")
            if flvl < 9:
                return
            mm1_dh = r.snap("gh")
            mm1_t = r.snap("t")
            y.wait("gh", mm1_dh)
            y.wait("t", mm1_t)
            for tc in range(NTC):
                v.waitev(r.getfree("HQ"))
                for fc in range(NFC):
                    if SERIAL_ATT:
                        v.wait("t", r.snap("t"))
                        y.wait("v", r.snap("v"))
                    hs = f"hin{fc % 2}"
                    y.waitev(r.getfree(hs))
                    hin = HTI[:, (fc % 2) * 1024:(fc % 2) * 1024 + 512]
                    y.dma_start(out=hin, in_=h_h[fc, tc]).inc("dh", 16)
                    v.wait("dh", r.snap("dh"))
                    if fc == 0 and tc == 0:
                        v.wait("v", sc2_v)
                    v.tensor_scalar(FSC[:, 0:512], hin, SMALL[:, cSCB + 3:cSCB + 4],
                                    0.5, ALU.mult, ALU.add)
                    r.setfree(hs, "v")
                    v.tensor_copy(ISC[:, 0:512], FSC[:, 0:512])
                    v.tensor_copy(FSC[:, 512:1024], ISC[:, 0:512])
                    v.tensor_tensor(FSC[:, 1024:1536], FSC[:, 512:1024], FSC[:, 0:512],
                                    ALU.is_gt)
                    v.scalar_tensor_tensor(CTXT[:, fc * 512:(fc + 1) * 512],
                                           FSC[:, 1024:1536], -1.0, FSC[:, 512:1024],
                                           ALU.mult, ALU.add).inc("v", 1)
                hq_v = r.snap("v")
                for dc in range(DC):
                    if SERIAL_ATT:
                        t.wait("v", r.snap("v"))
                        t.wait("s", r.snap("s"))
                        v.wait("t", r.snap("t"))
                        y.wait("t", r.snap("t"))
                        y.wait("v", r.snap("v"))
                    for half in range(2):
                        key = "W0W1" if half == 0 else "W2W3"
                        off = half * 2048
                        y.waitev(r.getfree(key))
                        y.dma_start(out=WT[:, off:off + 2048]
                                    .rearrange("p (a q) -> p a q", q=128),
                                    in_=w2qT_h[dc, half * 16:(half + 1) * 16, :, :]
                                    .rearrange("a p q -> p a q")).inc("dw", 16)
                    wdma = r.snap("dw")
                    t.wait("dw", wdma)
                    t.wait("v", hq_v)
                    ps, pn = (psC, "psC") if dc % 2 == 0 else (psD, "psD")
                    t.waitev(r.getfree(pn))
                    last = None
                    for fc in range(NFC):
                        last = t.matmul(ps[:, :], WT[:, fc * 128:(fc + 1) * 128],
                                        CTXT[:, fc * 512:(fc + 1) * 512],
                                        start=(fc == 0), stop=(fc == NFC - 1))
                    last.inc("t", 1)
                    r.setfree("W0W1", "t")
                    r.setfree("W2W3", "t")
                    v.wait("t", r.snap("t"))
                    dst = xt(dc)[:, 128 + tc * 512:128 + (tc + 1) * 512]
                    v.tensor_scalar(FSC[:, 0:512], ps[:, :],
                                    SMALL[:, cSWS2 + dc:cSWS2 + dc + 1], None, ALU.mult)
                    v.tensor_scalar(FSC[:, 0:512], FSC[:, 0:512],
                                    SMALL[:, cB2QS + dc:cB2QS + dc + 1], None, ALU.add)
                    v.tensor_tensor(dst, dst, FSC[:, 0:512], ALU.add).inc("v", 1)
                    r.setfree(pn, "v")
                r.setfree("HQ", "t")

        emit_ffn(0)
        if stop_after in ("p2zqd", "p2sind"):
            return
        if stop_after in ("p2", "p2ln", "p2zq", "p2m1", "p2s2"):
            emit_store()
            return

        phase_sync("tvsgy")
        emit_ln(LN_T, zb_out=True, zmax=False)
        ln3_v = r.snap("v")
        for hp in range(NHP):
            key = "W0" if hp % 2 == 0 else "W1"
            woff = 0 if hp % 2 == 0 else 1024
            g.waitev(r.getfree(key))
            g.dma_start(out=WT[:, woff:woff + 1024].rearrange("p (a q) -> p a q", q=128),
                        in_=wkvb.rearrange("(a p) q -> p a q", p=128)
                        [:, :, hp * 128:(hp + 1) * 128]).inc("gw", 16)
            wdma = r.snap("gw")
            t.wait("gw", wdma)
            if hp == 0:
                t.wait("v", ln3_v)
            for tc in range(NTC):
                ps, pn = (psA, "psA") if tc % 2 == 0 else (psB, "psB")
                t.waitev(r.getfree(pn))
                last = None
                for dc in range(DC):
                    last = t.matmul(ps[:, :], WT[:, woff + dc * 128:woff + (dc + 1) * 128],
                                    zbt(dc)[:, 128 + tc * 512:128 + (tc + 1) * 512],
                                    start=(dc == 0), stop=(dc == DC - 1))
                last.inc("t", 1)
                ks = f"kvs{tc % 2}"
                v.wait("t", r.snap("t"))
                v.waitev(r.getfree(ks))
                kstg = KVS[:, (tc % 2) * 512:(tc % 2) * 512 + 512]
                v.tensor_copy(kstg, ps[:, :]).inc("v", 1)
                r.setfree(pn, "v")
                g.wait("v", r.snap("v"))
                g.dma_start(out=kin_h[hp * 128:(hp + 1) * 128, tc * 512:(tc + 1) * 512],
                            in_=kstg).inc("gkv", 16)
                r.setfree(ks, "gkv")
            r.setfree(key, "v")
        g.waitev(r.getfree("VRHS"))
        for fh in range(2):
            g.dma_start(out=CTXT[:, fh * 4096:fh * 4096 + 4096]
                        .rearrange("p (a q) -> p a q", q=512),
                        in_=wkvb.rearrange("(a p) q -> p a q", p=128)
                        [:, :, D + fh * 512:D + (fh + 1) * 512]).inc("gw", 16)
        vw_gw = r.snap("gw")
        g.wait("gkv", r.snap("gkv"))
        g.collective_compute("AllGather", ALU.bypass, replica_groups=[list(range(NCORE))],
                             ins=[kin_h], outs=[kout_h]).inc("cc", 1)
        ag_k = r.snap("cc")
        for fh in range(2):
            t.wait("gw", vw_gw)
            for tt in range(16):
                ps, pn = (psA, "psA") if tt % 2 == 0 else (psB, "psB")
                t.waitev(r.getfree(pn))
                last = None
                for dc in range(DC):
                    last = t.matmul(ps[:, :], zbt(dc)[:, 128 + tt * 128:128 + (tt + 1) * 128],
                                    CTXT[:, fh * 4096 + dc * 512:fh * 4096 + (dc + 1) * 512],
                                    start=(dc == 0), stop=(dc == DC - 1))
                last.inc("t", 1)
                ks = f"kvs{tt % 2}"
                v.wait("t", r.snap("t"))
                v.waitev(r.getfree(ks))
                kstg = KVS[:, (tt % 2) * 512:(tt % 2) * 512 + 512]
                v.tensor_copy(kstg, ps[:, :]).inc("v", 1)
                r.setfree(pn, "v")
                y.wait("v", r.snap("v"))
                y.dma_start(out=vin_h[tt * 128:(tt + 1) * 128, fh * 512:(fh + 1) * 512],
                            in_=kstg).inc("dkv", 16)
                r.setfree(ks, "dkv")
        r.setfree("VRHS", "t")
        g.wait("dkv", r.snap("dkv"))
        g.collective_compute("AllGather", ALU.bypass, replica_groups=[list(range(NCORE))],
                             ins=[vin_h], outs=[vout_h]).inc("cc", 1)
        ag_v = r.snap("cc")
        for hp in range(NHP):
            key = "W0" if hp % 2 == 0 else "W1"
            woff = 0 if hp % 2 == 0 else 1024
            g.waitev(r.getfree(key))
            g.dma_start(out=WT[:, woff:woff + 1024].rearrange("p (a q) -> p a q", q=128),
                        in_=wqb.rearrange("(a p) q -> p a q", p=128)
                        [:, :, hp * 128:(hp + 1) * 128]).inc("gw", 16)
            wdma = r.snap("gw")
            t.wait("gw", wdma)
            qb0 = (hp % 2) * 2048
            for tc in range(NTC):
                ps, pn = (psA, "psA") if tc % 2 == 0 else (psB, "psB")
                t.waitev(r.getfree(pn))
                t.waitev(r.getfree(f"hpbq{hp % 2}"))
                last = None
                for dc in range(DC):
                    last = t.matmul(ps[:, :], WT[:, woff + dc * 128:woff + (dc + 1) * 128],
                                    zbt(dc)[:, 128 + tc * 512:128 + (tc + 1) * 512],
                                    start=(dc == 0), stop=(dc == DC - 1))
                last.inc("t", 1)
                s.wait("t", r.snap("t"))
                s.copy(HPB[:, qb0 + tc * 512:qb0 + (tc + 1) * 512], ps[:, :]).inc("s", 1)
                r.setfree(pn, "s")
            r.setfree(key, "s")
            qproj_s = r.snap("s")
            if hp == 0:
                g.wait("cc", ag_k)
                g.dma_start(out=P0S[0:1, 1663:1664], in_=mxo_h[0:1, 0:1]).inc("dmx", 16)
                agw_k = r.snap("dmx")
                g.wait("cc", ag_v)
                g.dma_start(out=P0S[0:1, 1663:1664], in_=mxo_h[0:1, 0:1]).inc("dmx", 16)
                agw_v = r.snap("dmx")
            for qs in range(4):
                q0 = qb0 + qs * 512
                t.waitev(r.getfree("psA"))
                t.waitev(r.getfree("psC"))
                if hp == 0 and qs == 0:
                    t.wait("s", qproj_s)
                t.matmul(psA[:, :], ZEROB[:, :], MASKN[:, :], start=True, stop=False,
                         skip_group_check=True)
                t.matmul(psC[:, :], ZEROB[:, :], MASKN[:, :], start=True, stop=False,
                         skip_group_check=True)
                def kv_fetch(kb):
                    ch = kb // 4
                    rank, cl0 = kb // 16, (kb % 16) * 128
                    kslot = f"bk{ch % 2}"
                    y.wait("dmx", agw_k)
                    y.waitev(r.getfree(kslot))
                    y.dma_start(out=KVS[:, (ch % 2) * 512:(ch % 2) * 512 + 512],
                                in_=kout_h[rank * D + hp * 128:rank * D + (hp + 1) * 128,
                                           cl0:cl0 + 512]).inc("dkv", 16)
                    ksnap = r.snap("dkv")
                    y.wait("dmx", agw_v)
                    y.dma_start(out=KVS[:, 1024 + (ch % 2) * 512:1024 + (ch % 2) * 512 + 512]
                                .rearrange("p (a q) -> p a q", q=128),
                                in_=vout_h[kb * 128:(kb + 4) * 128,
                                           hp * 128:(hp + 1) * 128]
                                .rearrange("(a p) q -> p a q", p=128)).inc("dkv", 16)
                    return ksnap, r.snap("dkv")

                def emit_scores(kb, kv_dma):
                    ch = kb // 4
                    ko = (ch % 2) * 512 + (kb % 4) * 128
                    pss = psS1 if kb % 2 == 0 else psS2
                    t.wait("dkv", kv_dma)
                    t.waitev(r.getfree(f"psS{kb % 2}"))
                    t.matmul(pss[:, 0:512], KVS[0:64, ko:ko + 128],
                             HPB[0:64, q0:q0 + 512], start=True, stop=True)
                    t.matmul(pss[:, 512:1024], KVS[64:128, ko:ko + 128],
                             HPB[64:128, q0:q0 + 512], start=True, stop=True,
                             tile_position=(64, 0)).inc("t", 1)
                    return r.snap("t")

                kv_dma, kv_vd = kv_fetch(0)
                vfet = {0: kv_vd}
                sc_t = {0: emit_scores(0, kv_dma)}
                for kb in range(NKB):
                    ch = kb // 4
                    if kb % 4 == 3 and kb < NKB - 1:
                        kv_dma, kv_vd = kv_fetch(kb + 1)
                        vfet[ch + 1] = kv_vd
                    if kb < NKB - 1:
                        sc_t[kb + 1] = emit_scores(kb + 1, kv_dma)
                    par = kb % 2
                    pbs = f"pb{par}"
                    pss = psS1 if par == 0 else psS2
                    pt1 = SHUF[:, par * 1024:par * 1024 + 512]
                    pt2 = SHUF[:, par * 1024 + 512:par * 1024 + 1024]
                    s.wait("t", sc_t.pop(kb))
                    s.waitev(r.getfree(pbs))
                    s.activation(SHUF[:, par * 1024:par * 1024 + 1024], pss[:, 0:1024],
                                 AF.Exp, bias=-EXPB_B, scale=SCALE).inc("s", 1)
                    r.setfree(f"psS{par}", "s")
                    exp_s = r.snap("s")
                    vo = 1024 + (ch % 2) * 512 + (kb % 4) * 128
                    t.wait("s", exp_s)
                    if kb % 4 == 0:
                        t.wait("dkv", vfet.pop(ch))
                    t.matmul(psA[0:64, :], KVS[:, vo:vo + 64], pt1,
                             start=False, stop=False, skip_group_check=True)
                    t.matmul(psA[64:128, :], KVS[:, vo + 64:vo + 128], pt2,
                             start=False, stop=(kb == NKB - 1), tile_position=(0, 64),
                             skip_group_check=True)
                    t.matmul(psC[0:1, :], ONESB[:, 0:1], pt1,
                             start=False, stop=False, skip_group_check=True)
                    last = t.matmul(psC[32:33, :], ONESB[:, 0:1], pt2,
                                    start=False, stop=(kb == NKB - 1),
                                    tile_position=(0, 32), skip_group_check=True)
                    last.inc("t", 1)
                    r.setfree(pbs, "t")
                    if kb % 4 == 3:
                        r.setfree(f"bk{ch % 2}", "t")
                v.wait("t", r.snap("t"))
                v.reciprocal(P0S[0:1, 0:512], psC[0:1, :])
                v.reciprocal(FSC[32:33, 0:512], psC[32:33, :])
                v.tensor_copy(P0S[0:1, 512:1024], FSC[32:33, 0:512]).inc("v", 1)
                r.setfree("psC", "v")
                t.wait("v", r.snap("v"))
                t.waitev(r.getfree("psB"))
                t.matmul(psB[:, :], ZEROB[:, :], MASKN[:, :], start=True, stop=False,
                         skip_group_check=True)
                t.matmul(psB[0:64, :], ONE_RF[0:1, 0:64], P0S[0:1, 0:512],
                         start=False, stop=False, skip_group_check=True)
                t.matmul(psB[64:128, :], ONE_RF[0:1, 0:64], P0S[0:1, 512:1024],
                         start=False, stop=True, tile_position=(0, 64),
                         skip_group_check=True).inc("t", 1)
                v.wait("t", r.snap("t"))
                v.tensor_copy(CTMP[:, 0:512], psA[:, :])
                v.tensor_tensor(ctxt(hp)[:, qs * 512:qs * 512 + 512], CTMP[:, 0:512],
                                psB[:, :], ALU.mult).inc("v", 1)
                r.setfree("psA", "v")
                r.setfree("psB", "v")
            r.setfree(f"hpbq{hp % 2}", "t")
        attnb_v = r.snap("v")
        for dc in range(DC):
            key = "W0" if dc % 2 == 0 else "W1"
            woff = 0 if dc % 2 == 0 else 1024
            g.waitev(r.getfree(key))
            g.dma_start(out=WT[:, woff:woff + 1024].rearrange("p (a q) -> p a q", q=128),
                        in_=wob.rearrange("(a p) q -> p a q", p=128)
                        [:, :, dc * 128:(dc + 1) * 128]).inc("gw", 16)
            wdma = r.snap("gw")
            t.wait("gw", wdma)
            if dc == 0:
                t.wait("v", attnb_v)
            for tc in range(NTC):
                ps, pn = (psA, "psA") if tc % 2 == 0 else (psB, "psB")
                t.waitev(r.getfree(pn))
                last = None
                for hp in range(NHP):
                    last = t.matmul(ps[:, :], WT[:, woff + hp * 128:woff + (hp + 1) * 128],
                                    ctxt(hp)[:, tc * 512:(tc + 1) * 512],
                                    start=(hp == 0), stop=(hp == NHP - 1))
                last.inc("t", 1)
                v.wait("t", r.snap("t"))
                dst = xt(dc)[:, 128 + tc * 512:128 + (tc + 1) * 512]
                v.scalar_tensor_tensor(dst, ps[:, :], SMALL[:, cBOB + dc:cBOB + dc + 1],
                                       dst, ALU.add, ALU.add).inc("v", 1)
                r.setfree(pn, "v")
            r.setfree(key, "v")

        if stop_after == "p3":
            emit_store()
            return

        emit_ffn(1)

        emit_store()

    return nc, gen


def compile_kernel(stop_after="p5"):
    nc, gen = build_program(stop_after)
    import contextlib
    stack = contextlib.ExitStack()
    semhs = {nm: stack.enter_context(nc.semaphore(f"sem_{nm}")) for nm in SEM_NAMES}
    with nc.Block() as block:
        @block.tensor
        def _(eng):
            gen(Router("t", {"t": eng, "v": None, "s": None, "g": None, "y": None}, semhs))
        @block.vector
        def _(eng):
            gen(Router("v", {"t": None, "v": eng, "s": None, "g": None, "y": None}, semhs))
        @block.scalar
        def _(eng):
            gen(Router("s", {"t": None, "v": None, "s": eng, "g": None, "y": None}, semhs))
        @block.gpsimd
        def _(eng):
            gen(Router("g", {"t": None, "v": None, "s": None, "g": eng, "y": None}, semhs))
        @block.sync
        def _(eng):
            gen(Router("y", {"t": None, "v": None, "s": None, "g": None, "y": eng}, semhs))
    stack.close()
    return nc


def make_tables(core):
    import ml_dtypes
    bf = ml_dtypes.bfloat16
    pos0 = core * T - HL
    pos = pos0 + np.arange(TH, dtype=np.float64)
    inv_freq = 1.0 / (10000.0 ** (np.arange(0, 64, 2, dtype=np.float64) / 64.0))
    ang = pos[:, None] * inv_freq[None, :]
    cos = np.concatenate([np.cos(ang), np.cos(ang)], axis=1)
    sin = np.concatenate([np.sin(ang), np.sin(ang)], axis=1)
    sin_signed = sin.copy()
    sin_signed[:, :32] *= -1.0
    cosT = np.zeros((128, TH), dtype=np.float64)
    sinT = np.zeros((128, TH), dtype=np.float64)
    cosT[0:64, :] = cos.T
    cosT[64:128, :] = cos.T
    sinT[0:64, :] = sin_signed.T
    sinT[64:128, :] = sin_signed.T
    kk = np.arange(128)[:, None]
    qq = np.arange(128)[None, :]
    mprev = (kk >= qq).astype(np.float64)
    mcur = (kk <= qq).astype(np.float64)
    maskn = np.concatenate([mprev, mcur, mprev, mcur], axis=1)
    maskf = maskn.copy()
    if core == 0:
        maskf[:, 0:128] = 0.0
        maskf[:, 256:384] = 0.0
    return (cosT.astype(bf), sinT.astype(bf), maskn.astype(bf), maskf.astype(bf))


def make_inmaps(inputs):
    import ml_dtypes
    bf = ml_dtypes.bfloat16
    x_full = np.ascontiguousarray(np.asarray(inputs["L0_attn_a_hidden_states"],
                                             dtype=np.float32).reshape(SEQ, D))
    common = dict(
        wqa=np.ascontiguousarray(np.asarray(inputs["L0_attn_a_Wq"], dtype=np.float32)),
        bqa=np.ascontiguousarray(np.asarray(inputs["L0_attn_a_bq"], dtype=np.float32)),
        wka=np.ascontiguousarray(np.asarray(inputs["L0_attn_a_Wk"], dtype=np.float32)),
        bka=np.ascontiguousarray(np.asarray(inputs["L0_attn_a_bk"], dtype=np.float32)),
        wva=np.ascontiguousarray(np.asarray(inputs["L0_attn_a_Wv"], dtype=np.float32)),
        bva=np.ascontiguousarray(np.asarray(inputs["L0_attn_a_bv"], dtype=np.float32)),
        woa=np.ascontiguousarray(np.asarray(inputs["L0_attn_a_Wo"], dtype=np.float32)),
        boa=np.ascontiguousarray(np.asarray(inputs["L0_attn_a_bo"], dtype=np.float32)),
        wqb=np.ascontiguousarray(np.asarray(inputs["L0_attn_b_Wq"], dtype=np.float32)),
        wkvb=np.ascontiguousarray(np.asarray(inputs["L0_attn_b_Wkv"], dtype=np.float32)),
        wob=np.ascontiguousarray(np.asarray(inputs["L0_attn_b_Wo"], dtype=np.float32)),
        bob=np.ascontiguousarray(np.asarray(inputs["L0_attn_b_bo"], dtype=np.float32)),
        w1=np.ascontiguousarray(np.asarray(inputs["L0_ffn_w1"], dtype=np.float32)),
        b1=np.ascontiguousarray(np.asarray(inputs["L0_ffn_b1"], dtype=np.float32)),
        w2=np.ascontiguousarray(np.asarray(inputs["L0_ffn_w2"], dtype=np.float32)),
        b2=np.ascontiguousarray(np.asarray(inputs["L0_ffn_b2"], dtype=np.float32)),
        ident=np.eye(128, dtype=np.float32),
        identb=np.eye(128).astype(bf),
        consts=np.array([[1e-5, -EXPB_A, -EXPB_B, 0.5, 0, 0, 0, 0]], dtype=np.float32),
        constsb=np.tile(np.array([[-EXPB_A, -EXPB_B]], dtype=np.float32), (128, 1)),
    )
    in_maps = []
    for c in range(NCORE):
        xs = np.zeros((TH, D), dtype=np.float32)
        if c == 0:
            xs[HL:, :] = x_full[0:T, :]
        else:
            xs[:, :] = x_full[c * T - HL:c * T - HL + TH, :]
        cosT, sinT, maskn, maskf = make_tables(c)
        m = dict(common)
        m.update(x=xs, cosT=cosT, sinT=sinT, maskn=maskn, maskf=maskf)
        in_maps.append(m)
    return in_maps


_NC_CACHE = {}


def kernel(**inputs):
    from concourse.bass_utils import run_bass_kernel_spmd
    if "nc" not in _NC_CACHE:
        _NC_CACHE["nc"] = compile_kernel("p5")
    nc = _NC_CACHE["nc"]
    in_maps = make_inmaps(inputs)
    res = run_bass_kernel_spmd(nc, in_maps, core_ids=list(range(NCORE)))
    out = np.concatenate([res.results[c]["out"] for c in range(NCORE)], axis=0)
    return out.reshape(1, SEQ, D).astype(np.float32)
